# version 09
# baseline (speedup 1.0000x reference)
import numpy as np
import ml_dtypes
from contextlib import ExitStack

import concourse.bass as bass
import concourse.mybir as mybir
from concourse.bass_utils import run_bass_kernel_spmd

F32 = mybir.dt.float32
BF16 = mybir.dt.bfloat16
AF = mybir.ActivationFunctionType
ALU = mybir.AluOpType
AX = mybir.AxisListType

NCORE = 8
S = 8192
D = 4096
T = S // NCORE
KC = D // 128
TB = T // 512
EPS = 1e-5
NEG = -1.0e9

L0_NQ, L0_NKV, HD = 32, 8, 128
ATT_SCALE = HD ** -0.5
L1_NH = 32
HPC = L1_NH // NCORE


AUTO_SERIAL = True


class Em:
    ENGS = ("tensor", "vector", "scalar", "gpsimd", "sync")

    def __init__(self, nc):
        self.nc = nc
        self.streams = {e: [] for e in self.ENGS}
        self.counts = {}

    def count(self, name):
        return self.counts.get(name, 0)

    def wait(self, eng, name, val=None):
        self.counts.setdefault(name, 0)
        v = self.counts[name] if val is None else val
        if v > 0:
            self.streams[eng].append(("w", name, v))
        return v

    ENG_SEM = {"vector": "dv", "scalar": "ac"}

    def op(self, eng, fn, inc=None, amt=1, noser=False):
        selfwait = None
        if eng in self.ENG_SEM:
            es = self.ENG_SEM[eng]
            assert inc is None or inc == es, (eng, inc)
            inc = es
            amt = 1
            self.counts.setdefault(es, 0)
            if AUTO_SERIAL and not noser and self.counts[es] > 0:
                selfwait = (es, self.counts[es])
        if inc is not None:
            self.counts.setdefault(inc, 0)
            self.counts[inc] += amt
        self.streams[eng].append(("o", fn, inc, amt, selfwait))
        return self.counts[inc] if inc is not None else None

    def wait_fin(self, eng, name):
        self.counts.setdefault(name, 0)
        self.streams[eng].append(("wf", name))

    def gop(self, fn):
        prev = self.counts.get("gp", 0)
        selfwait = ("gp", prev) if prev > 0 else None
        self.counts["gp"] = prev + 1
        self.streams["gpsimd"].append(("o", fn, "gp", 1, selfwait))
        return self.counts["gp"]

    def dma(self, eng, out, in_, inc, **kw):
        selfwait = None
        self.counts.setdefault(inc, 0)
        self.counts[inc] += 16
        self.streams[eng].append(
            ("o", (lambda E, o=out, i=in_, k=dict(kw): E.dma_start(out=o, in_=i, **k)),
             inc, 16, selfwait))
        return self.counts[inc]

    def finalize(self, ctx):
        handles = {n: ctx.enter_context(self.nc.semaphore(n)) for n in self.counts}
        block = ctx.enter_context(self.nc.Block())
        for eng in self.ENGS:
            items = self.streams[eng]
            if not items:
                continue

            def body(E, items=items):
                for it in items:
                    if it[0] == "w":
                        E.wait_ge(handles[it[1]], it[2])
                    elif it[0] == "wf":
                        if self.counts[it[1]] > 0:
                            E.wait_ge(handles[it[1]], self.counts[it[1]])
                    else:
                        _, fn, inc, amt, selfwait = it
                        if selfwait is not None:
                            E.wait_ge(handles[selfwait[0]], selfwait[1])
                        ins = fn(E)
                        if inc is not None:
                            ins.then_inc(handles[inc], amt)

            getattr(block, eng)(body)


class Ctx:

    def __init__(self, nc, em, ctx):
        self.nc = nc
        self.em = em
        self.free = {}
        E = ctx.enter_context
        self.A0 = E(nc.sbuf_tensor("A0", [128, 32768], BF16))
        self.A1 = E(nc.sbuf_tensor("A1", [128, 32768], BF16))
        self.ST = E(nc.sbuf_tensor("ST", [128, 6144], F32))
        self.WS = E(nc.sbuf_tensor("WS", [128, 8192], BF16))
        self.RW = E(nc.sbuf_tensor("RW", [128, 3072], F32))
        self.ident_f = E(nc.sbuf_tensor("ident_f", [128, 128], F32))
        self.ident_b = E(nc.sbuf_tensor("ident_b", [128, 128], BF16))
        self.ones_col_b = E(nc.sbuf_tensor("ones_col_b", [128, 1], BF16))
        self.ones_col_f = E(nc.sbuf_tensor("ones_col_f", [128, 1], F32))
        self.ones_row_b = E(nc.sbuf_tensor("ones_row_b", [1, 512], BF16))
        self.ones_row_f = E(nc.sbuf_tensor("ones_row_f", [1, 512], F32))
        self.SM = E(nc.sbuf_tensor("SM", [128, 1024], F32))
        self.SMB = E(nc.sbuf_tensor("SMB", [128, 2048], BF16))
        self.PS = E(nc.psum_tensor("PS", [128, 4096], F32))

    def bank(self, i, n=1):
        return self.PS[:, 512 * i:512 * (i + n)]

    def bank_bf(self, i, n=1):
        return self.PS[:, :].bitcast(BF16)[:, 1024 * i:1024 * (i + n)]

    def stb(self):
        return self.ST[:, :].bitcast(BF16)

    def wait_free(self, eng, key):
        fr = self.free.get(key)
        if fr:
            self.em.wait(eng, fr[0], fr[1])

    def set_free(self, key, sem, cnt):
        self.free[key] = (sem, cnt)


def emit_proj(C, tag, wt, njb, jw, x_rhs, out_cb, extra=None, nkc=KC,
              banks=(0, 1, 2, 3), tbs=(0, 1), jb_order=None):
    em = C.em
    nc = C.nc
    wsrc = wt.rearrange("(jb p) f -> p jb f", p=128)
    wready = {}
    order = list(jb_order) if jb_order is not None else list(range(njb))
    slot_of = {jb: idx % 2 for idx, jb in enumerate(order)}

    def issue_w(jb):
        if jb is None:
            return
        s = slot_of[jb]
        C.wait_free("gpsimd", ("WS", s))
        dst = C.WS[:, 4096 * s:4096 * s + nkc * jw]
        wready[jb] = em.dma("gpsimd", dst, wsrc[:, jb, :], "w%d" % s)

    issue_w(order[0])
    issue_w(order[1])
    for idx, jb in enumerate(order):
        s = slot_of[jb]
        em.wait("tensor", "w%d" % s, wready[jb])
        for ti, tb in enumerate(tbs):
            pb = banks[(idx % 2) * 2 + ti]
            C.wait_free("tensor", ("bank", pb))
            ps = C.PS[0:jw, 512 * pb:512 * pb + 512]
            ex = extra(jb, tb) if extra is not None else []
            for kc in range(nkc):
                last = (kc == nkc - 1) and not ex
                em.op("tensor",
                      (lambda E, o=ps, l=C.WS[:, 4096 * s + kc * jw:4096 * s + (kc + 1) * jw],
                       r=x_rhs(kc, tb), st=(kc == 0), sp=last:
                       E.matmul(o, l, r, start=st, stop=sp)),
                      inc=("mm" if last else None))
            for i, (lt, rr) in enumerate(ex):
                last = i == len(ex) - 1
                em.op("tensor",
                      (lambda E, o=ps, l=lt, r=rr, sp=last:
                       E.matmul(o, l, r, start=False, stop=sp)),
                      inc=("mm" if last else None))
            mmcnt = em.count("mm")
            fsem, fcnt = out_cb(jb, tb, ps, mmcnt)
            C.set_free(("bank", pb), fsem, fcnt)
        C.set_free(("WS", s), "mm", em.count("mm"))
        issue_w(order[idx + 2] if idx + 2 < njb else None)


class Ep:

    def __init__(self, C, tag, xprev, r_out, xf_out, bbuf, bias_cols=None):
        self.C = C
        self.tag = tag
        self.xprev = xprev
        self.r_out = r_out
        self.xf_out = xf_out
        self.bbuf = bbuf
        self.bias_cols = bias_cols
        self.pending = []
        self.first = {}

    def chunk_cb(self, jb, tb, ps, mmcnt):
        C, em = self.C, self.C.em
        tag = self.tag
        i = (jb * 2 + tb) % 4
        xsl = C.ST[:, 512 * i:512 * (i + 1)]
        rsl = C.ST[:, 2048 + 512 * i:2048 + 512 * (i + 1)]
        rbf = C.SMB[:, 1024 * (i % 2):1024 * (i % 2) + 512]
        sq = C.SMB[:, 1024 * (i % 2) + 512:1024 * (i % 2) + 1024]
        C.wait_free("sync", ("xsl", i))
        off = jb * 1024 + 512 * tb
        xc = em.dma("sync", xsl, self.xprev[:, off:off + 512], "x%d" % i)
        em.wait("vector", "mm", mmcnt)
        em.wait("vector", "x%d" % i, xc)
        C.wait_free("vector", ("rsl", i))
        C.wait_free("vector", ("rbf", i % 2))
        if self.bias_cols is not None:
            em.op("vector",
                  (lambda E, o=rsl, a=ps, b=xsl, s=self.bias_cols[:, jb:jb + 1]:
                   E.scalar_tensor_tensor(o, a, s, b, ALU.add, ALU.add)), inc="dv")
        else:
            em.op("vector", lambda E, o=rsl, a=ps, b=xsl: E.tensor_add(o, a, b), inc="dv")
        bankfree = ("dv", em.count("dv"))
        C.set_free(("xsl", i), "dv", em.count("dv"))
        em.op("vector", lambda E, o=rbf, a=rsl: E.tensor_copy(o, a), inc="dv")
        em.op("vector", lambda E, o=sq, a=rbf: E.tensor_mul(o, a, a), inc="dv")
        dvc = em.count("dv")
        em.wait("sync", "dv", dvc - 2)
        rs = em.dma("sync", self.r_out[:, off:off + 512], rsl, "r%d" % i)
        C.set_free(("rsl", i), "r%d" % i, rs)
        self.pending.append((jb, tb, rbf, sq, dvc, i % 2))
        if len(self.pending) > 1:
            self._flush_stat(self.pending.pop(0))
        return bankfree

    def _flush_stat(self, rec):
        C, em = self.C, self.C.em
        jb, tb, rbf, sq, dvc, ri = rec
        em.wait("tensor", "dv", dvc)
        st = (jb == 0)
        sp = (jb == KC - 1)
        em.op("tensor",
              (lambda E, o=C.PS[0:1, 512 * (4 + tb):512 * (4 + tb) + 512], l=C.ones_col_b[:, :],
               r=rbf, s0=st, s1=sp: E.matmul(o, l, r, start=s0, stop=s1)),
              inc=("mm" if sp else None))
        em.op("tensor",
              (lambda E, o=C.PS[0:1, 512 * (6 + tb):512 * (6 + tb) + 512], l=C.ones_col_b[:, :],
               r=sq, s0=st, s1=sp: E.matmul(o, l, r, start=s0, stop=s1)),
              inc=("mm" if sp else None))
        C.set_free(("rbf", ri), "mm", em.count("mm"))

    def finish(self):
        C, em = self.C, self.C.em
        tag = self.tag
        while self.pending:
            self._flush_stat(self.pending.pop(0))
        em.wait("vector", "mm")
        R = C.RW
        for tb in (0, 1):
            mu = R[0:1, 512 * 0:512 * 1]
            msq = R[0:1, 512 * 1:512 * 2]
            al = R[0:1, 512 * 2:512 * 3]
            be = R[0:1, 512 * 3:512 * 4]
            em.op("vector", lambda E, o=mu, a=C.PS[0:1, 512 * (4 + tb):512 * (4 + tb) + 512]:
                  E.tensor_scalar_mul(o, a, 1.0 / D), inc="dv")
            em.op("vector", lambda E, o=msq, a=C.PS[0:1, 512 * (6 + tb):512 * (6 + tb) + 512]:
                  E.tensor_scalar_mul(o, a, 1.0 / D), inc="dv")
            em.op("vector", lambda E, o=be, a=mu, b=mu: E.tensor_mul(o, a, b), inc="dv")
            em.op("vector", lambda E, o=msq, a=msq, b=be: E.tensor_sub(o, a, b), inc="dv")
            em.op("vector", lambda E, o=msq, a=msq: E.tensor_scalar_add(o, a, EPS), inc="dv")
            em.wait("scalar", "dv")
            em.op("scalar", lambda E, o=msq, a=msq: E.activation(o, a, AF.Ln), inc="ac")
            em.wait("vector", "ac")
            em.op("vector", lambda E, o=msq, a=msq: E.tensor_scalar_mul(o, a, -0.5), inc="dv")
            em.wait("scalar", "dv")
            em.op("scalar", lambda E, o=al, a=msq: E.activation(o, a, AF.Exp), inc="ac")
            em.wait("vector", "ac")
            em.op("vector", lambda E, o=be, a=mu, b=al: E.tensor_mul(o, a, b), inc="dv")
            em.op("vector", lambda E, o=be, a=be: E.tensor_scalar_mul(o, a, -1.0), inc="dv")
            em.wait("tensor", "dv")
            for k, row in ((0, al), (1, be)):
                pb = tb * 2 + k
                C.wait_free("tensor", ("bank", pb))
                em.op("tensor",
                      (lambda E, o=C.bank(pb), l=C.ones_row_f[0:1, 0:128], r=row:
                       E.matmul(o, l, r, start=True, stop=True)), inc="mm")
            em.wait("vector", "mm")
            for k in (0, 1):
                pb = tb * 2 + k
                em.op("vector",
                      (lambda E,
                       o=C.ST[:, 4096 + 1024 * tb + 512 * k:4096 + 1024 * tb + 512 * (k + 1)],
                       a=C.bank(pb): E.tensor_copy(o, a)), inc="dv")
                C.set_free(("bank", pb), "dv", em.count("dv"))
        spbase = em.count("dv")
        for i in range(4):
            em.wait("sync", "r%d" % i)
            em.wait("gpsimd", "r%d" % i)
        for jb in range(KC):
            for tb in (0, 1):
                i = (jb * 2 + tb) % 4
                off = jb * 1024 + 512 * tb
                xsl = C.ST[:, 512 * i:512 * (i + 1)]
                rsl = C.ST[:, 2048 + 512 * i:2048 + 512 * (i + 1)]
                arep = C.ST[:, 4096 + 1024 * tb:4096 + 1024 * tb + 512]
                brep = C.ST[:, 4096 + 1024 * tb + 512:4096 + 1024 * tb + 1024]
                C.wait_free("gpsimd", ("xsl", i))
                rc = em.dma("gpsimd", xsl, self.r_out[:, off:off + 512], "x%d" % i)
                em.wait("vector", "x%d" % i, rc)
                em.wait("vector", "dv", spbase)
                C.wait_free("vector", ("rsl", i))
                em.op("vector", lambda E, o=rsl, a=xsl, b=arep: E.tensor_mul(o, a, b), inc="dv")
                C.set_free(("xsl", i), "dv", em.count("dv"))
                em.op("vector", lambda E, o=rsl, a=rsl, b=brep: E.tensor_add(o, a, b), inc="dv")
                dvc = em.count("dv")
                em.wait("sync", "dv", dvc)
                rs = em.dma("sync", self.xf_out[:, off:off + 512], rsl, "r%d" % i)
                C.set_free(("rsl", i), "r%d" % i, rs)
                if self.bbuf is not None:
                    em.op("vector",
                          (lambda E,
                           o=self.bbuf[:, jb * 1024 + 512 * tb:jb * 1024 + 512 * (tb + 1)],
                           a=rsl: E.tensor_copy(o, a)), inc="dv")
        return ("dv", em.count("dv"))


def setup_consts(C, io):
    em = C.em
    em.dma("sync", C.ident_f[:, :], io["ident"][:, :], "c0")
    em.dma("gpsimd", C.ident_b[:, :], io["ident"][:, :], "c0g")
    em.op("vector", lambda E, t=C.ones_col_b: E.memset(t[:, :], 1.0), inc="dv")
    em.op("vector", lambda E, t=C.ones_col_f: E.memset(t[:, :], 1.0), inc="dv")
    em.op("vector", lambda E, t=C.ones_row_b: E.memset(t[:, :], 1.0), inc="dv")
    em.op("vector", lambda E, t=C.ones_row_f: E.memset(t[:, :], 1.0), inc="dv")
    em.wait_fin("tensor", "c0")
    em.wait_fin("tensor", "c0g")
    em.wait("tensor", "dv")
    em.wait("scalar", "dv")


def emit_l0ffn(C, io, bprev, ep):
    em = C.em
    tag = "f0"

    def xr(kc, tb):
        return bprev[:, kc * 1024 + tb * 512:kc * 1024 + (tb + 1) * 512]

    C.wait_free("gpsimd", ("WS", 0))
    C.wait_free("gpsimd", ("WS", 1))
    wr_rdy = em.dma("gpsimd", C.WS[:, 0:256], io["l0_wrT"][:, :], "wf0")
    at_rdy = em.dma("gpsimd", C.WS[:, 256:4352], io["l0_aT"][:, :], "wf0")
    C.wait_free("gpsimd", ("rbf", 1))
    br_rdy = em.dma("gpsimd", C.SMB[0:1, 2040:2048], io["l0_br"][:, :], "wf0")
    mx_rdy = em.dma("sync", C.RW[0:8, 2048:2176], io["mexp"][:, :], "c2")

    em.wait("tensor", "wf0", br_rdy)
    for tb in (0, 1):
        pb = 4 + tb
        ps = C.PS[0:8, 512 * pb:512 * pb + 512]
        for kc in range(KC):
            em.op("tensor",
                  (lambda E, o=ps, l=C.WS[:, kc * 8:(kc + 1) * 8], r=xr(kc, tb),
                   st=(kc == 0): E.matmul(o, l, r, start=st, stop=False)))
        em.op("tensor",
              (lambda E, o=ps, l=C.SMB[0:1, 2040:2048], r=C.ones_row_b[0:1, :]:
               E.matmul(o, l, r, start=False, stop=True)), inc="mm")
    lg_mm = em.count("mm")
    em.wait("vector", "mm", lg_mm)
    for i_ in range(2):
        C.wait_free("vector", ("xsl", i_))
    for tb in (0, 1):
        em.op("vector",
              (lambda E, o=C.ST[0:8, 512 * tb:512 * (tb + 1)],
               a=C.PS[0:8, 512 * (4 + tb):512 * (4 + tb) + 512]:
               E.tensor_copy(o, a)), inc="dv")
    lg_dv = em.count("dv")

    em.wait("tensor", "dv", lg_dv)
    em.wait("tensor", "c2", mx_rdy)
    for tb in (0, 1):
        for tc in range(4):
            gtc = tb * 4 + tc
            lgt = C.RW[:, 2176:2184]
            et = C.RW[:, 2184:2192]
            negmax = C.RW[:, 2192:2193]
            ssum = C.RW[:, 2193:2194]
            rc = C.RW[:, 2194:2195]
            sct = C.RW[:, 2196:2204]
            scT = C.RW[0:8, 2204:2332]
            em.wait("tensor", "dv")
            em.op("tensor",
                  (lambda E, o=C.PS[:, 512 * 6:512 * 6 + 8],
                   a=C.ST[0:8, gtc * 128:(gtc + 1) * 128], i=C.ident_f[0:8, 0:8]:
                   E.transpose(o, a, i)), inc="mm")
            em.wait("vector", "mm")
            em.op("vector", lambda E, o=lgt, a=C.PS[:, 512 * 6:512 * 6 + 8]:
                  E.tensor_copy(o, a), inc="dv")
            em.op("vector", lambda E, o=negmax, a=lgt:
                  E.reduce_max(o, a, axis=AX.X, negate=True), inc="dv")
            em.wait("scalar", "dv")
            em.op("scalar", lambda E, o=et, a=lgt, b=negmax, s=ssum:
                  E.activation(o, a, AF.Exp, bias=b, accum_out=s), inc="ac")
            em.wait("vector", "ac")
            em.op("vector", lambda E, o=rc, a=ssum: E.reciprocal(o, a), inc="dv")
            em.op("vector", lambda E, o=sct, a=et, s=rc:
                  E.tensor_scalar(o, a, s, 16.0, ALU.mult, ALU.mult), inc="dv")
            em.wait("tensor", "dv")
            em.op("tensor",
                  (lambda E, o=C.PS[0:8, 512 * 6 + 128:512 * 6 + 256], a=sct, i=C.ident_f[:, :]:
                   E.transpose(o, a, i)), inc="mm")
            em.wait("vector", "mm")
            em.op("vector", lambda E, o=scT, a=C.PS[0:8, 512 * 6 + 128:512 * 6 + 256]:
                  E.tensor_copy(o, a), inc="dv")
            em.wait("tensor", "dv")
            em.op("tensor",
                  (lambda E, o=C.PS[:, 512 * 7 + 128 * tc:512 * 7 + 128 * (tc + 1)],
                   l=C.RW[0:8, 2048:2176], r=scT:
                   E.matmul(o, l, r, start=True, stop=True)), inc="mm")
        em.wait("vector", "mm")
        em.op("vector", (lambda E, o=C.ST[:, 4096 + 512 * tb:4096 + 512 * (tb + 1)], a=C.bank(7):
                         E.tensor_copy(o, a)), inc="dv")
        if "dbg_srep" in io:
            em.wait("sync", "dv")
            em.dma("sync", io["dbg_srep"][:, 512 * tb:512 * (tb + 1)],
                   C.ST[:, 4096 + 512 * tb:4096 + 512 * (tb + 1)], "dg")
    srep_dv = em.count("dv")
    C.set_free(("xsl", 0), "mm", em.count("mm"))
    C.set_free(("xsl", 1), "mm", em.count("mm"))

    em.wait("tensor", "wf0", br_rdy)
    em.wait("tensor", "dv", srep_dv)
    for tb in (0, 1):
        pb = 4 + tb
        ps = C.bank(pb)
        for kc in range(KC):
            em.op("tensor",
                  (lambda E, o=ps, l=C.WS[:, 256 + kc * 128:256 + (kc + 1) * 128],
                   r=xr(kc, tb), st=(kc == 0), sp=(kc == KC - 1):
                   E.matmul(o, l, r, start=st, stop=sp)),
                  inc=("mm" if kc == KC - 1 else None))
        em.wait("vector", "mm")
        em.op("vector",
              (lambda E, o=C.RW[:, :].bitcast(BF16)[:, 4864 + 512 * tb:4864 + 512 * (tb + 1)],
               a=ps, b=C.ST[:, 4096 + 512 * tb:4096 + 512 * (tb + 1)]:
               E.tensor_mul(o, a, b)), inc="dv")
        if "dbg_lowS" in io:
            em.wait("sync", "dv")
            em.dma("sync", io["dbg_lowS"][:, 512 * tb:512 * (tb + 1)],
                   C.RW[:, :].bitcast(BF16)[:, 4864 + 512 * tb:4864 + 512 * (tb + 1)], "dg")
    low_dv = em.count("dv")

    em.wait("gpsimd", "mm")
    bt_rdy = em.dma("gpsimd", C.WS[:, 256:4352], io["l0_bT"][:, :], "wb")

    if "dbg_lowS" in io:
        em.wait("vector", "dg")
    if "dbg_bt" in io:
        em.wait("sync", "wb", bt_rdy)
        em.dma("sync", io["dbg_bt"][:, :], C.WS[:, 256:4352], "dg2")
        em.wait("tensor", "dg2")
    em.wait("tensor", "wb", bt_rdy)
    em.wait("tensor", "dv", low_dv)
    for jb in range(KC):
        for tb in (0, 1):
            pb = (jb % 2) * 2 + tb
            C.wait_free("tensor", ("bank", pb))
            ps = C.bank(pb)
            em.op("tensor",
                  (lambda E, o=ps, l=C.WS[:, 256 + jb * 128:256 + (jb + 1) * 128],
                   r=C.RW[:, :].bitcast(BF16)[:, 4864 + 512 * tb:4864 + 512 * (tb + 1)]:
                   E.matmul(o, l, r, start=True, stop=True)), inc="mm")
            fs = ep.chunk_cb(jb, tb, ps, em.count("mm"))
            C.set_free(("bank", pb), fs[0], fs[1])
    return ep.finish()


def _f(a):
    return np.ascontiguousarray(a, dtype=np.float32)


def _bm(wt, njb, jw):
    nkc = wt.shape[0] // 128
    return _f(wt.reshape(nkc, 128, njb, jw).transpose(2, 1, 0, 3)
              .reshape(njb * 128, nkc * jw))


def host_prep(inp):
    sh = {}
    x = inp["L0_attn_hidden_states"][0]
    xT = _f(x.T)
    sh["ident"] = np.eye(128, dtype=np.float32)
    sh["mexp"] = np.repeat(np.eye(8, dtype=np.float32), 16, axis=1)
    sh["jcv"] = np.broadcast_to(np.arange(4, dtype=np.float32)[None, :] * 128.0,
                                (128, 4)).copy()
    sh["posr"] = (np.arange(68, dtype=np.float32)[None, :] * 128.0).copy()
    sh["l0_qkvT"] = _bm(np.concatenate(
        [inp["L0_attn_Wq"], inp["L0_attn_Wk"], inp["L0_attn_Wv"]], axis=0).T, 48, 128)
    sh["l0_bv"] = _f(inp["L0_attn_bv"][None, :])
    sh["l0_woT"] = _bm(inp["L0_attn_Wo"].T, KC, 128)
    sh["l0_wrT"] = _bm(inp["L0_ffn_Wr"].T, 1, 8)
    sh["l0_br"] = _f(inp["L0_ffn_br"][None, :])
    sh["l0_aT"] = _bm(inp["L0_ffn_A"].transpose(2, 0, 1).reshape(D, 128), 1, 128)
    sh["l0_bT"] = _f(inp["L0_ffn_B"].transpose(0, 2, 1).reshape(128, D))
    sh["l1_woT"] = _bm(inp["L1_attn_w_out"].T, KC, 128)
    sh["l1_wT"] = _bm(inp["L1_ffn_W"].T, KC, 128)
    sh["l1_bias"] = _f(inp["L1_ffn_bias"].reshape(32, 128).T)
    sh["l1_laT"] = _bm(inp["L1_ffn_lora_A"].transpose(2, 0, 1).reshape(D, 128), 1, 128)
    sh["l1_lbT"] = _f(inp["L1_ffn_lora_B"].transpose(0, 2, 1).reshape(128, D))
    sh["l1_lw"] = _f(inp["L1_ffn_lora_weights"][:, None])

    wqkvT = _f(inp["L1_attn_w_qkv"].T)
    percore = []
    import math
    slopes = 1.0 / (2.0 ** (np.arange(1, 33, dtype=np.float64) * (8.0 / 32)))
    ii = np.arange(128)[:, None]
    jj = np.arange(512)[None, :]
    for c in range(NCORE):
        pc = {}
        pc["x_in"] = _f(xT[:, c * T:(c + 1) * T].reshape(KC, 128, T)
                        .transpose(1, 0, 2).reshape(128, KC * T))
        cols = np.concatenate([np.arange(512 * c, 512 * c + 512),
                               4096 + np.arange(512 * c, 512 * c + 512),
                               8192 + np.arange(512 * c, 512 * c + 512)])
        pc["l1_qkvT"] = _bm(wqkvT[:, cols], 12, 128)
        for nm, key in (("qw", "L1_attn_q_ln_w"), ("qb", "L1_attn_q_ln_b"),
                        ("kw", "L1_attn_k_ln_w"), ("kb", "L1_attn_k_ln_b")):
            pc["l1_" + nm] = _f(inp[key][512 * c:512 * (c + 1)].reshape(4, 128).T)
        sl = slopes[4 * c:4 * c + 4]
        pc["l1_slopes"] = _f(np.broadcast_to(sl[None, :], (128, 4)))
        strips = np.zeros((4, 5, 128, 512), dtype=np.float64)
        for h in range(4):
            base = sl[h] * (ii - jj)
            strips[h, 0] = base
            for jc in range(4):
                m = np.where(ii <= jj - 128 * jc, base, NEG)
                strips[h, 1 + jc] = m
        pc["l1_strips"] = strips.astype(np.float32).reshape(4 * 5 * 128, 512)
        percore.append(pc)
    return sh, percore


def _decl(nc, name, shape, dtype, kind):
    return nc.dram_tensor(name, list(shape), dtype, kind=kind).ap()


def build_l0ffn_stage(dbg=False):
    nc = bass.Bass("TRN2", target_bir_lowering=False, debug=False, num_devices=8)
    io = {}
    io["x1f"] = _decl(nc, "x1f", [128, KC * T], F32, "ExternalInput")
    for nm, shp in (("l0_wrT", [128, 256]), ("l0_br", [1, 8]), ("l0_aT", [128, KC * 128]),
                    ("l0_bT", [128, D]), ("mexp", [8, 128]), ("ident", [128, 128])):
        io[nm] = _decl(nc, nm, shp, F32, "ExternalInput")
    io["r2"] = _decl(nc, "r2", [128, KC * T], F32, "Internal")
    io["x2f"] = _decl(nc, "x2f", [128, KC * T], F32, "ExternalOutput")
    if dbg:
        io["dbg_lg"] = _decl(nc, "dbg_lg", [8, 1024], F32, "ExternalOutput")
        io["dbg_srep"] = _decl(nc, "dbg_srep", [128, 1024], F32, "ExternalOutput")
        io["dbg_lowS"] = _decl(nc, "dbg_lowS", [128, 1024], BF16, "ExternalOutput")
        io["dbg_bt"] = _decl(nc, "dbg_bt", [128, 4096], BF16, "ExternalOutput")
    with ExitStack() as ctx:
        em = Em(nc)
        C = Ctx(nc, em, ctx)
        setup_consts(C, io)
        bld = em.dma("gpsimd", C.A0[:, 0:KC * 1024], io["x1f"][:, :], "c0g")
        em.wait_fin("tensor", "c0g")
        ep = Ep(C, "e2", io["x1f"], io["r2"], io["x2f"], C.A1)
        emit_l0ffn(C, io, C.A0, ep)
        for i in range(4):
            em.wait("sync", "r%d" % i)
        em.finalize(ctx)
    return nc


_NC_CACHE = {}


def _get_nc(name, builder):
    if name not in _NC_CACHE:
        _NC_CACHE[name] = builder()
    return _NC_CACHE[name]


def kernel(**inputs):
    return kernel_fused(**inputs)


def kernel_multilaunch(**inputs):
    sh, pc = host_prep(inputs)
    cores = list(range(NCORE))

    nc1 = _get_nc("s1", build_l0attn_stage)
    m1 = []
    for c in cores:
        m1.append({"x_in": pc[c]["x_in"], "l0_qkvT": sh["l0_qkvT"],
                   "l0_bv": sh["l0_bv"], "l0_woT": sh["l0_woT"],
                   "ident": sh["ident"]})
    r1 = run_bass_kernel_spmd(nc1, m1, core_ids=cores)
    x1 = [r1.results[c]["x1f"] for c in cores]

    nc2 = _get_nc("s2", build_l0ffn_stage)
    m2 = []
    for c in cores:
        m2.append({"x1f": x1[c], "l0_wrT": sh["l0_wrT"], "l0_br": sh["l0_br"],
                   "l0_aT": sh["l0_aT"], "l0_bT": sh["l0_bT"],
                   "mexp": sh["mexp"], "ident": sh["ident"]})
    r2 = run_bass_kernel_spmd(nc2, m2, core_ids=cores)
    x2 = [r2.results[c]["x2f"] for c in cores]

    nc3 = _get_nc("s3", build_l1attn_stage)
    m3 = []
    for c in cores:
        m = {"x2f": x2[c], "ident": sh["ident"], "jcv": sh["jcv"],
             "posr": sh["posr"], "l1_woT": sh["l1_woT"]}
        for nm in ("l1_qkvT", "l1_qw", "l1_qb", "l1_kw", "l1_kb",
                   "l1_slopes", "l1_strips"):
            m[nm] = pc[c][nm]
        m3.append(m)
    r3 = run_bass_kernel_spmd(nc3, m3, core_ids=cores)
    x3 = [r3.results[c]["x3f"] for c in cores]

    nc4 = _get_nc("s4", build_l1ffn_stage)
    m4 = []
    for c in cores:
        m4.append({"x3f": x3[c], "l1_wT": sh["l1_wT"], "l1_bias": sh["l1_bias"],
                   "l1_laT": sh["l1_laT"], "l1_lbT": sh["l1_lbT"],
                   "l1_lw": sh["l1_lw"], "mexp": sh["mexp"], "ident": sh["ident"]})
    r4 = run_bass_kernel_spmd(nc4, m4, core_ids=cores)
    y = np.concatenate(
        [np.asarray(r4.results[c]["y"]).reshape(128, KC, T)
         .transpose(1, 0, 2).reshape(D, T).T for c in cores], axis=0)
    return y[None, :, :].astype(np.float32)


def emit_l0attn(C, io, ep):
    em = C.em
    stb = C.stb()

    bv_rdy = em.dma("gpsimd", C.stb()[0:1, 0:1024], io["l0_bv"][:, :], "c0g")
    em.wait_fin("tensor", "c0g")

    qsp = io["qspill"].rearrange("(h p) t -> p h t", p=128)
    kgi = io["kgat_in"].rearrange("(kc p) t -> p kc t", p=128)
    vgi = io["vgat_in"].rearrange("(tc p) d -> p tc d", p=128)

    def xr(kc, tb):
        return C.A0[:, kc * 1024 + tb * 512:kc * 1024 + (tb + 1) * 512]

    def extra(jb, tb):
        if jb >= 40:
            return [(C.stb()[0:1, (jb - 40) * 128:(jb - 39) * 128],
                     C.ones_row_b[0:1, :])]
        return []

    def qkv_cb(jb, tb, ps, mmcnt):
        i = (jb * 2 + tb) % 4
        slot = C.SMB[:, 512 * i:512 * (i + 1)]
        em.wait("vector", "mm", mmcnt)
        C.wait_free("vector", ("esl", i))
        if jb < 32:
            em.op("vector", lambda E, o=slot, a=ps:
                  E.tensor_scalar_mul(o, a, ATT_SCALE), inc="dv")
        else:
            em.op("vector", lambda E, o=slot, a=ps: E.tensor_copy(o, a), inc="dv")
        dvc = em.count("dv")
        bankfree = ("dv", dvc)
        if jb < 40:
            em.wait("sync", "dv", dvc)
            dst = (qsp[:, jb, 512 * tb:512 * (tb + 1)] if jb < 32
                   else kgi[:, jb - 32, 512 * tb:512 * (tb + 1)])
            sc = em.dma("sync", dst, slot, "e%d" % i)
            C.set_free(("esl", i), "e%d" % i, sc)
        else:
            dvch = jb - 40
            for tc in range(4):
                pb = 4 + (tc % 2)
                C.wait_free("tensor", ("vtb", pb))
                em.wait("tensor", "dv", dvc)
                em.op("tensor",
                      (lambda E, o=C.bank_bf(pb)[:, 0:128],
                       a=slot[:, 128 * tc:128 * (tc + 1)], ii=C.ident_b[:, :]:
                       E.transpose(o, a, ii)), inc="mm")
                em.wait("vector", "mm")
                C.wait_free("vector", ("vsl", pb))
                vslot = C.stb()[:, 1024 + 128 * (pb - 4):1024 + 128 * (pb - 3)]
                em.op("vector", (lambda E, o=vslot, a=C.bank_bf(pb)[:, 0:128]:
                                 E.tensor_copy(o, a)), inc="dv")
                C.set_free(("vtb", pb), "dv", em.count("dv"))
                em.wait("sync", "dv")
                vc = em.dma("sync", vgi[:, tb * 4 + tc, 128 * dvch:128 * (dvch + 1)],
                            vslot, "v%d" % (pb - 4))
                C.set_free(("vsl", pb), "v%d" % (pb - 4), vc)
            C.set_free(("esl", i), "mm", em.count("mm"))
        return bankfree

    emit_proj(C, "qkv", io["l0_qkvT"], 48, 128, xr, qkv_cb, extra=extra,
              jb_order=list(range(32, 48)) + list(range(32)))
    for t in range(4):
        if ("bank", t) in C.free:
            C.set_free(("ab", t), *C.free[("bank", t)])

    for s in ("g0", "g1", "g2", "g3", "v0", "v1"):
        em.wait("gpsimd", s)
    em.op("gpsimd",
          (lambda E: E.collective_compute(
              "AllGather", ALU.bypass, replica_groups=[list(range(NCORE))],
              ins=[io["kgat_in"].opt()], outs=[io["kgat"].opt()])), inc="cc")
    em.op("gpsimd",
          (lambda E: E.collective_compute(
              "AllGather", ALU.bypass, replica_groups=[list(range(NCORE))],
              ins=[io["vgat_in"].opt()], outs=[io["vgat"].opt()])), inc="cc")
    cc_done = em.count("cc")

    mm_qkv_done = em.count("mm")
    kgs = io["kgat"].rearrange("(r f) t -> f r t", f=1024)
    vgs = io["vgat"].rearrange("(ch p) d -> p ch d", p=128)
    em.wait("sync", "cc", cc_done)

    def load_kv(g):
        if g >= 8:
            return None
        s = g % 2
        C.wait_free("sync", ("kvs", s))
        em.dma("sync",
               C.A0[:, 16384 * s:16384 * s + 8192].rearrange("p (r t) -> p r t", t=1024),
               kgs[128 * g:128 * (g + 1), :, :], "k%d" % s)
        kc_ = em.dma("sync",
                     C.A0[:, 16384 * s + 8192:16384 * (s + 1)].rearrange(
                         "p (ch d) -> p ch d", d=128),
                     vgs[:, :, 128 * g:128 * (g + 1)], "k%d" % s)
        return kc_

    def load_q(qh, g):
        sq = (4 * g + qh) % 2
        C.wait_free("sync", ("qts", sq))
        return em.dma("sync", stb[:, 1024 * sq:1024 * (sq + 1)],
                      qsp[:, 4 * g + qh, :], "q%d" % sq)

    em.wait("sync", "v0")
    em.wait("sync", "v1")
    for s_ in ("e0", "e1", "e2", "e3"):
        em.wait("sync", s_)
    em.wait("sync", "mm", mm_qkv_done)
    kvrdy = {0: load_kv(0), 1: load_kv(1)}
    qrdy = {0: load_q(0, 0)}
    for g in range(8):
        s = g % 2
        kT = C.A0[:, 16384 * s:16384 * s + 8192]
        vT = C.A0[:, 16384 * s + 8192:16384 * (s + 1)]
        em.wait("tensor", "k%d" % s, kvrdy[g])
        for qh in range(4):
            hq = 4 * g + qh
            sq = hq % 2
            em.wait("tensor", "q%d" % sq, qrdy[hq])
            if hq + 1 < 32:
                qrdy[hq + 1] = load_q((hq + 1) % 4, (hq + 1) // 4)
            rhs_q0 = stb[:, 1024 * sq:1024 * sq + 512]
            for t in range(4):
                C.wait_free("tensor", ("bank", t))
                em.op("tensor",
                      (lambda E, o=C.bank(t), l=kT[:, 128 * t:128 * (t + 1)], r=rhs_q0:
                       E.matmul(o, l, r, start=True, stop=True)), inc="mm")
            cmm = em.count("mm")
            em.wait("vector", "mm", cmm)
            em.op("vector", (lambda E, o=C.ST[:, 3072:3073], a=C.PS[:, 0:2048]:
                             E.reduce_max(o, a, axis=AX.X)), inc="dv")
            for t in range(4):
                C.set_free(("bank", t), "dv", em.count("dv"))
            em.wait("tensor", "dv")
            em.op("tensor",
                  (lambda E, o=C.PS[0:1, 2048:2176], a=C.ST[:, 3072:3073],
                   ii=C.ident_f[:, :]: E.transpose(o, a, ii)), inc="mm")
            em.wait("vector", "mm")
            em.op("vector", (lambda E, o=C.RW[0:1, 0:1], a=C.PS[0:1, 2048:2176]:
                             E.reduce_max(o, a, axis=AX.X)), inc="dv")
            em.op("vector", (lambda E, o=C.RW[0:1, 1:2], a=C.RW[0:1, 0:1]:
                             E.tensor_scalar_mul(o, a, -1.0)), inc="dv")
            em.wait("tensor", "dv")
            em.op("tensor",
                  (lambda E, o=C.PS[:, 2048:2049], l=C.ones_row_f[0:1, 0:128],
                   r=C.RW[0:1, 1:2]: E.matmul(o, l, r, start=True, stop=True)),
                  inc="mm")
            em.wait("vector", "mm")
            em.wait("vector", "ac")
            em.op("vector", (lambda E, o=C.ST[:, 3074:3075], a=C.PS[:, 2048:2049]:
                             E.tensor_copy(o, a)), inc="dv")
            negc_dv = em.count("dv")
            em.wait("scalar", "dv", negc_dv)
            acc0 = C.ST[:, 4608:5120]
            for b in range(2):
                rhs_q = stb[:, 1024 * sq + 512 * b:1024 * sq + 512 * (b + 1)]
                pend = []

                def flush_l0(rec, vT=vT):
                    rs_, t0_, fin_ = rec
                    em.wait("tensor", "ac", rs_[1])
                    if fin_:
                        C.wait_free("tensor", ("cbank",))
                    for k2 in range(2):
                        t = t0_ + k2
                        pr = stb[:, 2048 + 1024 * rs_[0] + 512 * k2:
                                 2048 + 1024 * rs_[0] + 512 * (k2 + 1)]
                        st_ = t == 0
                        sp_ = t == 63
                        em.op("tensor",
                              (lambda E, o=C.PS[:, 3584:4096],
                               l=vT[:, 128 * t:128 * (t + 1)], r=pr, s0=st_, s1=sp_:
                               E.matmul(o, l, r, start=s0, stop=s1)),
                              inc=("mm" if k2 == 1 else None))
                    C.set_free(("ring", rs_[0]), "mm", em.count("mm"))

                for p in range(32):
                    t0 = 2 * p
                    pa = (p % 2) * 2
                    rs = p % 4
                    for k2 in range(2):
                        C.wait_free("tensor", ("bank", pa + k2))
                        em.op("tensor",
                              (lambda E, o=C.bank(pa + k2),
                               l=kT[:, 128 * (t0 + k2):128 * (t0 + k2 + 1)],
                               r=rhs_q: E.matmul(o, l, r, start=True, stop=True)),
                              inc="mm")
                    qkmm = em.count("mm")
                    em.wait("scalar", "mm", qkmm)
                    C.wait_free("scalar", ("ring", rs))
                    C.wait_free("scalar", ("ringv", rs))
                    em.op("scalar",
                          (lambda E, o=stb[:, 2048 + 1024 * rs:2048 + 1024 * (rs + 1)],
                           a=C.PS[:, 1024 * (p % 2):1024 * (p % 2) + 1024],
                           bb=C.ST[:, 3074:3075]: E.activation(o, a, AF.Exp, bias=bb)),
                          inc="ac", noser=True)
                    eac = em.count("ac")
                    C.set_free(("bank", pa), "ac", eac)
                    C.set_free(("bank", pa + 1), "ac", eac)
                    em.wait("gpsimd", "ac", eac)
                    pr0 = stb[:, 2048 + 1024 * rs:2048 + 1024 * rs + 512]
                    pr1 = stb[:, 2048 + 1024 * rs + 512:2048 + 1024 * (rs + 1)]
                    if p == 0:
                        C.wait_free("gpsimd", ("accl0",))
                        em.gop(lambda E, o=acc0, a=pr0: E.tensor_copy(o, a))
                    else:
                        em.gop(lambda E, o=acc0, a=pr0: E.tensor_add(o, o, a))
                    em.gop(lambda E, o=acc0, a=pr1: E.tensor_add(o, o, a))
                    C.set_free(("ringv", rs), "gp", em.count("gp"))
                    pend.append(((rs, eac), t0, p == 0))
                    if len(pend) > 2:
                        flush_l0(pend.pop(0))
                while pend:
                    flush_l0(pend.pop(0))
                C.wait_free("tensor", ("sbank",))
                em.wait("tensor", "gp", em.count("gp"))
                em.op("tensor",
                      (lambda E, o=C.PS[0:1, 3072:3584], l=C.ones_col_f[:, :], r=acc0:
                       E.matmul(o, l, r, start=True, stop=True)), inc="mm")
                C.set_free(("accl0",), "mm", em.count("mm"))
                pvmm = em.count("mm")
                em.wait("scalar", "mm", pvmm)
                em.op("scalar", (lambda E, o=C.RW[0:1, 512:1024], a=C.PS[0:1, 3072:3584]:
                                 E.activation(o, a, AF.Ln)), inc="ac")
                em.wait("vector", "ac")
                em.op("vector", (lambda E, o=C.RW[0:1, 512:1024], a=C.RW[0:1, 512:1024]:
                                 E.tensor_scalar_mul(o, a, -1.0)), inc="dv")
                em.wait("scalar", "dv")
                em.op("scalar", (lambda E, o=C.RW[0:1, 1024:1536], a=C.RW[0:1, 512:1024]:
                                 E.activation(o, a, AF.Exp)), inc="ac")
                em.wait("tensor", "ac")
                em.op("tensor",
                      (lambda E, o=C.PS[:, 2560:3072], l=C.ones_row_f[0:1, 0:128],
                       r=C.RW[0:1, 1024:1536]: E.matmul(o, l, r, start=True, stop=True)),
                      inc="mm")
                em.wait("vector", "mm")
                em.op("vector", (lambda E, o=C.ST[:, 3584:4096], a=C.PS[:, 2560:3072]:
                                 E.tensor_copy(o, a)), inc="dv")
                em.op("vector",
                      (lambda E, o=C.A1[:, hq * 1024 + 512 * b:hq * 1024 + 512 * (b + 1)],
                       a=C.PS[:, 3584:4096], bx=C.ST[:, 3584:4096]:
                       E.tensor_mul(o, a, bx)), inc="dv")
                C.set_free(("sbank",), "ac", em.count("ac"))
                C.set_free(("cbank",), "dv", em.count("dv"))
            C.set_free(("qts", sq), "mm", em.count("mm"))
        kvdone = em.count("mm")
        C.set_free(("kvs", s), "mm", kvdone)
        if g + 2 < 8:
            kvrdy[g + 2] = load_kv(g + 2)
    return em.count("dv")


def build_l0attn_stage():
    nc = bass.Bass("TRN2", target_bir_lowering=False, debug=False, num_devices=8)
    io = {}
    io["x_in"] = _decl(nc, "x_in", [128, KC * T], F32, "ExternalInput")
    io["l0_qkvT"] = _decl(nc, "l0_qkvT", [48 * 128, KC * 128], F32, "ExternalInput")
    io["l0_bv"] = _decl(nc, "l0_bv", [1, 1024], F32, "ExternalInput")
    io["l0_woT"] = _decl(nc, "l0_woT", [KC * 128, KC * 128], F32, "ExternalInput")
    io["ident"] = _decl(nc, "ident", [128, 128], F32, "ExternalInput")
    io["qspill"] = _decl(nc, "qspill", [D, T], BF16, "Internal")
    io["kgat_in"] = _decl(nc, "kgat_in", [1024, T], BF16, "Internal")
    io["kgat"] = _decl(nc, "kgat", [S, T], BF16, "Internal")
    io["vgat_in"] = _decl(nc, "vgat_in", [T, 1024], BF16, "Internal")
    io["vgat"] = _decl(nc, "vgat", [S, 1024], BF16, "Internal")
    io["r1"] = _decl(nc, "r1", [128, KC * T], F32, "Internal")
    io["x1f"] = _decl(nc, "x1f", [128, KC * T], F32, "ExternalOutput")
    with ExitStack() as ctx:
        em = Em(nc)
        C = Ctx(nc, em, ctx)
        setup_consts(C, io)
        em.dma("gpsimd", C.A0[:, 0:KC * 1024], io["x_in"][:, :], "c0g")
        em.wait_fin("tensor", "c0g")
        for t in range(4):
            C.free[("ab", t)] = None
        ctx_all = emit_l0attn_with_wo(C, io)
        for i in range(4):
            em.wait("sync", "r%d" % i)
        em.finalize(ctx)
    return nc


def emit_l0attn_with_wo(C, io):
    em = C.em
    ctx_all = emit_l0attn(C, io, None)
    ep = Ep(C, "e1", io["x_in"], io["r1"], io["x1f"], C.A0)
    for i in range(4):
        C.set_free(("xsl", i), "dv", ctx_all)
        C.set_free(("rsl", i), "dv", ctx_all)
    em.wait("tensor", "dv", ctx_all)
    for pb in range(4):
        C.set_free(("bank", pb), C.free.get(("ab", pb), ("dv", 0))[0]
                   if C.free.get(("ab", pb)) else "dv",
                   C.free.get(("ab", pb))[1] if C.free.get(("ab", pb)) else 0)

    def ctx_rhs(kc, tb):
        return C.A1[:, kc * 1024 + tb * 512:kc * 1024 + (tb + 1) * 512]

    emit_proj(C, "wo", io["l0_woT"], KC, 128, ctx_rhs, ep.chunk_cb)
    return ep.finish()


def emit_l1attn(C, io, ep):
    em = C.em
    stb = C.stb()
    RG = [list(range(NCORE))]

    gxc = em.dma("sync", io["xgat_in"].rearrange("(kc p) t -> p kc t", p=128),
                 C.A1[:, :].rearrange("p (kc t) -> p kc t", t=1024), "gx")
    em.wait("gpsimd", "gx", gxc)
    em.op("gpsimd",
          (lambda E: E.collective_compute(
              "AllGather", ALU.bypass, replica_groups=RG,
              ins=[io["xgat_in"].opt()], outs=[io["xgat"].opt()])), inc="cc")
    cc_x = em.count("cc")

    em.dma("sync", C.SM[:, 0:4], io["l1_slopes"][:, :], "c3")
    em.dma("sync", C.SM[:, 4:8], io["l1_qw"][:, :], "c3")
    em.dma("sync", C.SM[:, 8:12], io["l1_qb"][:, :], "c3")
    em.dma("sync", C.SM[:, 12:16], io["l1_kw"][:, :], "c3")
    em.dma("sync", C.SM[:, 16:20], io["l1_kb"][:, :], "c3")
    em.dma("sync", C.SM[:, 20:24], io["jcv"][:, :], "c3")
    em.dma("sync", C.RW[0:1, 2944:3012], io["posr"][:, :], "c3")
    em.wait("vector", "c3")
    em.op("vector", lambda E, o=C.SM[:, 28:32], a=C.SM[:, 4:8]:
          E.tensor_scalar_mul(o, a, ATT_SCALE), inc="dv")
    em.op("vector", lambda E, o=C.SM[:, 32:36], a=C.SM[:, 8:12]:
          E.tensor_scalar_mul(o, a, ATT_SCALE), inc="dv")

    em.wait("sync", "cc", cc_x)
    xg = io["xgat"].rearrange("(r kc p) t -> p r kc t", p=128, kc=KC)
    wq = io["l1_qkvT"].rearrange("(jb p) f -> p jb f", p=128)
    qrs = io["qraw"].rearrange("(h p) t -> p h t", p=128)
    krs = io["kraw"].rearrange("(h p) t -> p h t", p=128)
    vts = io["vtok1"].rearrange("(tc p) d -> p tc d", p=128)
    sqi = io["sqk_in"]

    em.wait("gpsimd", "cc", cc_x)
    groups = (list(range(8)), list(range(8, 12)))
    wg_rdy = {}
    pbctr = [0]
    for gi, grp in enumerate(groups):
        if gi == 1:
            em.wait("gpsimd", "mm")
        for ji, jb in enumerate(grp):
            wg_rdy[jb] = em.dma(
                "gpsimd",
                C.A1[:, ji * 4096:(ji + 1) * 4096],
                wq[:, jb, :], "wg")
        for tblk in range(16):
            s = tblk % 2
            C.wait_free("sync", ("xb", s))
            xrd = em.dma("sync",
                         C.A0[:, 16384 * s:16384 * (s + 1)].rearrange(
                             "p (kc t) -> p kc t", t=512),
                         xg[:, tblk // 2, :, 512 * (tblk % 2):512 * (tblk % 2 + 1)],
                         "xb%d" % s)
            em.wait("tensor", "xb%d" % s, xrd)
            em.wait("tensor", "wg", wg_rdy[grp[-1]])
            for ji, jb in enumerate(grp):
                pb = pbctr[0] % 4
                pbctr[0] += 1
                C.wait_free("tensor", ("bank", pb))
                ps = C.bank(pb)
                for kc in range(KC):
                    em.op("tensor",
                          (lambda E, o=ps, l=C.A1[:, ji * 4096 + kc * 128:ji * 4096 + (kc + 1) * 128],
                           r=C.A0[:, 16384 * s + kc * 512:16384 * s + (kc + 1) * 512],
                           st=(kc == 0), sp=(kc == KC - 1):
                           E.matmul(o, l, r, start=st, stop=sp)),
                          inc=("mm" if kc == KC - 1 else None))
                mmc = em.count("mm")
                ei = pb % 2
                em.wait("vector", "mm", mmc)
                C.wait_free("vector", ("cls", ei))
                cslot = C.SMB[:, 512 * ei:512 * (ei + 1)]
                em.op("vector", lambda E, o=cslot, a=ps:
                      E.tensor_scalar(o, a, 8.0, -8.0, ALU.min, ALU.max), inc="dv")
                C.set_free(("bank", pb), "dv", em.count("dv"))
                if jb < 8:
                    sq_ = C.SMB[:, 1024 + 512 * ei:1024 + 512 * (ei + 1)]
                    C.wait_free("vector", ("sqs", ei))
                    em.op("vector", lambda E, o=sq_, a=cslot: E.tensor_mul(o, a, a),
                          inc="dv")
                    dvc = em.count("dv")
                    em.wait("sync", "dv", dvc)
                    dst = (qrs[:, jb, 512 * tblk:512 * (tblk + 1)] if jb < 4
                           else krs[:, jb - 4, 512 * tblk:512 * (tblk + 1)])
                    sc_ = em.dma("sync", dst, cslot, "p%d" % ei)
                    C.set_free(("cls", ei), "p%d" % ei, sc_)
                    sb0 = 4 if jb < 4 else 6
                    st_ = jb % 4 == 0
                    sp_ = jb % 4 == 3
                    em.wait("tensor", "dv", dvc)
                    C.wait_free("tensor", ("bank", sb0))
                    C.wait_free("tensor", ("bank", sb0 + 1))
                    em.op("tensor",
                          (lambda E, o=C.PS[0:1, 512 * sb0:512 * sb0 + 512],
                           l=C.ones_col_b[:, :], r=cslot, s0=st_, s1=sp_:
                           E.matmul(o, l, r, start=s0, stop=s1)))
                    em.op("tensor",
                          (lambda E, o=C.PS[0:1, 512 * (sb0 + 1):512 * (sb0 + 1) + 512],
                           l=C.ones_col_b[:, :], r=sq_, s0=st_, s1=sp_:
                           E.matmul(o, l, r, start=s0, stop=s1)),
                          inc="mm")
                    C.set_free(("sqs", ei), "mm", em.count("mm"))
                    if sp_:
                        em.wait("vector", "mm")
                        ri = tblk % 2
                        C.wait_free("vector", ("srw", ri))
                        for k2 in range(2):
                            em.op("vector",
                                  (lambda E, o=C.RW[0:1, 1024 * ri + 512 * k2:1024 * ri + 512 * (k2 + 1)],
                                   a=C.PS[0:1, 512 * (sb0 + k2):512 * (sb0 + k2) + 512]:
                                   E.tensor_copy(o, a)), inc="dv")
                        C.set_free(("bank", sb0), "dv", em.count("dv"))
                        C.set_free(("bank", sb0 + 1), "dv", em.count("dv"))
                        em.wait("sync", "dv")
                        row0 = 0 if jb < 4 else 2
                        for k2 in range(2):
                            sc2 = em.dma("sync",
                                         sqi[row0 + k2:row0 + k2 + 1,
                                             512 * tblk:512 * (tblk + 1)],
                                         C.RW[0:1, 1024 * ri + 512 * k2:1024 * ri + 512 * (k2 + 1)],
                                         "sa%d" % ri)
                        C.set_free(("srw", ri), "sa%d" % ri, sc2)
                else:
                    dvch = jb - 8
                    dvc = em.count("dv")
                    for tc in range(4):
                        pt = 4 + (tc % 2)
                        C.wait_free("tensor", ("bank", pt))
                        em.wait("tensor", "dv", dvc)
                        em.op("tensor",
                              (lambda E, o=C.bank_bf(pt)[:, 0:128],
                               a=cslot[:, 128 * tc:128 * (tc + 1)], ii=C.ident_b[:, :]:
                               E.transpose(o, a, ii)), inc="mm")
                        em.wait("vector", "mm")
                        C.wait_free("vector", ("vsl", pt))
                        vslot = C.SMB[:, 1024 + 128 * (pt - 4):1024 + 128 * (pt - 3)]
                        em.op("vector", (lambda E, o=vslot, a=C.bank_bf(pt)[:, 0:128]:
                                         E.tensor_copy(o, a)), inc="dv")
                        C.set_free(("bank", pt), "dv", em.count("dv"))
                        em.wait("sync", "dv")
                        vc = em.dma("sync",
                                    vts[:, 4 * tblk + tc, 128 * dvch:128 * (dvch + 1)],
                                    vslot, "v%d" % (pt - 4))
                        C.set_free(("vsl", pt), "v%d" % (pt - 4), vc)
                    C.set_free(("cls", ei), "mm", em.count("mm"))
    qkv_mm = em.count("mm")

    for s_ in ("sa0", "sa1"):
        em.wait("gpsimd", s_)
    em.op("gpsimd",
          (lambda E: E.collective_compute(
              "AllReduce", ALU.add, replica_groups=RG,
              ins=[io["sqk_in"].opt()], outs=[io["sqk"].opt()])), inc="cc")
    cc_s = em.count("cc")

    em.wait("sync", "cc", cc_s)
    for s_ in ("p0", "p1", "v0", "v1", "sa0", "sa1"):
        em.wait("vector", s_)
        em.wait("sync", s_)
    qns = io["qn"].rearrange("(h p) t -> p h t", p=128)
    kns = io["kn"].rearrange("(h p) t -> p h t", p=128)
    for tblk in range(16):
        C.wait_free("sync", ("strow",))
        src = em.dma("sync",
                     C.RW[0:1, 0:2048].rearrange("p (k t) -> p k t", t=512),
                     io["sqk"][0:4, 512 * tblk:512 * (tblk + 1)], "sr")
        em.wait("vector", "sr", src)
        for qk in range(2):
            srow = C.RW[0:1, 1024 * qk:1024 * qk + 512]
            sqrow = C.RW[0:1, 1024 * qk + 512:1024 * qk + 1024]
            mu = C.RW[0:1, 2048:2560]
            al = C.RW[0:1, 2560:2944]
            em.op("vector", lambda E, o=mu, a=srow: E.tensor_scalar_mul(o, a, 1.0 / D),
                  inc="dv")
            em.op("vector", lambda E, o=sqrow, a=sqrow: E.tensor_scalar_mul(o, a, 1.0 / D),
                  inc="dv")
            em.op("vector", lambda E, o=srow, a=mu, b=mu: E.tensor_mul(o, a, b), inc="dv")
            em.op("vector", lambda E, o=sqrow, a=sqrow, b=srow: E.tensor_sub(o, a, b),
                  inc="dv")
            em.op("vector", lambda E, o=sqrow, a=sqrow: E.tensor_scalar_add(o, a, EPS),
                  inc="dv")
            em.wait("scalar", "dv")
            em.op("scalar", lambda E, o=sqrow, a=sqrow: E.activation(o, a, AF.Ln),
                  inc="ac")
            em.wait("vector", "ac")
            em.op("vector", lambda E, o=sqrow, a=sqrow: E.tensor_scalar_mul(o, a, -0.5),
                  inc="dv")
            em.wait("scalar", "dv")
            em.op("scalar", lambda E, o=sqrow, a=sqrow: E.activation(o, a, AF.Exp),
                  inc="ac")
            em.wait("vector", "ac")
            em.op("vector", lambda E, o=mu, a=mu, b=sqrow: E.tensor_mul(o, a, b), inc="dv")
            em.op("vector", lambda E, o=mu, a=mu: E.tensor_scalar_mul(o, a, -1.0), inc="dv")
            rowsdv = em.count("dv")
            em.wait("tensor", "dv", rowsdv)
            for k2, row in ((0, sqrow), (1, mu)):
                pb = 4 + 2 * qk + k2
                C.wait_free("tensor", ("bank", pb))
                em.op("tensor",
                      (lambda E, o=C.bank(pb), l=C.ones_row_f[0:1, 0:128], r=row:
                       E.matmul(o, l, r, start=True, stop=True)), inc="mm")
            em.wait("vector", "mm")
            for k2 in range(2):
                pb = 4 + 2 * qk + k2
                em.op("vector",
                      (lambda E, o=C.ST[:, 1024 + 1024 * qk + 512 * k2:1024 + 1024 * qk + 512 * (k2 + 1)],
                       a=C.bank(pb): E.tensor_copy(o, a)), inc="dv")
                C.set_free(("bank", pb), "dv", em.count("dv"))
        C.set_free(("strow",), "dv", em.count("dv"))
        repdv = em.count("dv")
        for qk in range(2):
            raws = qrs if qk == 0 else krs
            ns = qns if qk == 0 else kns
            wcol = C.SM[:, 28:32] if qk == 0 else C.SM[:, 12:16]
            bcol = C.SM[:, 32:36] if qk == 0 else C.SM[:, 16:20]
            arep = C.ST[:, 1024 + 1024 * qk:1024 + 1024 * qk + 512]
            brep = C.ST[:, 1024 + 1024 * qk + 512:1024 + 1024 * qk + 1024]
            for ch in range(4):
                mi = ch % 2
                C.wait_free("sync", ("mrs", mi))
                mrd = em.dma("sync", C.SMB[:, 512 * mi:512 * (mi + 1)],
                             raws[:, ch, 512 * tblk:512 * (tblk + 1)], "m%d" % mi)
                em.wait("vector", "m%d" % mi, mrd)
                em.wait("vector", "dv", repdv)
                tmp = C.ST[:, 512 * mi:512 * (mi + 1)]
                C.wait_free("vector", ("tmp", mi))
                em.op("vector", (lambda E, o=tmp, a=C.SMB[:, 512 * mi:512 * (mi + 1)],
                                 b=arep: E.tensor_mul(o, a, b)), inc="dv")
                C.set_free(("mrs", mi), "dv", em.count("dv"))
                em.op("vector", (lambda E, o=tmp, a=tmp, b=brep: E.tensor_add(o, a, b)),
                      inc="dv")
                ni = 2 + mi
                C.wait_free("vector", ("nsl", ni))
                em.op("vector",
                      (lambda E, o=C.SMB[:, 512 * ni:512 * (ni + 1)], a=tmp,
                       w=wcol[:, ch:ch + 1], bb=bcol[:, ch:ch + 1]:
                       E.tensor_scalar(o, a, w, bb, ALU.mult, ALU.add)), inc="dv")
                C.set_free(("tmp", mi), "dv", em.count("dv"))
                em.wait("sync", "dv")
                nc_ = em.dma("sync", ns[:, ch, 512 * tblk:512 * (tblk + 1)],
                             C.SMB[:, 512 * ni:512 * (ni + 1)], "n%d" % mi)
                C.set_free(("nsl", ni), "n%d" % mi, nc_)
    norm_dv = em.count("dv")

    kts = io["kn"].rearrange("(h p) t -> p h t", p=128)
    for s_ in ("n0", "n1"):
        em.wait("sync", s_)
    for h in range(HPC):
        C.wait_free("sync", ("hbuf",))
        em.dma("sync", C.A0[:, 0:8192], kts[:, h, :], "k0")
        em.dma("sync",
               C.A0[:, 8192:16384].rearrange("p (ch d) -> p ch d", d=128),
               vts[:, :, 128 * h:128 * (h + 1)], "k0")
        krd = em.dma("sync", C.A0[:, 16384:24576], qns[:, h, :], "k0")
        C.wait_free("sync", ("strips",))
        srd = em.dma("sync",
                     C.ST[:, 0:2560].rearrange("p (v j) -> p v j", j=512),
                     io["l1_strips"].rearrange("(h v p) j -> p h v j", p=128, v=5)[:, h, :, :],
                     "k0")
        em.wait("vector", "k0", srd)
        em.op("vector", (lambda E, o=C.SM[:, 24:28], a=C.SM[:, 20:24],
                         sl=C.SM[:, h:h + 1]:
                         E.tensor_scalar(o, a, sl, None, ALU.mult)), inc="dv")
        em.wait("tensor", "k0", srd)
        kT = C.A0[:, 0:8192]
        vT = C.A0[:, 8192:16384]
        if h == 0:
            for r_ in range(4):
                C.set_free(("ring", r_), "mm", qkv_mm)
        rhs_q0 = C.A0[:, 16384:16896]
        for jc in range(4):
            C.wait_free("tensor", ("bank", jc))
            em.op("tensor",
                  (lambda E, o=C.bank(jc), l=kT[:, 128 * jc:128 * (jc + 1)], r=rhs_q0:
                   E.matmul(o, l, r, start=True, stop=True)), inc="mm")
        dmm = em.count("mm")
        em.wait("vector", "mm", dmm)
        for jc in range(4):
            em.op("vector",
                  (lambda E, o=C.ST[:, 2560 + 512 * jc:2560 + 512 * (jc + 1)],
                   a=C.bank(jc), s=C.SM[:, 24 + jc:25 + jc],
                   strip=C.ST[:, 512 * (1 + jc):512 * (2 + jc)]:
                   E.scalar_tensor_tensor(o, a, s, strip, ALU.add, ALU.add)),
                  inc="dv")
            C.set_free(("bank", jc), "dv", em.count("dv"))
        em.op("vector", (lambda E, o=C.ST[:, 5188:5189], a=C.ST[:, 2560:4608]:
                         E.reduce_max(o, a, axis=AX.X)), inc="dv")
        em.wait("tensor", "dv")
        em.op("tensor",
              (lambda E, o=C.PS[0:1, 2048:2176], a=C.ST[:, 5188:5189],
               ii=C.ident_f[:, :]: E.transpose(o, a, ii)), inc="mm")
        em.wait("vector", "mm")
        em.op("vector", (lambda E, o=C.RW[0:1, 0:1], a=C.PS[0:1, 2048:2176]:
                         E.reduce_max(o, a, axis=AX.X)), inc="dv")
        for b in range(16):
            nt = 4 * b + 4
            rhs_q = C.A0[:, 16384 + 512 * b:16384 + 512 * (b + 1)]
            em.op("vector", (lambda E, o=C.RW[0:1, 2:3], a=C.SM[0:1, h:h + 1], b_=b:
                             E.tensor_scalar_mul(o, a, 512.0 * b_)), inc="dv")
            em.op("vector", (lambda E, o=C.RW[0:1, 2:3], a=C.RW[0:1, 2:3],
                             b=C.RW[0:1, 0:1]: E.tensor_add(o, a, b)), inc="dv")
            em.op("vector",
                  (lambda E, o=C.RW[0:1, 4:4 + nt], a=C.RW[0:1, 2944:2944 + nt],
                   s1=C.SM[0:1, h:h + 1], s2=C.RW[0:1, 2:3], n_=nt:
                   E.tensor_scalar(o, a, s1, s2, ALU.mult, ALU.subtract)), inc="dv")
            bdv = em.count("dv")
            em.wait("tensor", "dv", bdv)
            em.op("tensor",
                  (lambda E, o=C.PS[:, 2048:2048 + nt], l=C.ones_row_f[0:1, 0:128],
                   r=C.RW[0:1, 4:4 + nt], n_=nt: E.matmul(o, l, r, start=True, stop=True)),
                  inc="mm")
            em.wait("vector", "mm")
            em.op("vector", (lambda E, o=C.ST[:, 4608:4608 + nt], a=C.PS[:, 2048:2048 + nt]:
                             E.tensor_copy(o, a)), inc="dv")
            pend = []
            acc1 = C.ST[:, 5504:6016]

            def flush_l1(rec, vT=vT, b=b):
                rs_, eac_, t0_ = rec
                em.wait("tensor", "ac", eac_)
                if t0_ == 0:
                    C.wait_free("tensor", ("cbank",))
                for k2 in range(2):
                    t = t0_ + k2
                    pr = C.A1[:, 1024 * rs_ + 512 * k2:1024 * rs_ + 512 * (k2 + 1)]
                    st_ = t == 0
                    sp_ = t == 4 * b + 3
                    em.op("tensor",
                          (lambda E, o=C.PS[:, 3584:4096],
                           l=vT[:, 128 * t:128 * (t + 1)], r=pr, s0=st_, s1=sp_:
                           E.matmul(o, l, r, start=s0, stop=s1)),
                          inc=("mm" if k2 == 1 else None))
                C.set_free(("ring", rs_), "mm", em.count("mm"))

            for p_ in range(nt // 2):
                t0 = 2 * p_
                rs = p_ % 4
                for k2, t in enumerate((t0, t0 + 1)):
                    C.wait_free("tensor", ("bank", k2))
                    em.op("tensor",
                          (lambda E, o=C.bank(k2), l=kT[:, 128 * t:128 * (t + 1)],
                           r=rhs_q: E.matmul(o, l, r, start=True, stop=True)), inc="mm")
                qkmm = em.count("mm")
                em.wait("vector", "mm", qkmm)
                for k2, t in enumerate((t0, t0 + 1)):
                    v_ = 0 if t < 4 * b else 1 + (t - 4 * b)
                    C.wait_free("vector", ("bank", 2 + k2))
                    em.op("vector",
                          (lambda E, o=C.bank(2 + k2), a=C.bank(k2),
                           s=C.ST[:, 4608 + t:4609 + t],
                           strip=C.ST[:, 512 * v_:512 * (v_ + 1)]:
                           E.scalar_tensor_tensor(o, a, s, strip, ALU.add, ALU.add)),
                          inc="dv")
                    C.set_free(("bank", k2), "dv", em.count("dv"))
                sttdv = em.count("dv")
                em.wait("scalar", "dv", sttdv)
                C.wait_free("scalar", ("ring", rs))
                C.wait_free("scalar", ("ringv", rs))
                em.op("scalar",
                      (lambda E, o=C.A1[:, 1024 * rs:1024 * (rs + 1)],
                       a=C.PS[:, 1024:2048]:
                       E.activation(o, a, AF.Exp)), inc="ac", noser=True)
                eac = em.count("ac")
                C.set_free(("bank", 2), "ac", eac)
                C.set_free(("bank", 3), "ac", eac)
                em.wait("gpsimd", "ac", eac)
                pr0 = C.A1[:, 1024 * rs:1024 * rs + 512]
                pr1 = C.A1[:, 1024 * rs + 512:1024 * (rs + 1)]
                if p_ == 0:
                    C.wait_free("gpsimd", ("accl0",))
                    em.gop(lambda E, o=acc1, a=pr0: E.tensor_copy(o, a))
                else:
                    em.gop(lambda E, o=acc1, a=pr0: E.tensor_add(o, o, a))
                em.gop(lambda E, o=acc1, a=pr1: E.tensor_add(o, o, a))
                C.set_free(("ringv", rs), "gp", em.count("gp"))
                pend.append((rs, eac, t0))
                if len(pend) > 2:
                    flush_l1(pend.pop(0))
            while pend:
                flush_l1(pend.pop(0))
            C.wait_free("tensor", ("sbank",))
            em.wait("tensor", "gp", em.count("gp"))
            em.op("tensor",
                  (lambda E, o=C.PS[0:1, 3072:3584], l=C.ones_col_f[:, :], r=acc1:
                   E.matmul(o, l, r, start=True, stop=True)), inc="mm")
            C.set_free(("accl0",), "mm", em.count("mm"))
            pvmm = em.count("mm")
            em.wait("scalar", "mm", pvmm)
            em.op("scalar", (lambda E, o=C.RW[0:1, 512:1024], a=C.PS[0:1, 3072:3584]:
                             E.activation(o, a, AF.Ln)), inc="ac")
            em.wait("vector", "ac")
            em.op("vector", (lambda E, o=C.RW[0:1, 512:1024], a=C.RW[0:1, 512:1024]:
                             E.tensor_scalar_mul(o, a, -1.0)), inc="dv")
            em.wait("scalar", "dv")
            em.op("scalar", (lambda E, o=C.RW[0:1, 1024:1536], a=C.RW[0:1, 512:1024]:
                             E.activation(o, a, AF.Exp)), inc="ac")
            em.wait("tensor", "ac")
            em.op("tensor",
                  (lambda E, o=C.PS[:, 2560:3072], l=C.ones_row_f[0:1, 0:128],
                   r=C.RW[0:1, 1024:1536]: E.matmul(o, l, r, start=True, stop=True)),
                  inc="mm")
            em.wait("vector", "mm")
            em.op("vector", (lambda E, o=C.ST[:, 4676:5188], a=C.PS[:, 2560:3072]:
                             E.tensor_copy(o, a)), inc="dv")
            C.wait_free("vector", ("ctxh",))
            em.op("vector",
                  (lambda E, o=C.A0[:, 24576 + 512 * b:24576 + 512 * (b + 1)],
                   a=C.PS[:, 3584:4096], bx=C.ST[:, 4676:5188]:
                   E.tensor_mul(o, a, bx)), inc="dv")
            C.set_free(("sbank",), "ac", em.count("ac"))
            C.set_free(("cbank",), "dv", em.count("dv"))
        em.wait("sync", "dv")
        cgc = em.dma("sync",
                     io["ctxg_in"].rearrange("(h p) t -> p h t", p=128)[:, h, :],
                     C.A0[:, 24576:32768], "cg")
        C.set_free(("ctxh",), "cg", cgc)
        C.set_free(("hbuf",), "mm", em.count("mm"))
        C.set_free(("strips",), "dv", em.count("dv"))
    attn_mm = em.count("mm")
    attn_ac = em.count("ac")
    attn_dv = em.count("dv")

    em.wait("gpsimd", "cg")
    em.op("gpsimd",
          (lambda E: E.collective_compute(
              "AllGather", ALU.bypass, replica_groups=RG,
              ins=[io["ctxg_in"].opt()], outs=[io["ctxg"].opt()])), inc="cc")
    cc_c = em.count("cc")

    em.wait("gpsimd", "cc", cc_c)
    em.wait("gpsimd", "mm", attn_mm)
    cgr = io["ctxg"].rearrange("(kc p) t -> p kc t", p=128)
    clc = em.op(
        "gpsimd",
        (lambda E: E.dma_start(
            out=C.A1[:, :].rearrange("p (kc t) -> p kc t", t=1024),
            in_=cgr[:, :, bass.ts(E.partition_id(), 1024)])),
        inc="cl", amt=16)

    for i in range(4):
        C.set_free(("xsl", i), "dv", attn_dv)
        C.set_free(("rsl", i), "dv", attn_dv)
    em.wait("tensor", "cl", clc)

    def ctx_rhs(kc, tb):
        return C.A1[:, kc * 1024 + tb * 512:kc * 1024 + (tb + 1) * 512]

    emit_proj(C, "wo1", io["l1_woT"], KC, 128, ctx_rhs, ep.chunk_cb)
    return ep.finish()


def build_l1attn_stage():
    nc = bass.Bass("TRN2", target_bir_lowering=False, debug=False, num_devices=8)
    io = {}
    io["x2f"] = _decl(nc, "x2f", [128, KC * T], F32, "ExternalInput")
    io["l1_qkvT"] = _decl(nc, "l1_qkvT", [12 * 128, KC * 128], F32, "ExternalInput")
    for nm in ("l1_qw", "l1_qb", "l1_kw", "l1_kb"):
        io[nm] = _decl(nc, nm, [128, 4], F32, "ExternalInput")
    io["l1_slopes"] = _decl(nc, "l1_slopes", [128, 4], F32, "ExternalInput")
    io["l1_strips"] = _decl(nc, "l1_strips", [4 * 5 * 128, 512], F32, "ExternalInput")
    io["l1_woT"] = _decl(nc, "l1_woT", [KC * 128, KC * 128], F32, "ExternalInput")
    io["ident"] = _decl(nc, "ident", [128, 128], F32, "ExternalInput")
    io["jcv"] = _decl(nc, "jcv", [128, 4], F32, "ExternalInput")
    io["posr"] = _decl(nc, "posr", [1, 68], F32, "ExternalInput")
    io["xgat_in"] = _decl(nc, "xgat_in", [D, T], BF16, "Internal")
    io["xgat"] = _decl(nc, "xgat", [NCORE * D, T], BF16, "Internal")
    io["qraw"] = _decl(nc, "qraw", [512, S], BF16, "Internal")
    io["kraw"] = _decl(nc, "kraw", [512, S], BF16, "Internal")
    io["vtok1"] = _decl(nc, "vtok1", [S, 512], BF16, "Internal")
    io["qn"] = _decl(nc, "qn", [512, S], BF16, "Internal")
    io["kn"] = _decl(nc, "kn", [512, S], BF16, "Internal")
    io["sqk_in"] = _decl(nc, "sqk_in", [4, S], F32, "Internal")
    io["sqk"] = _decl(nc, "sqk", [4, S], F32, "Internal")
    io["ctxg_in"] = _decl(nc, "ctxg_in", [512, S], BF16, "Internal")
    io["ctxg"] = _decl(nc, "ctxg", [D, S], BF16, "Internal")
    io["r3"] = _decl(nc, "r3", [128, KC * T], F32, "Internal")
    io["x3f"] = _decl(nc, "x3f", [128, KC * T], F32, "ExternalOutput")
    with ExitStack() as ctx:
        em = Em(nc)
        C = Ctx(nc, em, ctx)
        setup_consts(C, io)
        em.dma("gpsimd", C.A1[:, 0:KC * 1024], io["x2f"][:, :], "c0g")
        em.wait_fin("sync", "c0g")
        ep = Ep(C, "e3", io["x2f"], io["r3"], io["x3f"], C.A0)
        emit_l1attn(C, io, ep)
        for i in range(4):
            em.wait("sync", "r%d" % i)
        em.finalize(ctx)
    return nc


def emit_l1ffn(C, io, bprev, ep):
    em = C.em

    def xr(kc, tb):
        return bprev[:, kc * 1024 + tb * 512:kc * 1024 + (tb + 1) * 512]

    la_rdy = em.dma("gpsimd", C.A1[:, 0:4096], io["l1_laT"][:, :], "lw1")
    lb_rdy = em.dma("gpsimd", C.A1[:, 4096:8192], io["l1_lbT"][:, :], "lw1")
    em.dma("sync", C.SM[0:8, 36:37], io["l1_lw"][:, :], "c4")
    em.dma("sync", C.SM[0:8, 80:208], io["mexp"][:, :], "c4")
    em.dma("sync", C.SM[:, 48:80], io["l1_bias"][:, :], "c4")
    em.wait("tensor", "c4")
    em.wait("tensor", "lw1", lb_rdy)

    C.wait_free("tensor", ("bank", 6))
    em.op("tensor",
          (lambda E, o=C.PS[:, 512 * 6:512 * 6 + 1], l=C.SM[0:8, 80:208],
           r=C.SM[0:8, 36:37]: E.matmul(o, l, r, start=True, stop=True)), inc="mm")
    em.wait("vector", "mm")
    em.op("vector", (lambda E, o=C.SM[:, 40:41], a=C.PS[:, 512 * 6:512 * 6 + 1]:
                     E.tensor_scalar_mul(o, a, 2.0)), inc="dv")
    C.set_free(("bank", 6), "dv", em.count("dv"))

    for tb in (0, 1):
        pb = 4 + tb
        C.wait_free("tensor", ("bank", pb))
        ps = C.bank(pb)
        for kc in range(KC):
            em.op("tensor",
                  (lambda E, o=ps, l=C.A1[:, kc * 128:(kc + 1) * 128],
                   r=xr(kc, tb), st=(kc == 0), sp=(kc == KC - 1):
                   E.matmul(o, l, r, start=st, stop=sp)),
                  inc=("mm" if kc == KC - 1 else None))
        em.wait("vector", "mm")
        em.op("vector",
              (lambda E, o=C.RW[:, :].bitcast(BF16)[:, 4864 + 512 * tb:4864 + 512 * (tb + 1)],
               a=ps, s=C.SM[:, 40:41]: E.tensor_scalar(o, a, s, None, ALU.mult)),
              inc="dv")
        C.set_free(("bank", pb), "dv", em.count("dv"))
    low_dv = em.count("dv")

    em.wait("tensor", "dv", low_dv)

    def extra(jb, tb):
        return [(C.A1[:, 4096 + jb * 128:4096 + (jb + 1) * 128],
                 C.RW[:, :].bitcast(BF16)[:, 4864 + 512 * tb:4864 + 512 * (tb + 1)])]

    emit_proj(C, "f1", io["l1_wT"], KC, 128, xr, ep.chunk_cb, extra=extra)
    return ep.finish()


def build_l1ffn_stage():
    nc = bass.Bass("TRN2", target_bir_lowering=False, debug=False, num_devices=8)
    io = {}
    io["x3f"] = _decl(nc, "x3f", [128, KC * T], F32, "ExternalInput")
    io["l1_wT"] = _decl(nc, "l1_wT", [KC * 128, KC * 128], F32, "ExternalInput")
    io["l1_bias"] = _decl(nc, "l1_bias", [128, 32], F32, "ExternalInput")
    io["l1_laT"] = _decl(nc, "l1_laT", [128, KC * 128], F32, "ExternalInput")
    io["l1_lbT"] = _decl(nc, "l1_lbT", [128, D], F32, "ExternalInput")
    io["l1_lw"] = _decl(nc, "l1_lw", [8, 1], F32, "ExternalInput")
    io["mexp"] = _decl(nc, "mexp", [8, 128], F32, "ExternalInput")
    io["ident"] = _decl(nc, "ident", [128, 128], F32, "ExternalInput")
    io["r4"] = _decl(nc, "r4", [128, KC * T], F32, "Internal")
    io["y"] = _decl(nc, "y", [128, KC * T], F32, "ExternalOutput")
    with ExitStack() as ctx:
        em = Em(nc)
        C = Ctx(nc, em, ctx)
        setup_consts(C, io)
        em.dma("gpsimd", C.A0[:, 0:KC * 1024], io["x3f"][:, :], "c0g")
        em.wait_fin("tensor", "c0g")
        ep = Ep(C, "e4", io["x3f"], io["r4"], io["y"], None,
                bias_cols=C.SM[:, 48:80])
        emit_l1ffn(C, io, C.A0, ep)
        for i in range(4):
            em.wait("sync", "r%d" % i)
        em.finalize(ctx)
    return nc


def build_fused():
    nc = bass.Bass("TRN2", target_bir_lowering=False, debug=False, num_devices=8)
    io = {}
    EI = lambda n, s, d=F32: _decl(nc, n, s, d, "ExternalInput")
    IN = lambda n, s, d=F32: _decl(nc, n, s, d, "Internal")
    io["x_in"] = EI("x_in", [128, KC * T])
    io["ident"] = EI("ident", [128, 128])
    io["mexp"] = EI("mexp", [8, 128])
    io["jcv"] = EI("jcv", [128, 4])
    io["posr"] = EI("posr", [1, 68])
    io["l0_qkvT"] = EI("l0_qkvT", [48 * 128, KC * 128])
    io["l0_bv"] = EI("l0_bv", [1, 1024])
    io["l0_woT"] = EI("l0_woT", [KC * 128, KC * 128])
    io["l0_wrT"] = EI("l0_wrT", [128, 256])
    io["l0_br"] = EI("l0_br", [1, 8])
    io["l0_aT"] = EI("l0_aT", [128, KC * 128])
    io["l0_bT"] = EI("l0_bT", [128, D])
    io["l1_qkvT"] = EI("l1_qkvT", [12 * 128, KC * 128])
    for nm in ("l1_qw", "l1_qb", "l1_kw", "l1_kb", "l1_slopes"):
        io[nm] = EI(nm, [128, 4])
    io["l1_strips"] = EI("l1_strips", [4 * 5 * 128, 512])
    io["l1_woT"] = EI("l1_woT", [KC * 128, KC * 128])
    io["l1_wT"] = EI("l1_wT", [KC * 128, KC * 128])
    io["l1_bias"] = EI("l1_bias", [128, 32])
    io["l1_laT"] = EI("l1_laT", [128, KC * 128])
    io["l1_lbT"] = EI("l1_lbT", [128, D])
    io["l1_lw"] = EI("l1_lw", [8, 1])
    io["qspill"] = IN("qspill", [D, T], BF16)
    io["kgat_in"] = IN("kgat_in", [1024, T], BF16)
    io["kgat"] = IN("kgat", [S, T], BF16)
    io["vgat_in"] = IN("vgat_in", [T, 1024], BF16)
    io["vgat"] = IN("vgat", [S, 1024], BF16)
    io["r1"] = IN("r1", [128, KC * T]); io["x1f"] = IN("x1f", [128, KC * T])
    io["r2"] = IN("r2", [128, KC * T]); io["x2f"] = IN("x2f", [128, KC * T])
    io["r3"] = IN("r3", [128, KC * T]); io["x3f"] = IN("x3f", [128, KC * T])
    io["r4"] = IN("r4", [128, KC * T])
    io["xgat_in"] = IN("xgat_in", [D, T], BF16)
    io["xgat"] = IN("xgat", [NCORE * D, T], BF16)
    io["qraw"] = IN("qraw", [512, S], BF16)
    io["kraw"] = IN("kraw", [512, S], BF16)
    io["vtok1"] = IN("vtok1", [S, 512], BF16)
    io["qn"] = IN("qn", [512, S], BF16)
    io["kn"] = IN("kn", [512, S], BF16)
    io["sqk_in"] = IN("sqk_in", [4, S])
    io["sqk"] = IN("sqk", [4, S])
    io["ctxg_in"] = IN("ctxg_in", [512, S], BF16)
    io["ctxg"] = IN("ctxg", [D, S], BF16)
    io["y"] = _decl(nc, "y", [128, KC * T], F32, "ExternalOutput")

    with ExitStack() as ctx:
        em = Em(nc)
        C = Ctx(nc, em, ctx)
        setup_consts(C, io)
        em.dma("gpsimd", C.A0[:, 0:KC * 1024], io["x_in"][:, :], "c0g")
        em.wait_fin("tensor", "c0g")

        def barrier(done):
            for eng in ("tensor", "vector", "scalar", "gpsimd", "sync"):
                em.wait(eng, "dv", done)
            for i in range(4):
                em.wait("sync", "r%d" % i)

        d1 = emit_l0attn_with_wo(C, io)
        barrier(d1[1])
        ep2 = Ep(C, "e2", io["x1f"], io["r2"], io["x2f"], C.A1)
        d2 = emit_l0ffn(C, io, C.A0, ep2)
        barrier(d2[1])
        ep3 = Ep(C, "e3", io["x2f"], io["r3"], io["x3f"], C.A0)
        d3 = emit_l1attn(C, io, ep3)
        barrier(d3[1])
        ep4 = Ep(C, "e4", io["x3f"], io["r4"], io["y"], None,
                 bias_cols=C.SM[:, 48:80])
        emit_l1ffn(C, io, C.A0, ep4)
        for i in range(4):
            em.wait("sync", "r%d" % i)
        em.finalize(ctx)
    return nc


def kernel_fused(**inputs):
    sh, pc = host_prep(inputs)
    cores = list(range(NCORE))
    nc = _get_nc("fused", build_fused)
    maps = []
    for c in cores:
        m = {}
        for nm in ("ident", "mexp", "jcv", "posr", "l0_qkvT", "l0_bv", "l0_woT",
                   "l0_wrT", "l0_br", "l0_aT", "l0_bT", "l1_woT", "l1_wT",
                   "l1_bias", "l1_laT", "l1_lbT", "l1_lw"):
            m[nm] = sh[nm]
        for nm in ("x_in", "l1_qkvT", "l1_qw", "l1_qb", "l1_kw", "l1_kb",
                   "l1_slopes", "l1_strips"):
            m[nm] = pc[c][nm]
        maps.append(m)
    res = run_bass_kernel_spmd(nc, maps, core_ids=cores)
    y = np.concatenate(
        [np.asarray(res.results[c]["y"]).reshape(128, KC, T)
         .transpose(1, 0, 2).reshape(D, T).T for c in cores], axis=0)
    return y[None, :, :].astype(np.float32)


# version 10
# speedup vs baseline: 1.0134x; 1.0134x over previous
import numpy as np
import ml_dtypes
from contextlib import ExitStack

import concourse.bass as bass
import concourse.mybir as mybir
from concourse.bass_utils import run_bass_kernel_spmd

F32 = mybir.dt.float32
BF16 = mybir.dt.bfloat16
AF = mybir.ActivationFunctionType
ALU = mybir.AluOpType
AX = mybir.AxisListType

NCORE = 8
S = 8192
D = 4096
T = S // NCORE
KC = D // 128
TB = T // 512
EPS = 1e-5
NEG = -1.0e9

L0_NQ, L0_NKV, HD = 32, 8, 128
ATT_SCALE = HD ** -0.5
L1_NH = 32
HPC = L1_NH // NCORE


AUTO_SERIAL = True


class Em:
    ENGS = ("tensor", "vector", "scalar", "gpsimd", "sync")

    def __init__(self, nc):
        self.nc = nc
        self.streams = {e: [] for e in self.ENGS}
        self.counts = {}

    def count(self, name):
        return self.counts.get(name, 0)

    def wait(self, eng, name, val=None):
        self.counts.setdefault(name, 0)
        v = self.counts[name] if val is None else val
        if v > 0:
            self.streams[eng].append(("w", name, v))
        return v

    ENG_SEM = {"vector": "dv", "scalar": "ac"}

    def op(self, eng, fn, inc=None, amt=1, noser=False):
        selfwait = None
        if eng in self.ENG_SEM:
            es = self.ENG_SEM[eng]
            assert inc is None or inc == es, (eng, inc)
            inc = es
            amt = 1
            self.counts.setdefault(es, 0)
            if AUTO_SERIAL and not noser and self.counts[es] > 0:
                selfwait = (es, self.counts[es])
        if inc is not None:
            self.counts.setdefault(inc, 0)
            self.counts[inc] += amt
        self.streams[eng].append(("o", fn, inc, amt, selfwait))
        return self.counts[inc] if inc is not None else None

    def wait_fin(self, eng, name):
        self.counts.setdefault(name, 0)
        self.streams[eng].append(("wf", name))

    def gop(self, fn):
        prev = self.counts.get("gp", 0)
        selfwait = ("gp", prev) if prev > 0 else None
        self.counts["gp"] = prev + 1
        self.streams["gpsimd"].append(("o", fn, "gp", 1, selfwait))
        return self.counts["gp"]

    def dma(self, eng, out, in_, inc, **kw):
        selfwait = None
        self.counts.setdefault(inc, 0)
        self.counts[inc] += 16
        self.streams[eng].append(
            ("o", (lambda E, o=out, i=in_, k=dict(kw): E.dma_start(out=o, in_=i, **k)),
             inc, 16, selfwait))
        return self.counts[inc]

    def finalize(self, ctx):
        handles = {n: ctx.enter_context(self.nc.semaphore(n)) for n in self.counts}
        block = ctx.enter_context(self.nc.Block())
        for eng in self.ENGS:
            items = self.streams[eng]
            if not items:
                continue

            def body(E, items=items):
                for it in items:
                    if it[0] == "w":
                        E.wait_ge(handles[it[1]], it[2])
                    elif it[0] == "wf":
                        if self.counts[it[1]] > 0:
                            E.wait_ge(handles[it[1]], self.counts[it[1]])
                    else:
                        _, fn, inc, amt, selfwait = it
                        if selfwait is not None:
                            E.wait_ge(handles[selfwait[0]], selfwait[1])
                        ins = fn(E)
                        if inc is not None:
                            ins.then_inc(handles[inc], amt)

            getattr(block, eng)(body)


class Ctx:

    def __init__(self, nc, em, ctx):
        self.nc = nc
        self.em = em
        self.free = {}
        E = ctx.enter_context
        self.A0 = E(nc.sbuf_tensor("A0", [128, 32768], BF16))
        self.A1 = E(nc.sbuf_tensor("A1", [128, 32768], BF16))
        self.ST = E(nc.sbuf_tensor("ST", [128, 6144], F32))
        self.WS = E(nc.sbuf_tensor("WS", [128, 8192], BF16))
        self.RW = E(nc.sbuf_tensor("RW", [128, 3072], F32))
        self.ident_f = E(nc.sbuf_tensor("ident_f", [128, 128], F32))
        self.ident_b = E(nc.sbuf_tensor("ident_b", [128, 128], BF16))
        self.ones_col_b = E(nc.sbuf_tensor("ones_col_b", [128, 1], BF16))
        self.ones_col_f = E(nc.sbuf_tensor("ones_col_f", [128, 1], F32))
        self.ones_row_b = E(nc.sbuf_tensor("ones_row_b", [1, 512], BF16))
        self.ones_row_f = E(nc.sbuf_tensor("ones_row_f", [1, 512], F32))
        self.SM = E(nc.sbuf_tensor("SM", [128, 1024], F32))
        self.SMB = E(nc.sbuf_tensor("SMB", [128, 2048], BF16))
        self.PS = E(nc.psum_tensor("PS", [128, 4096], F32))

    def bank(self, i, n=1):
        return self.PS[:, 512 * i:512 * (i + n)]

    def bank_bf(self, i, n=1):
        return self.PS[:, :].bitcast(BF16)[:, 1024 * i:1024 * (i + n)]

    def stb(self):
        return self.ST[:, :].bitcast(BF16)

    def wait_free(self, eng, key):
        fr = self.free.get(key)
        if fr:
            self.em.wait(eng, fr[0], fr[1])

    def set_free(self, key, sem, cnt):
        self.free[key] = (sem, cnt)


def emit_proj(C, tag, wt, njb, jw, x_rhs, out_cb, extra=None, nkc=KC,
              banks=(0, 1, 2, 3), tbs=(0, 1), jb_order=None):
    em = C.em
    nc = C.nc
    wsrc = wt.rearrange("(jb p) f -> p jb f", p=128)
    wready = {}
    order = list(jb_order) if jb_order is not None else list(range(njb))
    slot_of = {jb: idx % 2 for idx, jb in enumerate(order)}

    def issue_w(jb):
        if jb is None:
            return
        s = slot_of[jb]
        C.wait_free("gpsimd", ("WS", s))
        dst = C.WS[:, 4096 * s:4096 * s + nkc * jw]
        wready[jb] = em.dma("gpsimd", dst, wsrc[:, jb, :], "w%d" % s)

    issue_w(order[0])
    issue_w(order[1])
    for idx, jb in enumerate(order):
        s = slot_of[jb]
        em.wait("tensor", "w%d" % s, wready[jb])
        for ti, tb in enumerate(tbs):
            pb = banks[(idx % 2) * 2 + ti]
            C.wait_free("tensor", ("bank", pb))
            ps = C.PS[0:jw, 512 * pb:512 * pb + 512]
            ex = extra(jb, tb) if extra is not None else []
            for kc in range(nkc):
                last = (kc == nkc - 1) and not ex
                em.op("tensor",
                      (lambda E, o=ps, l=C.WS[:, 4096 * s + kc * jw:4096 * s + (kc + 1) * jw],
                       r=x_rhs(kc, tb), st=(kc == 0), sp=last:
                       E.matmul(o, l, r, start=st, stop=sp)),
                      inc=("mm" if last else None))
            for i, (lt, rr) in enumerate(ex):
                last = i == len(ex) - 1
                em.op("tensor",
                      (lambda E, o=ps, l=lt, r=rr, sp=last:
                       E.matmul(o, l, r, start=False, stop=sp)),
                      inc=("mm" if last else None))
            mmcnt = em.count("mm")
            fsem, fcnt = out_cb(jb, tb, ps, mmcnt)
            C.set_free(("bank", pb), fsem, fcnt)
        C.set_free(("WS", s), "mm", em.count("mm"))
        issue_w(order[idx + 2] if idx + 2 < njb else None)


class Ep:

    def __init__(self, C, tag, xprev, r_out, xf_out, bbuf, bias_cols=None):
        self.C = C
        self.tag = tag
        self.xprev = xprev
        self.r_out = r_out
        self.xf_out = xf_out
        self.bbuf = bbuf
        self.bias_cols = bias_cols
        self.pending = []
        self.first = {}

    def chunk_cb(self, jb, tb, ps, mmcnt):
        C, em = self.C, self.C.em
        tag = self.tag
        i = (jb * 2 + tb) % 4
        xsl = C.ST[:, 512 * i:512 * (i + 1)]
        rsl = C.ST[:, 2048 + 512 * i:2048 + 512 * (i + 1)]
        rbf = C.SMB[:, 1024 * (i % 2):1024 * (i % 2) + 512]
        sq = C.SMB[:, 1024 * (i % 2) + 512:1024 * (i % 2) + 1024]
        C.wait_free("sync", ("xsl", i))
        off = jb * 1024 + 512 * tb
        xc = em.dma("sync", xsl, self.xprev[:, off:off + 512], "x%d" % i)
        em.wait("vector", "mm", mmcnt)
        em.wait("vector", "x%d" % i, xc)
        C.wait_free("vector", ("rsl", i))
        C.wait_free("vector", ("rbf", i % 2))
        if self.bias_cols is not None:
            em.op("vector",
                  (lambda E, o=rsl, a=ps, b=xsl, s=self.bias_cols[:, jb:jb + 1]:
                   E.scalar_tensor_tensor(o, a, s, b, ALU.add, ALU.add)), inc="dv")
        else:
            em.op("vector", lambda E, o=rsl, a=ps, b=xsl: E.tensor_add(o, a, b), inc="dv")
        bankfree = ("dv", em.count("dv"))
        C.set_free(("xsl", i), "dv", em.count("dv"))
        em.op("vector", lambda E, o=rbf, a=rsl: E.tensor_copy(o, a), inc="dv")
        em.op("vector", lambda E, o=sq, a=rbf: E.tensor_mul(o, a, a), inc="dv")
        dvc = em.count("dv")
        em.wait("sync", "dv", dvc - 2)
        rs = em.dma("sync", self.r_out[:, off:off + 512], rsl, "r%d" % i)
        C.set_free(("rsl", i), "r%d" % i, rs)
        self.pending.append((jb, tb, rbf, sq, dvc, i % 2))
        if len(self.pending) > 1:
            self._flush_stat(self.pending.pop(0))
        return bankfree

    def _flush_stat(self, rec):
        C, em = self.C, self.C.em
        jb, tb, rbf, sq, dvc, ri = rec
        em.wait("tensor", "dv", dvc)
        st = (jb == 0)
        sp = (jb == KC - 1)
        em.op("tensor",
              (lambda E, o=C.PS[0:1, 512 * (4 + tb):512 * (4 + tb) + 512], l=C.ones_col_b[:, :],
               r=rbf, s0=st, s1=sp: E.matmul(o, l, r, start=s0, stop=s1)),
              inc=("mm" if sp else None))
        em.op("tensor",
              (lambda E, o=C.PS[0:1, 512 * (6 + tb):512 * (6 + tb) + 512], l=C.ones_col_b[:, :],
               r=sq, s0=st, s1=sp: E.matmul(o, l, r, start=s0, stop=s1)),
              inc=("mm" if sp else None))
        C.set_free(("rbf", ri), "mm", em.count("mm"))

    def finish(self):
        C, em = self.C, self.C.em
        tag = self.tag
        while self.pending:
            self._flush_stat(self.pending.pop(0))
        em.wait("vector", "mm")
        R = C.RW
        for tb in (0, 1):
            mu = R[0:1, 512 * 0:512 * 1]
            msq = R[0:1, 512 * 1:512 * 2]
            al = R[0:1, 512 * 2:512 * 3]
            be = R[0:1, 512 * 3:512 * 4]
            em.op("vector", lambda E, o=mu, a=C.PS[0:1, 512 * (4 + tb):512 * (4 + tb) + 512]:
                  E.tensor_scalar_mul(o, a, 1.0 / D), inc="dv")
            em.op("vector", lambda E, o=msq, a=C.PS[0:1, 512 * (6 + tb):512 * (6 + tb) + 512]:
                  E.tensor_scalar_mul(o, a, 1.0 / D), inc="dv")
            em.op("vector", lambda E, o=be, a=mu, b=mu: E.tensor_mul(o, a, b), inc="dv")
            em.op("vector", lambda E, o=msq, a=msq, b=be: E.tensor_sub(o, a, b), inc="dv")
            em.op("vector", lambda E, o=msq, a=msq: E.tensor_scalar_add(o, a, EPS), inc="dv")
            em.wait("scalar", "dv")
            em.op("scalar", lambda E, o=msq, a=msq: E.activation(o, a, AF.Ln), inc="ac")
            em.wait("vector", "ac")
            em.op("vector", lambda E, o=msq, a=msq: E.tensor_scalar_mul(o, a, -0.5), inc="dv")
            em.wait("scalar", "dv")
            em.op("scalar", lambda E, o=al, a=msq: E.activation(o, a, AF.Exp), inc="ac")
            em.wait("vector", "ac")
            em.op("vector", lambda E, o=be, a=mu, b=al: E.tensor_mul(o, a, b), inc="dv")
            em.op("vector", lambda E, o=be, a=be: E.tensor_scalar_mul(o, a, -1.0), inc="dv")
            em.wait("tensor", "dv")
            for k, row in ((0, al), (1, be)):
                pb = tb * 2 + k
                C.wait_free("tensor", ("bank", pb))
                em.op("tensor",
                      (lambda E, o=C.bank(pb), l=C.ones_row_f[0:1, 0:128], r=row:
                       E.matmul(o, l, r, start=True, stop=True)), inc="mm")
            em.wait("vector", "mm")
            for k in (0, 1):
                pb = tb * 2 + k
                em.op("vector",
                      (lambda E,
                       o=C.ST[:, 4096 + 1024 * tb + 512 * k:4096 + 1024 * tb + 512 * (k + 1)],
                       a=C.bank(pb): E.tensor_copy(o, a)), inc="dv")
                C.set_free(("bank", pb), "dv", em.count("dv"))
        spbase = em.count("dv")
        for i in range(4):
            em.wait("sync", "r%d" % i)
            em.wait("gpsimd", "r%d" % i)
        for jb in range(KC):
            for tb in (0, 1):
                i = (jb * 2 + tb) % 4
                off = jb * 1024 + 512 * tb
                xsl = C.ST[:, 512 * i:512 * (i + 1)]
                rsl = C.ST[:, 2048 + 512 * i:2048 + 512 * (i + 1)]
                arep = C.ST[:, 4096 + 1024 * tb:4096 + 1024 * tb + 512]
                brep = C.ST[:, 4096 + 1024 * tb + 512:4096 + 1024 * tb + 1024]
                C.wait_free("gpsimd", ("xsl", i))
                rc = em.dma("gpsimd", xsl, self.r_out[:, off:off + 512], "x%d" % i)
                em.wait("vector", "x%d" % i, rc)
                em.wait("vector", "dv", spbase)
                C.wait_free("vector", ("rsl", i))
                em.op("vector", lambda E, o=rsl, a=xsl, b=arep: E.tensor_mul(o, a, b), inc="dv")
                C.set_free(("xsl", i), "dv", em.count("dv"))
                em.op("vector", lambda E, o=rsl, a=rsl, b=brep: E.tensor_add(o, a, b), inc="dv")
                dvc = em.count("dv")
                em.wait("sync", "dv", dvc)
                rs = em.dma("sync", self.xf_out[:, off:off + 512], rsl, "r%d" % i)
                C.set_free(("rsl", i), "r%d" % i, rs)
                if self.bbuf is not None:
                    em.op("vector",
                          (lambda E,
                           o=self.bbuf[:, jb * 1024 + 512 * tb:jb * 1024 + 512 * (tb + 1)],
                           a=rsl: E.tensor_copy(o, a)), inc="dv")
        return ("dv", em.count("dv"))


def setup_consts(C, io):
    em = C.em
    em.dma("sync", C.ident_f[:, :], io["ident"][:, :], "c0")
    em.dma("gpsimd", C.ident_b[:, :], io["ident"][:, :], "c0g")
    em.op("vector", lambda E, t=C.ones_col_b: E.memset(t[:, :], 1.0), inc="dv")
    em.op("vector", lambda E, t=C.ones_col_f: E.memset(t[:, :], 1.0), inc="dv")
    em.op("vector", lambda E, t=C.ones_row_b: E.memset(t[:, :], 1.0), inc="dv")
    em.op("vector", lambda E, t=C.ones_row_f: E.memset(t[:, :], 1.0), inc="dv")
    em.wait_fin("tensor", "c0")
    em.wait_fin("tensor", "c0g")
    em.wait("tensor", "dv")
    em.wait("scalar", "dv")


def emit_l0ffn(C, io, bprev, ep):
    em = C.em
    tag = "f0"

    def xr(kc, tb):
        return bprev[:, kc * 1024 + tb * 512:kc * 1024 + (tb + 1) * 512]

    C.wait_free("gpsimd", ("WS", 0))
    C.wait_free("gpsimd", ("WS", 1))
    wr_rdy = em.dma("gpsimd", C.WS[:, 0:256], io["l0_wrT"][:, :], "wf0")
    at_rdy = em.dma("gpsimd", C.WS[:, 256:4352], io["l0_aT"][:, :], "wf0")
    C.wait_free("gpsimd", ("rbf", 1))
    br_rdy = em.dma("gpsimd", C.SMB[0:1, 2040:2048], io["l0_br"][:, :], "wf0")
    mx_rdy = em.dma("sync", C.RW[0:8, 2048:2176], io["mexp"][:, :], "c2")

    em.wait("tensor", "wf0", br_rdy)
    for tb in (0, 1):
        pb = 4 + tb
        ps = C.PS[0:8, 512 * pb:512 * pb + 512]
        for kc in range(KC):
            em.op("tensor",
                  (lambda E, o=ps, l=C.WS[:, kc * 8:(kc + 1) * 8], r=xr(kc, tb),
                   st=(kc == 0): E.matmul(o, l, r, start=st, stop=False)))
        em.op("tensor",
              (lambda E, o=ps, l=C.SMB[0:1, 2040:2048], r=C.ones_row_b[0:1, :]:
               E.matmul(o, l, r, start=False, stop=True)), inc="mm")
    lg_mm = em.count("mm")
    em.wait("vector", "mm", lg_mm)
    for i_ in range(2):
        C.wait_free("vector", ("xsl", i_))
    for tb in (0, 1):
        em.op("vector",
              (lambda E, o=C.ST[0:8, 512 * tb:512 * (tb + 1)],
               a=C.PS[0:8, 512 * (4 + tb):512 * (4 + tb) + 512]:
               E.tensor_copy(o, a)), inc="dv")
    lg_dv = em.count("dv")

    em.wait("tensor", "dv", lg_dv)
    em.wait("tensor", "c2", mx_rdy)
    for tb in (0, 1):
        for tc in range(4):
            gtc = tb * 4 + tc
            lgt = C.RW[:, 2176:2184]
            et = C.RW[:, 2184:2192]
            negmax = C.RW[:, 2192:2193]
            ssum = C.RW[:, 2193:2194]
            rc = C.RW[:, 2194:2195]
            sct = C.RW[:, 2196:2204]
            scT = C.RW[0:8, 2204:2332]
            em.wait("tensor", "dv")
            em.op("tensor",
                  (lambda E, o=C.PS[:, 512 * 6:512 * 6 + 8],
                   a=C.ST[0:8, gtc * 128:(gtc + 1) * 128], i=C.ident_f[0:8, 0:8]:
                   E.transpose(o, a, i)), inc="mm")
            em.wait("vector", "mm")
            em.op("vector", lambda E, o=lgt, a=C.PS[:, 512 * 6:512 * 6 + 8]:
                  E.tensor_copy(o, a), inc="dv")
            em.op("vector", lambda E, o=negmax, a=lgt:
                  E.reduce_max(o, a, axis=AX.X, negate=True), inc="dv")
            em.wait("scalar", "dv")
            em.op("scalar", lambda E, o=et, a=lgt, b=negmax, s=ssum:
                  E.activation(o, a, AF.Exp, bias=b, accum_out=s), inc="ac")
            em.wait("vector", "ac")
            em.op("vector", lambda E, o=rc, a=ssum: E.reciprocal(o, a), inc="dv")
            em.op("vector", lambda E, o=sct, a=et, s=rc:
                  E.tensor_scalar(o, a, s, 16.0, ALU.mult, ALU.mult), inc="dv")
            em.wait("tensor", "dv")
            em.op("tensor",
                  (lambda E, o=C.PS[0:8, 512 * 6 + 128:512 * 6 + 256], a=sct, i=C.ident_f[:, :]:
                   E.transpose(o, a, i)), inc="mm")
            em.wait("vector", "mm")
            em.op("vector", lambda E, o=scT, a=C.PS[0:8, 512 * 6 + 128:512 * 6 + 256]:
                  E.tensor_copy(o, a), inc="dv")
            em.wait("tensor", "dv")
            em.op("tensor",
                  (lambda E, o=C.PS[:, 512 * 7 + 128 * tc:512 * 7 + 128 * (tc + 1)],
                   l=C.RW[0:8, 2048:2176], r=scT:
                   E.matmul(o, l, r, start=True, stop=True)), inc="mm")
        em.wait("vector", "mm")
        em.op("vector", (lambda E, o=C.ST[:, 4096 + 512 * tb:4096 + 512 * (tb + 1)], a=C.bank(7):
                         E.tensor_copy(o, a)), inc="dv")
        if "dbg_srep" in io:
            em.wait("sync", "dv")
            em.dma("sync", io["dbg_srep"][:, 512 * tb:512 * (tb + 1)],
                   C.ST[:, 4096 + 512 * tb:4096 + 512 * (tb + 1)], "dg")
    srep_dv = em.count("dv")
    C.set_free(("xsl", 0), "mm", em.count("mm"))
    C.set_free(("xsl", 1), "mm", em.count("mm"))

    em.wait("tensor", "wf0", br_rdy)
    em.wait("tensor", "dv", srep_dv)
    for tb in (0, 1):
        pb = 4 + tb
        ps = C.bank(pb)
        for kc in range(KC):
            em.op("tensor",
                  (lambda E, o=ps, l=C.WS[:, 256 + kc * 128:256 + (kc + 1) * 128],
                   r=xr(kc, tb), st=(kc == 0), sp=(kc == KC - 1):
                   E.matmul(o, l, r, start=st, stop=sp)),
                  inc=("mm" if kc == KC - 1 else None))
        em.wait("vector", "mm")
        em.op("vector",
              (lambda E, o=C.RW[:, :].bitcast(BF16)[:, 4864 + 512 * tb:4864 + 512 * (tb + 1)],
               a=ps, b=C.ST[:, 4096 + 512 * tb:4096 + 512 * (tb + 1)]:
               E.tensor_mul(o, a, b)), inc="dv")
        if "dbg_lowS" in io:
            em.wait("sync", "dv")
            em.dma("sync", io["dbg_lowS"][:, 512 * tb:512 * (tb + 1)],
                   C.RW[:, :].bitcast(BF16)[:, 4864 + 512 * tb:4864 + 512 * (tb + 1)], "dg")
    low_dv = em.count("dv")

    em.wait("gpsimd", "mm")
    bt_rdy = em.dma("gpsimd", C.WS[:, 256:4352], io["l0_bT"][:, :], "wb")

    if "dbg_lowS" in io:
        em.wait("vector", "dg")
    if "dbg_bt" in io:
        em.wait("sync", "wb", bt_rdy)
        em.dma("sync", io["dbg_bt"][:, :], C.WS[:, 256:4352], "dg2")
        em.wait("tensor", "dg2")
    em.wait("tensor", "wb", bt_rdy)
    em.wait("tensor", "dv", low_dv)
    for jb in range(KC):
        for tb in (0, 1):
            pb = (jb % 2) * 2 + tb
            C.wait_free("tensor", ("bank", pb))
            ps = C.bank(pb)
            em.op("tensor",
                  (lambda E, o=ps, l=C.WS[:, 256 + jb * 128:256 + (jb + 1) * 128],
                   r=C.RW[:, :].bitcast(BF16)[:, 4864 + 512 * tb:4864 + 512 * (tb + 1)]:
                   E.matmul(o, l, r, start=True, stop=True)), inc="mm")
            fs = ep.chunk_cb(jb, tb, ps, em.count("mm"))
            C.set_free(("bank", pb), fs[0], fs[1])
    return ep.finish()


def _f(a):
    return np.ascontiguousarray(a, dtype=np.float32)


def _bm(wt, njb, jw):
    nkc = wt.shape[0] // 128
    return _f(wt.reshape(nkc, 128, njb, jw).transpose(2, 1, 0, 3)
              .reshape(njb * 128, nkc * jw))


def host_prep(inp):
    sh = {}
    x = inp["L0_attn_hidden_states"][0]
    xT = _f(x.T)
    sh["ident"] = np.eye(128, dtype=np.float32)
    sh["mexp"] = np.repeat(np.eye(8, dtype=np.float32), 16, axis=1)
    sh["jcv"] = np.broadcast_to(np.arange(4, dtype=np.float32)[None, :] * 128.0,
                                (128, 4)).copy()
    sh["posr"] = (np.arange(68, dtype=np.float32)[None, :] * 128.0).copy()
    sh["l0_qkvT"] = _bm(np.concatenate(
        [inp["L0_attn_Wq"], inp["L0_attn_Wk"], inp["L0_attn_Wv"]], axis=0).T, 48, 128)
    sh["l0_bv"] = _f(inp["L0_attn_bv"][None, :])
    sh["l0_woT"] = _bm(inp["L0_attn_Wo"].T, KC, 128)
    sh["l0_wrT"] = _bm(inp["L0_ffn_Wr"].T, 1, 8)
    sh["l0_br"] = _f(inp["L0_ffn_br"][None, :])
    sh["l0_aT"] = _bm(inp["L0_ffn_A"].transpose(2, 0, 1).reshape(D, 128), 1, 128)
    sh["l0_bT"] = _f(inp["L0_ffn_B"].transpose(0, 2, 1).reshape(128, D))
    sh["l1_woT"] = _bm(inp["L1_attn_w_out"].T, KC, 128)
    sh["l1_wT"] = _bm(inp["L1_ffn_W"].T, KC, 128)
    sh["l1_bias"] = _f(inp["L1_ffn_bias"].reshape(32, 128).T)
    sh["l1_laT"] = _bm(inp["L1_ffn_lora_A"].transpose(2, 0, 1).reshape(D, 128), 1, 128)
    sh["l1_lbT"] = _f(inp["L1_ffn_lora_B"].transpose(0, 2, 1).reshape(128, D))
    sh["l1_lw"] = _f(inp["L1_ffn_lora_weights"][:, None])

    wqkvT = _f(inp["L1_attn_w_qkv"].T)
    percore = []
    import math
    slopes = 1.0 / (2.0 ** (np.arange(1, 33, dtype=np.float64) * (8.0 / 32)))
    ii = np.arange(128)[:, None]
    jj = np.arange(512)[None, :]
    for c in range(NCORE):
        pc = {}
        pc["x_in"] = _f(xT[:, c * T:(c + 1) * T].reshape(KC, 128, T)
                        .transpose(1, 0, 2).reshape(128, KC * T))
        cols = np.concatenate([np.arange(512 * c, 512 * c + 512),
                               4096 + np.arange(512 * c, 512 * c + 512),
                               8192 + np.arange(512 * c, 512 * c + 512)])
        pc["l1_qkvT"] = _bm(wqkvT[:, cols], 12, 128)
        for nm, key in (("qw", "L1_attn_q_ln_w"), ("qb", "L1_attn_q_ln_b"),
                        ("kw", "L1_attn_k_ln_w"), ("kb", "L1_attn_k_ln_b")):
            pc["l1_" + nm] = _f(inp[key][512 * c:512 * (c + 1)].reshape(4, 128).T)
        sl = slopes[4 * c:4 * c + 4]
        pc["l1_slopes"] = _f(np.broadcast_to(sl[None, :], (128, 4)))
        strips = np.zeros((4, 5, 128, 512), dtype=np.float64)
        for h in range(4):
            base = sl[h] * (ii - jj)
            strips[h, 0] = base
            for jc in range(4):
                m = np.where(ii <= jj - 128 * jc, base, NEG)
                strips[h, 1 + jc] = m
        pc["l1_strips"] = strips.astype(np.float32).reshape(4 * 5 * 128, 512)
        percore.append(pc)
    return sh, percore


def _decl(nc, name, shape, dtype, kind):
    return nc.dram_tensor(name, list(shape), dtype, kind=kind).ap()


def build_l0ffn_stage(dbg=False):
    nc = bass.Bass("TRN2", target_bir_lowering=False, debug=False, num_devices=8)
    io = {}
    io["x1f"] = _decl(nc, "x1f", [128, KC * T], F32, "ExternalInput")
    for nm, shp in (("l0_wrT", [128, 256]), ("l0_br", [1, 8]), ("l0_aT", [128, KC * 128]),
                    ("l0_bT", [128, D]), ("mexp", [8, 128]), ("ident", [128, 128])):
        io[nm] = _decl(nc, nm, shp, F32, "ExternalInput")
    io["r2"] = _decl(nc, "r2", [128, KC * T], F32, "Internal")
    io["x2f"] = _decl(nc, "x2f", [128, KC * T], F32, "ExternalOutput")
    if dbg:
        io["dbg_lg"] = _decl(nc, "dbg_lg", [8, 1024], F32, "ExternalOutput")
        io["dbg_srep"] = _decl(nc, "dbg_srep", [128, 1024], F32, "ExternalOutput")
        io["dbg_lowS"] = _decl(nc, "dbg_lowS", [128, 1024], BF16, "ExternalOutput")
        io["dbg_bt"] = _decl(nc, "dbg_bt", [128, 4096], BF16, "ExternalOutput")
    with ExitStack() as ctx:
        em = Em(nc)
        C = Ctx(nc, em, ctx)
        setup_consts(C, io)
        bld = em.dma("gpsimd", C.A0[:, 0:KC * 1024], io["x1f"][:, :], "c0g")
        em.wait_fin("tensor", "c0g")
        ep = Ep(C, "e2", io["x1f"], io["r2"], io["x2f"], C.A1)
        emit_l0ffn(C, io, C.A0, ep)
        for i in range(4):
            em.wait("sync", "r%d" % i)
        em.finalize(ctx)
    return nc


_NC_CACHE = {}


def _get_nc(name, builder):
    if name not in _NC_CACHE:
        _NC_CACHE[name] = builder()
    return _NC_CACHE[name]


def kernel(**inputs):
    return kernel_fused(**inputs)


def kernel_multilaunch(**inputs):
    sh, pc = host_prep(inputs)
    cores = list(range(NCORE))

    nc1 = _get_nc("s1", build_l0attn_stage)
    m1 = []
    for c in cores:
        m1.append({"x_in": pc[c]["x_in"], "l0_qkvT": sh["l0_qkvT"],
                   "l0_bv": sh["l0_bv"], "l0_woT": sh["l0_woT"],
                   "ident": sh["ident"]})
    r1 = run_bass_kernel_spmd(nc1, m1, core_ids=cores)
    x1 = [r1.results[c]["x1f"] for c in cores]

    nc2 = _get_nc("s2", build_l0ffn_stage)
    m2 = []
    for c in cores:
        m2.append({"x1f": x1[c], "l0_wrT": sh["l0_wrT"], "l0_br": sh["l0_br"],
                   "l0_aT": sh["l0_aT"], "l0_bT": sh["l0_bT"],
                   "mexp": sh["mexp"], "ident": sh["ident"]})
    r2 = run_bass_kernel_spmd(nc2, m2, core_ids=cores)
    x2 = [r2.results[c]["x2f"] for c in cores]

    nc3 = _get_nc("s3", build_l1attn_stage)
    m3 = []
    for c in cores:
        m = {"x2f": x2[c], "ident": sh["ident"], "jcv": sh["jcv"],
             "posr": sh["posr"], "l1_woT": sh["l1_woT"]}
        for nm in ("l1_qkvT", "l1_qw", "l1_qb", "l1_kw", "l1_kb",
                   "l1_slopes", "l1_strips"):
            m[nm] = pc[c][nm]
        m3.append(m)
    r3 = run_bass_kernel_spmd(nc3, m3, core_ids=cores)
    x3 = [r3.results[c]["x3f"] for c in cores]

    nc4 = _get_nc("s4", build_l1ffn_stage)
    m4 = []
    for c in cores:
        m4.append({"x3f": x3[c], "l1_wT": sh["l1_wT"], "l1_bias": sh["l1_bias"],
                   "l1_laT": sh["l1_laT"], "l1_lbT": sh["l1_lbT"],
                   "l1_lw": sh["l1_lw"], "mexp": sh["mexp"], "ident": sh["ident"]})
    r4 = run_bass_kernel_spmd(nc4, m4, core_ids=cores)
    y = np.concatenate(
        [np.asarray(r4.results[c]["y"]).reshape(128, KC, T)
         .transpose(1, 0, 2).reshape(D, T).T for c in cores], axis=0)
    return y[None, :, :].astype(np.float32)


def emit_l0attn(C, io, ep):
    em = C.em
    stb = C.stb()

    bv_rdy = em.dma("gpsimd", C.stb()[0:1, 0:1024], io["l0_bv"][:, :], "c0g")
    em.wait_fin("tensor", "c0g")

    qsp = io["qspill"].rearrange("(h p) t -> p h t", p=128)
    kgi = io["kgat_in"].rearrange("(kc p) t -> p kc t", p=128)
    vgi = io["vgat_in"].rearrange("(tc p) d -> p tc d", p=128)

    def xr(kc, tb):
        return C.A0[:, kc * 1024 + tb * 512:kc * 1024 + (tb + 1) * 512]

    def extra(jb, tb):
        if jb >= 40:
            return [(C.stb()[0:1, (jb - 40) * 128:(jb - 39) * 128],
                     C.ones_row_b[0:1, :])]
        return []

    def qkv_cb(jb, tb, ps, mmcnt):
        i = (jb * 2 + tb) % 4
        slot = C.SMB[:, 512 * i:512 * (i + 1)]
        em.wait("vector", "mm", mmcnt)
        C.wait_free("vector", ("esl", i))
        if jb < 32:
            em.op("vector", lambda E, o=slot, a=ps:
                  E.tensor_scalar_mul(o, a, ATT_SCALE), inc="dv")
        else:
            em.op("vector", lambda E, o=slot, a=ps: E.tensor_copy(o, a), inc="dv")
        dvc = em.count("dv")
        bankfree = ("dv", dvc)
        if jb < 40:
            em.wait("sync", "dv", dvc)
            dst = (qsp[:, jb, 512 * tb:512 * (tb + 1)] if jb < 32
                   else kgi[:, jb - 32, 512 * tb:512 * (tb + 1)])
            sc = em.dma("sync", dst, slot, "e%d" % i)
            C.set_free(("esl", i), "e%d" % i, sc)
        else:
            dvch = jb - 40
            for tc in range(4):
                pb = 4 + (tc % 2)
                C.wait_free("tensor", ("vtb", pb))
                em.wait("tensor", "dv", dvc)
                em.op("tensor",
                      (lambda E, o=C.bank_bf(pb)[:, 0:128],
                       a=slot[:, 128 * tc:128 * (tc + 1)], ii=C.ident_b[:, :]:
                       E.transpose(o, a, ii)), inc="mm")
                em.wait("vector", "mm")
                C.wait_free("vector", ("vsl", pb))
                vslot = C.stb()[:, 1024 + 128 * (pb - 4):1024 + 128 * (pb - 3)]
                em.op("vector", (lambda E, o=vslot, a=C.bank_bf(pb)[:, 0:128]:
                                 E.tensor_copy(o, a)), inc="dv")
                C.set_free(("vtb", pb), "dv", em.count("dv"))
                em.wait("sync", "dv")
                vc = em.dma("sync", vgi[:, tb * 4 + tc, 128 * dvch:128 * (dvch + 1)],
                            vslot, "v%d" % (pb - 4))
                C.set_free(("vsl", pb), "v%d" % (pb - 4), vc)
            C.set_free(("esl", i), "mm", em.count("mm"))
        return bankfree

    emit_proj(C, "qkv", io["l0_qkvT"], 48, 128, xr, qkv_cb, extra=extra,
              jb_order=list(range(32, 48)) + list(range(32)))
    for t in range(4):
        if ("bank", t) in C.free:
            C.set_free(("ab", t), *C.free[("bank", t)])

    for s in ("g0", "g1", "g2", "g3", "v0", "v1"):
        em.wait("gpsimd", s)
    em.op("gpsimd",
          (lambda E: E.collective_compute(
              "AllGather", ALU.bypass, replica_groups=[list(range(NCORE))],
              ins=[io["kgat_in"].opt()], outs=[io["kgat"].opt()])), inc="cc")
    em.op("gpsimd",
          (lambda E: E.collective_compute(
              "AllGather", ALU.bypass, replica_groups=[list(range(NCORE))],
              ins=[io["vgat_in"].opt()], outs=[io["vgat"].opt()])), inc="cc")
    cc_done = em.count("cc")

    mm_qkv_done = em.count("mm")
    kgs = io["kgat"].rearrange("(r f) t -> f r t", f=1024)
    vgs = io["vgat"].rearrange("(ch p) d -> p ch d", p=128)
    em.wait("sync", "cc", cc_done)

    def load_kv(g):
        if g >= 8:
            return None
        s = g % 2
        C.wait_free("sync", ("kvs", s))
        em.dma("sync",
               C.A0[:, 16384 * s:16384 * s + 8192].rearrange("p (r t) -> p r t", t=1024),
               kgs[128 * g:128 * (g + 1), :, :], "k%d" % s)
        kc_ = em.dma("sync",
                     C.A0[:, 16384 * s + 8192:16384 * (s + 1)].rearrange(
                         "p (ch d) -> p ch d", d=128),
                     vgs[:, :, 128 * g:128 * (g + 1)], "k%d" % s)
        return kc_

    def load_q(qh, g):
        sq = (4 * g + qh) % 2
        C.wait_free("sync", ("qts", sq))
        return em.dma("sync", stb[:, 1024 * sq:1024 * (sq + 1)],
                      qsp[:, 4 * g + qh, :], "q%d" % sq)

    em.wait("sync", "v0")
    em.wait("sync", "v1")
    for s_ in ("e0", "e1", "e2", "e3"):
        em.wait("sync", s_)
    em.wait("sync", "mm", mm_qkv_done)
    kvrdy = {0: load_kv(0), 1: load_kv(1)}
    qrdy = {0: load_q(0, 0)}
    for g in range(8):
        s = g % 2
        kT = C.A0[:, 16384 * s:16384 * s + 8192]
        vT = C.A0[:, 16384 * s + 8192:16384 * (s + 1)]
        em.wait("tensor", "k%d" % s, kvrdy[g])
        for qh in range(4):
            hq = 4 * g + qh
            sq = hq % 2
            em.wait("tensor", "q%d" % sq, qrdy[hq])
            if hq + 1 < 32:
                qrdy[hq + 1] = load_q((hq + 1) % 4, (hq + 1) // 4)
            rhs_q0 = stb[:, 1024 * sq:1024 * sq + 512]
            for t in range(4):
                C.wait_free("tensor", ("bank", t))
                em.op("tensor",
                      (lambda E, o=C.bank(t), l=kT[:, 128 * t:128 * (t + 1)], r=rhs_q0:
                       E.matmul(o, l, r, start=True, stop=True)), inc="mm")
            cmm = em.count("mm")
            em.wait("vector", "mm", cmm)
            em.op("vector", (lambda E, o=C.ST[:, 3072:3073], a=C.PS[:, 0:2048]:
                             E.reduce_max(o, a, axis=AX.X)), inc="dv")
            for t in range(4):
                C.set_free(("bank", t), "dv", em.count("dv"))
            em.wait("tensor", "dv")
            em.op("tensor",
                  (lambda E, o=C.PS[0:1, 2048:2176], a=C.ST[:, 3072:3073],
                   ii=C.ident_f[:, :]: E.transpose(o, a, ii)), inc="mm")
            em.wait("vector", "mm")
            em.op("vector", (lambda E, o=C.RW[0:1, 0:1], a=C.PS[0:1, 2048:2176]:
                             E.reduce_max(o, a, axis=AX.X)), inc="dv")
            em.op("vector", (lambda E, o=C.RW[0:1, 1:2], a=C.RW[0:1, 0:1]:
                             E.tensor_scalar_mul(o, a, -1.0)), inc="dv")
            em.wait("tensor", "dv")
            em.op("tensor",
                  (lambda E, o=C.PS[:, 2048:2049], l=C.ones_row_f[0:1, 0:128],
                   r=C.RW[0:1, 1:2]: E.matmul(o, l, r, start=True, stop=True)),
                  inc="mm")
            em.wait("vector", "mm")
            em.wait("vector", "ac")
            em.op("vector", (lambda E, o=C.ST[:, 3074:3075], a=C.PS[:, 2048:2049]:
                             E.tensor_copy(o, a)), inc="dv")
            negc_dv = em.count("dv")
            em.wait("scalar", "dv", negc_dv)
            acc0 = C.ST[:, 4608:5120]
            for b in range(2):
                rhs_q = stb[:, 1024 * sq + 512 * b:1024 * sq + 512 * (b + 1)]
                pend = []

                def flush_l0(rec, vT=vT):
                    rs_, t0_, fin_ = rec
                    em.wait("tensor", "ac", rs_[1])
                    if fin_:
                        C.wait_free("tensor", ("cbank",))
                    for k2 in range(2):
                        t = t0_ + k2
                        pr = stb[:, 2048 + 1024 * rs_[0] + 512 * k2:
                                 2048 + 1024 * rs_[0] + 512 * (k2 + 1)]
                        st_ = t == 0
                        sp_ = t == 63
                        em.op("tensor",
                              (lambda E, o=C.PS[:, 3584:4096],
                               l=vT[:, 128 * t:128 * (t + 1)], r=pr, s0=st_, s1=sp_:
                               E.matmul(o, l, r, start=s0, stop=s1)),
                              inc=("mm" if k2 == 1 else None))
                    C.set_free(("ring", rs_[0]), "mm", em.count("mm"))

                for p in range(32):
                    t0 = 2 * p
                    pa = (p % 2) * 2
                    rs = p % 4
                    for k2 in range(2):
                        C.wait_free("tensor", ("bank", pa + k2))
                        em.op("tensor",
                              (lambda E, o=C.bank(pa + k2),
                               l=kT[:, 128 * (t0 + k2):128 * (t0 + k2 + 1)],
                               r=rhs_q: E.matmul(o, l, r, start=True, stop=True)),
                              inc="mm")
                    qkmm = em.count("mm")
                    em.wait("scalar", "mm", qkmm)
                    C.wait_free("scalar", ("ring", rs))
                    C.wait_free("scalar", ("ringv", rs))
                    em.op("scalar",
                          (lambda E, o=stb[:, 2048 + 1024 * rs:2048 + 1024 * (rs + 1)],
                           a=C.PS[:, 1024 * (p % 2):1024 * (p % 2) + 1024],
                           bb=C.ST[:, 3074:3075]: E.activation(o, a, AF.Exp, bias=bb)),
                          inc="ac", noser=True)
                    eac = em.count("ac")
                    C.set_free(("bank", pa), "ac", eac)
                    C.set_free(("bank", pa + 1), "ac", eac)
                    em.wait("vector", "ac", eac)
                    pr0 = stb[:, 2048 + 1024 * rs:2048 + 1024 * rs + 512]
                    pr1 = stb[:, 2048 + 1024 * rs + 512:2048 + 1024 * (rs + 1)]
                    if p == 0:
                        C.wait_free("vector", ("accl0",))
                        em.op("vector", lambda E, o=acc0, a=pr0: E.tensor_copy(o, a),
                              inc="dv")
                    else:
                        em.op("vector", lambda E, o=acc0, a=pr0: E.tensor_add(o, o, a),
                              inc="dv")
                    em.op("vector", lambda E, o=acc0, a=pr1: E.tensor_add(o, o, a),
                          inc="dv")
                    C.set_free(("ringv", rs), "dv", em.count("dv"))
                    pend.append(((rs, eac), t0, p == 0))
                    if len(pend) > 2:
                        flush_l0(pend.pop(0))
                while pend:
                    flush_l0(pend.pop(0))
                C.wait_free("tensor", ("sbank",))
                em.wait("tensor", "dv", em.count("dv"))
                em.op("tensor",
                      (lambda E, o=C.PS[0:1, 3072:3584], l=C.ones_col_f[:, :], r=acc0:
                       E.matmul(o, l, r, start=True, stop=True)), inc="mm")
                C.set_free(("accl0",), "mm", em.count("mm"))
                pvmm = em.count("mm")
                em.wait("scalar", "mm", pvmm)
                em.op("scalar", (lambda E, o=C.RW[0:1, 512:1024], a=C.PS[0:1, 3072:3584]:
                                 E.activation(o, a, AF.Ln)), inc="ac")
                em.wait("vector", "ac")
                em.op("vector", (lambda E, o=C.RW[0:1, 512:1024], a=C.RW[0:1, 512:1024]:
                                 E.tensor_scalar_mul(o, a, -1.0)), inc="dv")
                em.wait("scalar", "dv")
                em.op("scalar", (lambda E, o=C.RW[0:1, 1024:1536], a=C.RW[0:1, 512:1024]:
                                 E.activation(o, a, AF.Exp)), inc="ac")
                em.wait("tensor", "ac")
                em.op("tensor",
                      (lambda E, o=C.PS[:, 2560:3072], l=C.ones_row_f[0:1, 0:128],
                       r=C.RW[0:1, 1024:1536]: E.matmul(o, l, r, start=True, stop=True)),
                      inc="mm")
                em.wait("vector", "mm")
                em.op("vector", (lambda E, o=C.ST[:, 3584:4096], a=C.PS[:, 2560:3072]:
                                 E.tensor_copy(o, a)), inc="dv")
                em.op("vector",
                      (lambda E, o=C.A1[:, hq * 1024 + 512 * b:hq * 1024 + 512 * (b + 1)],
                       a=C.PS[:, 3584:4096], bx=C.ST[:, 3584:4096]:
                       E.tensor_mul(o, a, bx)), inc="dv")
                C.set_free(("sbank",), "ac", em.count("ac"))
                C.set_free(("cbank",), "dv", em.count("dv"))
            C.set_free(("qts", sq), "mm", em.count("mm"))
        kvdone = em.count("mm")
        C.set_free(("kvs", s), "mm", kvdone)
        if g + 2 < 8:
            kvrdy[g + 2] = load_kv(g + 2)
    return em.count("dv")


def build_l0attn_stage():
    nc = bass.Bass("TRN2", target_bir_lowering=False, debug=False, num_devices=8)
    io = {}
    io["x_in"] = _decl(nc, "x_in", [128, KC * T], F32, "ExternalInput")
    io["l0_qkvT"] = _decl(nc, "l0_qkvT", [48 * 128, KC * 128], F32, "ExternalInput")
    io["l0_bv"] = _decl(nc, "l0_bv", [1, 1024], F32, "ExternalInput")
    io["l0_woT"] = _decl(nc, "l0_woT", [KC * 128, KC * 128], F32, "ExternalInput")
    io["ident"] = _decl(nc, "ident", [128, 128], F32, "ExternalInput")
    io["qspill"] = _decl(nc, "qspill", [D, T], BF16, "Internal")
    io["kgat_in"] = _decl(nc, "kgat_in", [1024, T], BF16, "Internal")
    io["kgat"] = _decl(nc, "kgat", [S, T], BF16, "Internal")
    io["vgat_in"] = _decl(nc, "vgat_in", [T, 1024], BF16, "Internal")
    io["vgat"] = _decl(nc, "vgat", [S, 1024], BF16, "Internal")
    io["r1"] = _decl(nc, "r1", [128, KC * T], F32, "Internal")
    io["x1f"] = _decl(nc, "x1f", [128, KC * T], F32, "ExternalOutput")
    with ExitStack() as ctx:
        em = Em(nc)
        C = Ctx(nc, em, ctx)
        setup_consts(C, io)
        em.dma("gpsimd", C.A0[:, 0:KC * 1024], io["x_in"][:, :], "c0g")
        em.wait_fin("tensor", "c0g")
        for t in range(4):
            C.free[("ab", t)] = None
        ctx_all = emit_l0attn_with_wo(C, io)
        for i in range(4):
            em.wait("sync", "r%d" % i)
        em.finalize(ctx)
    return nc


def emit_l0attn_with_wo(C, io):
    em = C.em
    ctx_all = emit_l0attn(C, io, None)
    ep = Ep(C, "e1", io["x_in"], io["r1"], io["x1f"], C.A0)
    for i in range(4):
        C.set_free(("xsl", i), "dv", ctx_all)
        C.set_free(("rsl", i), "dv", ctx_all)
    em.wait("tensor", "dv", ctx_all)
    for pb in range(4):
        C.set_free(("bank", pb), C.free.get(("ab", pb), ("dv", 0))[0]
                   if C.free.get(("ab", pb)) else "dv",
                   C.free.get(("ab", pb))[1] if C.free.get(("ab", pb)) else 0)

    def ctx_rhs(kc, tb):
        return C.A1[:, kc * 1024 + tb * 512:kc * 1024 + (tb + 1) * 512]

    emit_proj(C, "wo", io["l0_woT"], KC, 128, ctx_rhs, ep.chunk_cb)
    return ep.finish()


def emit_l1attn(C, io, ep):
    em = C.em
    stb = C.stb()
    RG = [list(range(NCORE))]

    gxc = em.dma("sync", io["xgat_in"].rearrange("(kc p) t -> p kc t", p=128),
                 C.A1[:, :].rearrange("p (kc t) -> p kc t", t=1024), "gx")
    em.wait("gpsimd", "gx", gxc)
    em.op("gpsimd",
          (lambda E: E.collective_compute(
              "AllGather", ALU.bypass, replica_groups=RG,
              ins=[io["xgat_in"].opt()], outs=[io["xgat"].opt()])), inc="cc")
    cc_x = em.count("cc")

    em.dma("sync", C.SM[:, 0:4], io["l1_slopes"][:, :], "c3")
    em.dma("sync", C.SM[:, 4:8], io["l1_qw"][:, :], "c3")
    em.dma("sync", C.SM[:, 8:12], io["l1_qb"][:, :], "c3")
    em.dma("sync", C.SM[:, 12:16], io["l1_kw"][:, :], "c3")
    em.dma("sync", C.SM[:, 16:20], io["l1_kb"][:, :], "c3")
    em.dma("sync", C.SM[:, 20:24], io["jcv"][:, :], "c3")
    em.dma("sync", C.RW[0:1, 2944:3012], io["posr"][:, :], "c3")
    em.wait("vector", "c3")
    em.op("vector", lambda E, o=C.SM[:, 28:32], a=C.SM[:, 4:8]:
          E.tensor_scalar_mul(o, a, ATT_SCALE), inc="dv")
    em.op("vector", lambda E, o=C.SM[:, 32:36], a=C.SM[:, 8:12]:
          E.tensor_scalar_mul(o, a, ATT_SCALE), inc="dv")

    em.wait("sync", "cc", cc_x)
    xg = io["xgat"].rearrange("(r kc p) t -> p r kc t", p=128, kc=KC)
    wq = io["l1_qkvT"].rearrange("(jb p) f -> p jb f", p=128)
    qrs = io["qraw"].rearrange("(h p) t -> p h t", p=128)
    krs = io["kraw"].rearrange("(h p) t -> p h t", p=128)
    vts = io["vtok1"].rearrange("(tc p) d -> p tc d", p=128)
    sqi = io["sqk_in"]

    em.wait("gpsimd", "cc", cc_x)
    groups = (list(range(8)), list(range(8, 12)))
    wg_rdy = {}
    pbctr = [0]
    for gi, grp in enumerate(groups):
        if gi == 1:
            em.wait("gpsimd", "mm")
        for ji, jb in enumerate(grp):
            wg_rdy[jb] = em.dma(
                "gpsimd",
                C.A1[:, ji * 4096:(ji + 1) * 4096],
                wq[:, jb, :], "wg")
        for tblk in range(16):
            s = tblk % 2
            C.wait_free("sync", ("xb", s))
            xrd = em.dma("sync",
                         C.A0[:, 16384 * s:16384 * (s + 1)].rearrange(
                             "p (kc t) -> p kc t", t=512),
                         xg[:, tblk // 2, :, 512 * (tblk % 2):512 * (tblk % 2 + 1)],
                         "xb%d" % s)
            em.wait("tensor", "xb%d" % s, xrd)
            em.wait("tensor", "wg", wg_rdy[grp[-1]])
            for ji, jb in enumerate(grp):
                pb = pbctr[0] % 4
                pbctr[0] += 1
                C.wait_free("tensor", ("bank", pb))
                ps = C.bank(pb)
                for kc in range(KC):
                    em.op("tensor",
                          (lambda E, o=ps, l=C.A1[:, ji * 4096 + kc * 128:ji * 4096 + (kc + 1) * 128],
                           r=C.A0[:, 16384 * s + kc * 512:16384 * s + (kc + 1) * 512],
                           st=(kc == 0), sp=(kc == KC - 1):
                           E.matmul(o, l, r, start=st, stop=sp)),
                          inc=("mm" if kc == KC - 1 else None))
                mmc = em.count("mm")
                ei = pb % 2
                em.wait("vector", "mm", mmc)
                C.wait_free("vector", ("cls", ei))
                cslot = C.SMB[:, 512 * ei:512 * (ei + 1)]
                em.op("vector", lambda E, o=cslot, a=ps:
                      E.tensor_scalar(o, a, 8.0, -8.0, ALU.min, ALU.max), inc="dv")
                C.set_free(("bank", pb), "dv", em.count("dv"))
                if jb < 8:
                    sq_ = C.SMB[:, 1024 + 512 * ei:1024 + 512 * (ei + 1)]
                    C.wait_free("vector", ("sqs", ei))
                    em.op("vector", lambda E, o=sq_, a=cslot: E.tensor_mul(o, a, a),
                          inc="dv")
                    dvc = em.count("dv")
                    em.wait("sync", "dv", dvc)
                    dst = (qrs[:, jb, 512 * tblk:512 * (tblk + 1)] if jb < 4
                           else krs[:, jb - 4, 512 * tblk:512 * (tblk + 1)])
                    sc_ = em.dma("sync", dst, cslot, "p%d" % ei)
                    C.set_free(("cls", ei), "p%d" % ei, sc_)
                    sb0 = 4 if jb < 4 else 6
                    st_ = jb % 4 == 0
                    sp_ = jb % 4 == 3
                    em.wait("tensor", "dv", dvc)
                    C.wait_free("tensor", ("bank", sb0))
                    C.wait_free("tensor", ("bank", sb0 + 1))
                    em.op("tensor",
                          (lambda E, o=C.PS[0:1, 512 * sb0:512 * sb0 + 512],
                           l=C.ones_col_b[:, :], r=cslot, s0=st_, s1=sp_:
                           E.matmul(o, l, r, start=s0, stop=s1)))
                    em.op("tensor",
                          (lambda E, o=C.PS[0:1, 512 * (sb0 + 1):512 * (sb0 + 1) + 512],
                           l=C.ones_col_b[:, :], r=sq_, s0=st_, s1=sp_:
                           E.matmul(o, l, r, start=s0, stop=s1)),
                          inc="mm")
                    C.set_free(("sqs", ei), "mm", em.count("mm"))
                    if sp_:
                        em.wait("vector", "mm")
                        ri = tblk % 2
                        C.wait_free("vector", ("srw", ri))
                        for k2 in range(2):
                            em.op("vector",
                                  (lambda E, o=C.RW[0:1, 1024 * ri + 512 * k2:1024 * ri + 512 * (k2 + 1)],
                                   a=C.PS[0:1, 512 * (sb0 + k2):512 * (sb0 + k2) + 512]:
                                   E.tensor_copy(o, a)), inc="dv")
                        C.set_free(("bank", sb0), "dv", em.count("dv"))
                        C.set_free(("bank", sb0 + 1), "dv", em.count("dv"))
                        em.wait("sync", "dv")
                        row0 = 0 if jb < 4 else 2
                        for k2 in range(2):
                            sc2 = em.dma("sync",
                                         sqi[row0 + k2:row0 + k2 + 1,
                                             512 * tblk:512 * (tblk + 1)],
                                         C.RW[0:1, 1024 * ri + 512 * k2:1024 * ri + 512 * (k2 + 1)],
                                         "sa%d" % ri)
                        C.set_free(("srw", ri), "sa%d" % ri, sc2)
                else:
                    dvch = jb - 8
                    dvc = em.count("dv")
                    for tc in range(4):
                        pt = 4 + (tc % 2)
                        C.wait_free("tensor", ("bank", pt))
                        em.wait("tensor", "dv", dvc)
                        em.op("tensor",
                              (lambda E, o=C.bank_bf(pt)[:, 0:128],
                               a=cslot[:, 128 * tc:128 * (tc + 1)], ii=C.ident_b[:, :]:
                               E.transpose(o, a, ii)), inc="mm")
                        em.wait("vector", "mm")
                        C.wait_free("vector", ("vsl", pt))
                        vslot = C.SMB[:, 1024 + 128 * (pt - 4):1024 + 128 * (pt - 3)]
                        em.op("vector", (lambda E, o=vslot, a=C.bank_bf(pt)[:, 0:128]:
                                         E.tensor_copy(o, a)), inc="dv")
                        C.set_free(("bank", pt), "dv", em.count("dv"))
                        em.wait("sync", "dv")
                        vc = em.dma("sync",
                                    vts[:, 4 * tblk + tc, 128 * dvch:128 * (dvch + 1)],
                                    vslot, "v%d" % (pt - 4))
                        C.set_free(("vsl", pt), "v%d" % (pt - 4), vc)
                    C.set_free(("cls", ei), "mm", em.count("mm"))
    qkv_mm = em.count("mm")

    for s_ in ("sa0", "sa1"):
        em.wait("gpsimd", s_)
    em.op("gpsimd",
          (lambda E: E.collective_compute(
              "AllReduce", ALU.add, replica_groups=RG,
              ins=[io["sqk_in"].opt()], outs=[io["sqk"].opt()])), inc="cc")
    cc_s = em.count("cc")

    em.wait("sync", "cc", cc_s)
    for s_ in ("p0", "p1", "v0", "v1", "sa0", "sa1"):
        em.wait("vector", s_)
        em.wait("sync", s_)
    qns = io["qn"].rearrange("(h p) t -> p h t", p=128)
    kns = io["kn"].rearrange("(h p) t -> p h t", p=128)
    for tblk in range(16):
        C.wait_free("sync", ("strow",))
        src = em.dma("sync",
                     C.RW[0:1, 0:2048].rearrange("p (k t) -> p k t", t=512),
                     io["sqk"][0:4, 512 * tblk:512 * (tblk + 1)], "sr")
        em.wait("vector", "sr", src)
        for qk in range(2):
            srow = C.RW[0:1, 1024 * qk:1024 * qk + 512]
            sqrow = C.RW[0:1, 1024 * qk + 512:1024 * qk + 1024]
            mu = C.RW[0:1, 2048:2560]
            al = C.RW[0:1, 2560:2944]
            em.op("vector", lambda E, o=mu, a=srow: E.tensor_scalar_mul(o, a, 1.0 / D),
                  inc="dv")
            em.op("vector", lambda E, o=sqrow, a=sqrow: E.tensor_scalar_mul(o, a, 1.0 / D),
                  inc="dv")
            em.op("vector", lambda E, o=srow, a=mu, b=mu: E.tensor_mul(o, a, b), inc="dv")
            em.op("vector", lambda E, o=sqrow, a=sqrow, b=srow: E.tensor_sub(o, a, b),
                  inc="dv")
            em.op("vector", lambda E, o=sqrow, a=sqrow: E.tensor_scalar_add(o, a, EPS),
                  inc="dv")
            em.wait("scalar", "dv")
            em.op("scalar", lambda E, o=sqrow, a=sqrow: E.activation(o, a, AF.Ln),
                  inc="ac")
            em.wait("vector", "ac")
            em.op("vector", lambda E, o=sqrow, a=sqrow: E.tensor_scalar_mul(o, a, -0.5),
                  inc="dv")
            em.wait("scalar", "dv")
            em.op("scalar", lambda E, o=sqrow, a=sqrow: E.activation(o, a, AF.Exp),
                  inc="ac")
            em.wait("vector", "ac")
            em.op("vector", lambda E, o=mu, a=mu, b=sqrow: E.tensor_mul(o, a, b), inc="dv")
            em.op("vector", lambda E, o=mu, a=mu: E.tensor_scalar_mul(o, a, -1.0), inc="dv")
            rowsdv = em.count("dv")
            em.wait("tensor", "dv", rowsdv)
            for k2, row in ((0, sqrow), (1, mu)):
                pb = 4 + 2 * qk + k2
                C.wait_free("tensor", ("bank", pb))
                em.op("tensor",
                      (lambda E, o=C.bank(pb), l=C.ones_row_f[0:1, 0:128], r=row:
                       E.matmul(o, l, r, start=True, stop=True)), inc="mm")
            em.wait("vector", "mm")
            for k2 in range(2):
                pb = 4 + 2 * qk + k2
                em.op("vector",
                      (lambda E, o=C.ST[:, 1024 + 1024 * qk + 512 * k2:1024 + 1024 * qk + 512 * (k2 + 1)],
                       a=C.bank(pb): E.tensor_copy(o, a)), inc="dv")
                C.set_free(("bank", pb), "dv", em.count("dv"))
        C.set_free(("strow",), "dv", em.count("dv"))
        repdv = em.count("dv")
        for qk in range(2):
            raws = qrs if qk == 0 else krs
            ns = qns if qk == 0 else kns
            wcol = C.SM[:, 28:32] if qk == 0 else C.SM[:, 12:16]
            bcol = C.SM[:, 32:36] if qk == 0 else C.SM[:, 16:20]
            arep = C.ST[:, 1024 + 1024 * qk:1024 + 1024 * qk + 512]
            brep = C.ST[:, 1024 + 1024 * qk + 512:1024 + 1024 * qk + 1024]
            for ch in range(4):
                mi = ch % 2
                C.wait_free("sync", ("mrs", mi))
                mrd = em.dma("sync", C.SMB[:, 512 * mi:512 * (mi + 1)],
                             raws[:, ch, 512 * tblk:512 * (tblk + 1)], "m%d" % mi)
                em.wait("vector", "m%d" % mi, mrd)
                em.wait("vector", "dv", repdv)
                tmp = C.ST[:, 512 * mi:512 * (mi + 1)]
                C.wait_free("vector", ("tmp", mi))
                em.op("vector", (lambda E, o=tmp, a=C.SMB[:, 512 * mi:512 * (mi + 1)],
                                 b=arep: E.tensor_mul(o, a, b)), inc="dv")
                C.set_free(("mrs", mi), "dv", em.count("dv"))
                em.op("vector", (lambda E, o=tmp, a=tmp, b=brep: E.tensor_add(o, a, b)),
                      inc="dv")
                ni = 2 + mi
                C.wait_free("vector", ("nsl", ni))
                em.op("vector",
                      (lambda E, o=C.SMB[:, 512 * ni:512 * (ni + 1)], a=tmp,
                       w=wcol[:, ch:ch + 1], bb=bcol[:, ch:ch + 1]:
                       E.tensor_scalar(o, a, w, bb, ALU.mult, ALU.add)), inc="dv")
                C.set_free(("tmp", mi), "dv", em.count("dv"))
                em.wait("sync", "dv")
                nc_ = em.dma("sync", ns[:, ch, 512 * tblk:512 * (tblk + 1)],
                             C.SMB[:, 512 * ni:512 * (ni + 1)], "n%d" % mi)
                C.set_free(("nsl", ni), "n%d" % mi, nc_)
    norm_dv = em.count("dv")

    kts = io["kn"].rearrange("(h p) t -> p h t", p=128)
    for s_ in ("n0", "n1"):
        em.wait("sync", s_)
    for h in range(HPC):
        C.wait_free("sync", ("hbuf",))
        em.dma("sync", C.A0[:, 0:8192], kts[:, h, :], "k0")
        em.dma("sync",
               C.A0[:, 8192:16384].rearrange("p (ch d) -> p ch d", d=128),
               vts[:, :, 128 * h:128 * (h + 1)], "k0")
        krd = em.dma("sync", C.A0[:, 16384:24576], qns[:, h, :], "k0")
        C.wait_free("sync", ("strips",))
        srd = em.dma("sync",
                     C.ST[:, 0:2560].rearrange("p (v j) -> p v j", j=512),
                     io["l1_strips"].rearrange("(h v p) j -> p h v j", p=128, v=5)[:, h, :, :],
                     "k0")
        em.wait("vector", "k0", srd)
        em.op("vector", (lambda E, o=C.SM[:, 24:28], a=C.SM[:, 20:24],
                         sl=C.SM[:, h:h + 1]:
                         E.tensor_scalar(o, a, sl, None, ALU.mult)), inc="dv")
        em.wait("tensor", "k0", srd)
        kT = C.A0[:, 0:8192]
        vT = C.A0[:, 8192:16384]
        if h == 0:
            for r_ in range(4):
                C.set_free(("ring", r_), "mm", qkv_mm)
        rhs_q0 = C.A0[:, 16384:16896]
        for jc in range(4):
            C.wait_free("tensor", ("bank", jc))
            em.op("tensor",
                  (lambda E, o=C.bank(jc), l=kT[:, 128 * jc:128 * (jc + 1)], r=rhs_q0:
                   E.matmul(o, l, r, start=True, stop=True)), inc="mm")
        dmm = em.count("mm")
        em.wait("vector", "mm", dmm)
        for jc in range(4):
            em.op("vector",
                  (lambda E, o=C.ST[:, 2560 + 512 * jc:2560 + 512 * (jc + 1)],
                   a=C.bank(jc), s=C.SM[:, 24 + jc:25 + jc],
                   strip=C.ST[:, 512 * (1 + jc):512 * (2 + jc)]:
                   E.scalar_tensor_tensor(o, a, s, strip, ALU.add, ALU.add)),
                  inc="dv")
            C.set_free(("bank", jc), "dv", em.count("dv"))
        em.op("vector", (lambda E, o=C.ST[:, 5188:5189], a=C.ST[:, 2560:4608]:
                         E.reduce_max(o, a, axis=AX.X)), inc="dv")
        em.wait("tensor", "dv")
        em.op("tensor",
              (lambda E, o=C.PS[0:1, 2048:2176], a=C.ST[:, 5188:5189],
               ii=C.ident_f[:, :]: E.transpose(o, a, ii)), inc="mm")
        em.wait("vector", "mm")
        em.op("vector", (lambda E, o=C.RW[0:1, 0:1], a=C.PS[0:1, 2048:2176]:
                         E.reduce_max(o, a, axis=AX.X)), inc="dv")
        for b in range(16):
            nt = 4 * b + 4
            rhs_q = C.A0[:, 16384 + 512 * b:16384 + 512 * (b + 1)]
            em.op("vector", (lambda E, o=C.RW[0:1, 2:3], a=C.SM[0:1, h:h + 1], b_=b:
                             E.tensor_scalar_mul(o, a, 512.0 * b_)), inc="dv")
            em.op("vector", (lambda E, o=C.RW[0:1, 2:3], a=C.RW[0:1, 2:3],
                             b=C.RW[0:1, 0:1]: E.tensor_add(o, a, b)), inc="dv")
            em.op("vector",
                  (lambda E, o=C.RW[0:1, 4:4 + nt], a=C.RW[0:1, 2944:2944 + nt],
                   s1=C.SM[0:1, h:h + 1], s2=C.RW[0:1, 2:3], n_=nt:
                   E.tensor_scalar(o, a, s1, s2, ALU.mult, ALU.subtract)), inc="dv")
            bdv = em.count("dv")
            em.wait("tensor", "dv", bdv)
            em.op("tensor",
                  (lambda E, o=C.PS[:, 2048:2048 + nt], l=C.ones_row_f[0:1, 0:128],
                   r=C.RW[0:1, 4:4 + nt], n_=nt: E.matmul(o, l, r, start=True, stop=True)),
                  inc="mm")
            em.wait("vector", "mm")
            em.op("vector", (lambda E, o=C.ST[:, 4608:4608 + nt], a=C.PS[:, 2048:2048 + nt]:
                             E.tensor_copy(o, a)), inc="dv")
            pend = []

            def flush_l1(rec, vT=vT, b=b):
                rs_, eac_, t0_ = rec
                em.wait("tensor", "ac", eac_)
                if t0_ == 0:
                    C.wait_free("tensor", ("sbank",))
                    C.wait_free("tensor", ("cbank",))
                for k2 in range(2):
                    t = t0_ + k2
                    pr = C.A1[:, 1024 * rs_ + 512 * k2:1024 * rs_ + 512 * (k2 + 1)]
                    st_ = t == 0
                    sp_ = t == 4 * b + 3
                    em.op("tensor",
                          (lambda E, o=C.PS[0:1, 3072:3584], l=C.ones_col_b[:, :],
                           r=pr, s0=st_, s1=sp_: E.matmul(o, l, r, start=s0, stop=s1)))
                    em.op("tensor",
                          (lambda E, o=C.PS[:, 3584:4096],
                           l=vT[:, 128 * t:128 * (t + 1)], r=pr, s0=st_, s1=sp_:
                           E.matmul(o, l, r, start=s0, stop=s1)),
                          inc=("mm" if k2 == 1 else None))
                C.set_free(("ring", rs_), "mm", em.count("mm"))

            for p_ in range(nt // 2):
                t0 = 2 * p_
                rs = p_ % 4
                for k2, t in enumerate((t0, t0 + 1)):
                    C.wait_free("tensor", ("bank", k2))
                    em.op("tensor",
                          (lambda E, o=C.bank(k2), l=kT[:, 128 * t:128 * (t + 1)],
                           r=rhs_q: E.matmul(o, l, r, start=True, stop=True)), inc="mm")
                qkmm = em.count("mm")
                em.wait("vector", "mm", qkmm)
                for k2, t in enumerate((t0, t0 + 1)):
                    v_ = 0 if t < 4 * b else 1 + (t - 4 * b)
                    C.wait_free("vector", ("bank", 2 + k2))
                    em.op("vector",
                          (lambda E, o=C.bank(2 + k2), a=C.bank(k2),
                           s=C.ST[:, 4608 + t:4609 + t],
                           strip=C.ST[:, 512 * v_:512 * (v_ + 1)]:
                           E.scalar_tensor_tensor(o, a, s, strip, ALU.add, ALU.add)),
                          inc="dv")
                    C.set_free(("bank", k2), "dv", em.count("dv"))
                sttdv = em.count("dv")
                em.wait("scalar", "dv", sttdv)
                C.wait_free("scalar", ("ring", rs))
                em.op("scalar",
                      (lambda E, o=C.A1[:, 1024 * rs:1024 * (rs + 1)],
                       a=C.PS[:, 1024:2048]:
                       E.activation(o, a, AF.Exp)), inc="ac", noser=True)
                eac = em.count("ac")
                C.set_free(("bank", 2), "ac", eac)
                C.set_free(("bank", 3), "ac", eac)
                pend.append((rs, eac, t0))
                if len(pend) > 2:
                    flush_l1(pend.pop(0))
            while pend:
                flush_l1(pend.pop(0))
            pvmm = em.count("mm")
            em.wait("scalar", "mm", pvmm)
            em.op("scalar", (lambda E, o=C.RW[0:1, 512:1024], a=C.PS[0:1, 3072:3584]:
                             E.activation(o, a, AF.Ln)), inc="ac")
            em.wait("vector", "ac")
            em.op("vector", (lambda E, o=C.RW[0:1, 512:1024], a=C.RW[0:1, 512:1024]:
                             E.tensor_scalar_mul(o, a, -1.0)), inc="dv")
            em.wait("scalar", "dv")
            em.op("scalar", (lambda E, o=C.RW[0:1, 1024:1536], a=C.RW[0:1, 512:1024]:
                             E.activation(o, a, AF.Exp)), inc="ac")
            em.wait("tensor", "ac")
            em.op("tensor",
                  (lambda E, o=C.PS[:, 2560:3072], l=C.ones_row_f[0:1, 0:128],
                   r=C.RW[0:1, 1024:1536]: E.matmul(o, l, r, start=True, stop=True)),
                  inc="mm")
            em.wait("vector", "mm")
            em.op("vector", (lambda E, o=C.ST[:, 4676:5188], a=C.PS[:, 2560:3072]:
                             E.tensor_copy(o, a)), inc="dv")
            C.wait_free("vector", ("ctxh",))
            em.op("vector",
                  (lambda E, o=C.A0[:, 24576 + 512 * b:24576 + 512 * (b + 1)],
                   a=C.PS[:, 3584:4096], bx=C.ST[:, 4676:5188]:
                   E.tensor_mul(o, a, bx)), inc="dv")
            C.set_free(("sbank",), "ac", em.count("ac"))
            C.set_free(("cbank",), "dv", em.count("dv"))
        em.wait("sync", "dv")
        cgc = em.dma("sync",
                     io["ctxg_in"].rearrange("(h p) t -> p h t", p=128)[:, h, :],
                     C.A0[:, 24576:32768], "cg")
        C.set_free(("ctxh",), "cg", cgc)
        C.set_free(("hbuf",), "mm", em.count("mm"))
        C.set_free(("strips",), "dv", em.count("dv"))
    attn_mm = em.count("mm")
    attn_ac = em.count("ac")
    attn_dv = em.count("dv")

    em.wait("gpsimd", "cg")
    em.op("gpsimd",
          (lambda E: E.collective_compute(
              "AllGather", ALU.bypass, replica_groups=RG,
              ins=[io["ctxg_in"].opt()], outs=[io["ctxg"].opt()])), inc="cc")
    cc_c = em.count("cc")

    em.wait("gpsimd", "cc", cc_c)
    em.wait("gpsimd", "mm", attn_mm)
    cgr = io["ctxg"].rearrange("(kc p) t -> p kc t", p=128)
    clc = em.op(
        "gpsimd",
        (lambda E: E.dma_start(
            out=C.A1[:, :].rearrange("p (kc t) -> p kc t", t=1024),
            in_=cgr[:, :, bass.ts(E.partition_id(), 1024)])),
        inc="cl", amt=16)

    for i in range(4):
        C.set_free(("xsl", i), "dv", attn_dv)
        C.set_free(("rsl", i), "dv", attn_dv)
    em.wait("tensor", "cl", clc)

    def ctx_rhs(kc, tb):
        return C.A1[:, kc * 1024 + tb * 512:kc * 1024 + (tb + 1) * 512]

    emit_proj(C, "wo1", io["l1_woT"], KC, 128, ctx_rhs, ep.chunk_cb)
    return ep.finish()


def build_l1attn_stage():
    nc = bass.Bass("TRN2", target_bir_lowering=False, debug=False, num_devices=8)
    io = {}
    io["x2f"] = _decl(nc, "x2f", [128, KC * T], F32, "ExternalInput")
    io["l1_qkvT"] = _decl(nc, "l1_qkvT", [12 * 128, KC * 128], F32, "ExternalInput")
    for nm in ("l1_qw", "l1_qb", "l1_kw", "l1_kb"):
        io[nm] = _decl(nc, nm, [128, 4], F32, "ExternalInput")
    io["l1_slopes"] = _decl(nc, "l1_slopes", [128, 4], F32, "ExternalInput")
    io["l1_strips"] = _decl(nc, "l1_strips", [4 * 5 * 128, 512], F32, "ExternalInput")
    io["l1_woT"] = _decl(nc, "l1_woT", [KC * 128, KC * 128], F32, "ExternalInput")
    io["ident"] = _decl(nc, "ident", [128, 128], F32, "ExternalInput")
    io["jcv"] = _decl(nc, "jcv", [128, 4], F32, "ExternalInput")
    io["posr"] = _decl(nc, "posr", [1, 68], F32, "ExternalInput")
    io["xgat_in"] = _decl(nc, "xgat_in", [D, T], BF16, "Internal")
    io["xgat"] = _decl(nc, "xgat", [NCORE * D, T], BF16, "Internal")
    io["qraw"] = _decl(nc, "qraw", [512, S], BF16, "Internal")
    io["kraw"] = _decl(nc, "kraw", [512, S], BF16, "Internal")
    io["vtok1"] = _decl(nc, "vtok1", [S, 512], BF16, "Internal")
    io["qn"] = _decl(nc, "qn", [512, S], BF16, "Internal")
    io["kn"] = _decl(nc, "kn", [512, S], BF16, "Internal")
    io["sqk_in"] = _decl(nc, "sqk_in", [4, S], F32, "Internal")
    io["sqk"] = _decl(nc, "sqk", [4, S], F32, "Internal")
    io["ctxg_in"] = _decl(nc, "ctxg_in", [512, S], BF16, "Internal")
    io["ctxg"] = _decl(nc, "ctxg", [D, S], BF16, "Internal")
    io["r3"] = _decl(nc, "r3", [128, KC * T], F32, "Internal")
    io["x3f"] = _decl(nc, "x3f", [128, KC * T], F32, "ExternalOutput")
    with ExitStack() as ctx:
        em = Em(nc)
        C = Ctx(nc, em, ctx)
        setup_consts(C, io)
        em.dma("gpsimd", C.A1[:, 0:KC * 1024], io["x2f"][:, :], "c0g")
        em.wait_fin("sync", "c0g")
        ep = Ep(C, "e3", io["x2f"], io["r3"], io["x3f"], C.A0)
        emit_l1attn(C, io, ep)
        for i in range(4):
            em.wait("sync", "r%d" % i)
        em.finalize(ctx)
    return nc


def emit_l1ffn(C, io, bprev, ep):
    em = C.em

    def xr(kc, tb):
        return bprev[:, kc * 1024 + tb * 512:kc * 1024 + (tb + 1) * 512]

    la_rdy = em.dma("gpsimd", C.A1[:, 0:4096], io["l1_laT"][:, :], "lw1")
    lb_rdy = em.dma("gpsimd", C.A1[:, 4096:8192], io["l1_lbT"][:, :], "lw1")
    em.dma("sync", C.SM[0:8, 36:37], io["l1_lw"][:, :], "c4")
    em.dma("sync", C.SM[0:8, 80:208], io["mexp"][:, :], "c4")
    em.dma("sync", C.SM[:, 48:80], io["l1_bias"][:, :], "c4")
    em.wait("tensor", "c4")
    em.wait("tensor", "lw1", lb_rdy)

    C.wait_free("tensor", ("bank", 6))
    em.op("tensor",
          (lambda E, o=C.PS[:, 512 * 6:512 * 6 + 1], l=C.SM[0:8, 80:208],
           r=C.SM[0:8, 36:37]: E.matmul(o, l, r, start=True, stop=True)), inc="mm")
    em.wait("vector", "mm")
    em.op("vector", (lambda E, o=C.SM[:, 40:41], a=C.PS[:, 512 * 6:512 * 6 + 1]:
                     E.tensor_scalar_mul(o, a, 2.0)), inc="dv")
    C.set_free(("bank", 6), "dv", em.count("dv"))

    for tb in (0, 1):
        pb = 4 + tb
        C.wait_free("tensor", ("bank", pb))
        ps = C.bank(pb)
        for kc in range(KC):
            em.op("tensor",
                  (lambda E, o=ps, l=C.A1[:, kc * 128:(kc + 1) * 128],
                   r=xr(kc, tb), st=(kc == 0), sp=(kc == KC - 1):
                   E.matmul(o, l, r, start=st, stop=sp)),
                  inc=("mm" if kc == KC - 1 else None))
        em.wait("vector", "mm")
        em.op("vector",
              (lambda E, o=C.RW[:, :].bitcast(BF16)[:, 4864 + 512 * tb:4864 + 512 * (tb + 1)],
               a=ps, s=C.SM[:, 40:41]: E.tensor_scalar(o, a, s, None, ALU.mult)),
              inc="dv")
        C.set_free(("bank", pb), "dv", em.count("dv"))
    low_dv = em.count("dv")

    em.wait("tensor", "dv", low_dv)

    def extra(jb, tb):
        return [(C.A1[:, 4096 + jb * 128:4096 + (jb + 1) * 128],
                 C.RW[:, :].bitcast(BF16)[:, 4864 + 512 * tb:4864 + 512 * (tb + 1)])]

    emit_proj(C, "f1", io["l1_wT"], KC, 128, xr, ep.chunk_cb, extra=extra)
    return ep.finish()


def build_l1ffn_stage():
    nc = bass.Bass("TRN2", target_bir_lowering=False, debug=False, num_devices=8)
    io = {}
    io["x3f"] = _decl(nc, "x3f", [128, KC * T], F32, "ExternalInput")
    io["l1_wT"] = _decl(nc, "l1_wT", [KC * 128, KC * 128], F32, "ExternalInput")
    io["l1_bias"] = _decl(nc, "l1_bias", [128, 32], F32, "ExternalInput")
    io["l1_laT"] = _decl(nc, "l1_laT", [128, KC * 128], F32, "ExternalInput")
    io["l1_lbT"] = _decl(nc, "l1_lbT", [128, D], F32, "ExternalInput")
    io["l1_lw"] = _decl(nc, "l1_lw", [8, 1], F32, "ExternalInput")
    io["mexp"] = _decl(nc, "mexp", [8, 128], F32, "ExternalInput")
    io["ident"] = _decl(nc, "ident", [128, 128], F32, "ExternalInput")
    io["r4"] = _decl(nc, "r4", [128, KC * T], F32, "Internal")
    io["y"] = _decl(nc, "y", [128, KC * T], F32, "ExternalOutput")
    with ExitStack() as ctx:
        em = Em(nc)
        C = Ctx(nc, em, ctx)
        setup_consts(C, io)
        em.dma("gpsimd", C.A0[:, 0:KC * 1024], io["x3f"][:, :], "c0g")
        em.wait_fin("tensor", "c0g")
        ep = Ep(C, "e4", io["x3f"], io["r4"], io["y"], None,
                bias_cols=C.SM[:, 48:80])
        emit_l1ffn(C, io, C.A0, ep)
        for i in range(4):
            em.wait("sync", "r%d" % i)
        em.finalize(ctx)
    return nc


def build_fused():
    nc = bass.Bass("TRN2", target_bir_lowering=False, debug=False, num_devices=8)
    io = {}
    EI = lambda n, s, d=F32: _decl(nc, n, s, d, "ExternalInput")
    IN = lambda n, s, d=F32: _decl(nc, n, s, d, "Internal")
    io["x_in"] = EI("x_in", [128, KC * T])
    io["ident"] = EI("ident", [128, 128])
    io["mexp"] = EI("mexp", [8, 128])
    io["jcv"] = EI("jcv", [128, 4])
    io["posr"] = EI("posr", [1, 68])
    io["l0_qkvT"] = EI("l0_qkvT", [48 * 128, KC * 128])
    io["l0_bv"] = EI("l0_bv", [1, 1024])
    io["l0_woT"] = EI("l0_woT", [KC * 128, KC * 128])
    io["l0_wrT"] = EI("l0_wrT", [128, 256])
    io["l0_br"] = EI("l0_br", [1, 8])
    io["l0_aT"] = EI("l0_aT", [128, KC * 128])
    io["l0_bT"] = EI("l0_bT", [128, D])
    io["l1_qkvT"] = EI("l1_qkvT", [12 * 128, KC * 128])
    for nm in ("l1_qw", "l1_qb", "l1_kw", "l1_kb", "l1_slopes"):
        io[nm] = EI(nm, [128, 4])
    io["l1_strips"] = EI("l1_strips", [4 * 5 * 128, 512])
    io["l1_woT"] = EI("l1_woT", [KC * 128, KC * 128])
    io["l1_wT"] = EI("l1_wT", [KC * 128, KC * 128])
    io["l1_bias"] = EI("l1_bias", [128, 32])
    io["l1_laT"] = EI("l1_laT", [128, KC * 128])
    io["l1_lbT"] = EI("l1_lbT", [128, D])
    io["l1_lw"] = EI("l1_lw", [8, 1])
    io["qspill"] = IN("qspill", [D, T], BF16)
    io["kgat_in"] = IN("kgat_in", [1024, T], BF16)
    io["kgat"] = IN("kgat", [S, T], BF16)
    io["vgat_in"] = IN("vgat_in", [T, 1024], BF16)
    io["vgat"] = IN("vgat", [S, 1024], BF16)
    io["r1"] = IN("r1", [128, KC * T]); io["x1f"] = IN("x1f", [128, KC * T])
    io["r2"] = IN("r2", [128, KC * T]); io["x2f"] = IN("x2f", [128, KC * T])
    io["r3"] = IN("r3", [128, KC * T]); io["x3f"] = IN("x3f", [128, KC * T])
    io["r4"] = IN("r4", [128, KC * T])
    io["xgat_in"] = IN("xgat_in", [D, T], BF16)
    io["xgat"] = IN("xgat", [NCORE * D, T], BF16)
    io["qraw"] = IN("qraw", [512, S], BF16)
    io["kraw"] = IN("kraw", [512, S], BF16)
    io["vtok1"] = IN("vtok1", [S, 512], BF16)
    io["qn"] = IN("qn", [512, S], BF16)
    io["kn"] = IN("kn", [512, S], BF16)
    io["sqk_in"] = IN("sqk_in", [4, S])
    io["sqk"] = IN("sqk", [4, S])
    io["ctxg_in"] = IN("ctxg_in", [512, S], BF16)
    io["ctxg"] = IN("ctxg", [D, S], BF16)
    io["y"] = _decl(nc, "y", [128, KC * T], F32, "ExternalOutput")

    with ExitStack() as ctx:
        em = Em(nc)
        C = Ctx(nc, em, ctx)
        setup_consts(C, io)
        em.dma("gpsimd", C.A0[:, 0:KC * 1024], io["x_in"][:, :], "c0g")
        em.wait_fin("tensor", "c0g")

        def barrier(done):
            for eng in ("tensor", "vector", "scalar", "gpsimd", "sync"):
                em.wait(eng, "dv", done)
            for i in range(4):
                em.wait("sync", "r%d" % i)

        d1 = emit_l0attn_with_wo(C, io)
        barrier(d1[1])
        ep2 = Ep(C, "e2", io["x1f"], io["r2"], io["x2f"], C.A1)
        d2 = emit_l0ffn(C, io, C.A0, ep2)
        barrier(d2[1])
        ep3 = Ep(C, "e3", io["x2f"], io["r3"], io["x3f"], C.A0)
        d3 = emit_l1attn(C, io, ep3)
        barrier(d3[1])
        ep4 = Ep(C, "e4", io["x3f"], io["r4"], io["y"], None,
                 bias_cols=C.SM[:, 48:80])
        emit_l1ffn(C, io, C.A0, ep4)
        for i in range(4):
            em.wait("sync", "r%d" % i)
        em.finalize(ctx)
    return nc


def kernel_fused(**inputs):
    sh, pc = host_prep(inputs)
    cores = list(range(NCORE))
    nc = _get_nc("fused", build_fused)
    maps = []
    for c in cores:
        m = {}
        for nm in ("ident", "mexp", "jcv", "posr", "l0_qkvT", "l0_bv", "l0_woT",
                   "l0_wrT", "l0_br", "l0_aT", "l0_bT", "l1_woT", "l1_wT",
                   "l1_bias", "l1_laT", "l1_lbT", "l1_lw"):
            m[nm] = sh[nm]
        for nm in ("x_in", "l1_qkvT", "l1_qw", "l1_qb", "l1_kw", "l1_kb",
                   "l1_slopes", "l1_strips"):
            m[nm] = pc[c][nm]
        maps.append(m)
    res = run_bass_kernel_spmd(nc, maps, core_ids=cores)
    y = np.concatenate(
        [np.asarray(res.results[c]["y"]).reshape(128, KC, T)
         .transpose(1, 0, 2).reshape(D, T).T for c in cores], axis=0)
    return y[None, :, :].astype(np.float32)


# version 11 (final)
# speedup vs baseline: 1.1993x; 1.1834x over previous
import numpy as np
import ml_dtypes
from contextlib import ExitStack

import concourse.bass as bass
import concourse.mybir as mybir
from concourse.bass_utils import run_bass_kernel_spmd

F32 = mybir.dt.float32
BF16 = mybir.dt.bfloat16
AF = mybir.ActivationFunctionType
ALU = mybir.AluOpType
AX = mybir.AxisListType

NCORE = 8
S = 8192
D = 4096
T = S // NCORE
KC = D // 128
TB = T // 512
EPS = 1e-5
NEG = -1.0e9

L0_NQ, L0_NKV, HD = 32, 8, 128
ATT_SCALE = HD ** -0.5
L1_NH = 32
HPC = L1_NH // NCORE


AUTO_SERIAL = True


class Em:
    ENGS = ("tensor", "vector", "scalar", "gpsimd", "sync")

    def __init__(self, nc):
        self.nc = nc
        self.streams = {e: [] for e in self.ENGS}
        self.counts = {}

    def count(self, name):
        return self.counts.get(name, 0)

    def wait(self, eng, name, val=None):
        self.counts.setdefault(name, 0)
        v = self.counts[name] if val is None else val
        if v > 0:
            self.streams[eng].append(("w", name, v))
        return v

    ENG_SEM = {"vector": "dv", "scalar": "ac"}

    def op(self, eng, fn, inc=None, amt=1, noser=False):
        selfwait = None
        if eng in self.ENG_SEM:
            es = self.ENG_SEM[eng]
            assert inc is None or inc == es, (eng, inc)
            inc = es
            amt = 1
            self.counts.setdefault(es, 0)
            if AUTO_SERIAL and not noser and self.counts[es] > 0:
                selfwait = (es, self.counts[es])
        if inc is not None:
            self.counts.setdefault(inc, 0)
            self.counts[inc] += amt
        self.streams[eng].append(("o", fn, inc, amt, selfwait))
        return self.counts[inc] if inc is not None else None

    def wait_fin(self, eng, name):
        self.counts.setdefault(name, 0)
        self.streams[eng].append(("wf", name))

    def gop(self, fn):
        prev = self.counts.get("gp", 0)
        selfwait = ("gp", prev) if prev > 0 else None
        self.counts["gp"] = prev + 1
        self.streams["gpsimd"].append(("o", fn, "gp", 1, selfwait))
        return self.counts["gp"]

    def dma(self, eng, out, in_, inc, **kw):
        selfwait = None
        self.counts.setdefault(inc, 0)
        self.counts[inc] += 16
        self.streams[eng].append(
            ("o", (lambda E, o=out, i=in_, k=dict(kw): E.dma_start(out=o, in_=i, **k)),
             inc, 16, selfwait))
        return self.counts[inc]

    def finalize(self, ctx):
        handles = {n: ctx.enter_context(self.nc.semaphore(n)) for n in self.counts}
        block = ctx.enter_context(self.nc.Block())
        for eng in self.ENGS:
            items = self.streams[eng]
            if not items:
                continue

            def body(E, items=items):
                for it in items:
                    if it[0] == "w":
                        E.wait_ge(handles[it[1]], it[2])
                    elif it[0] == "wf":
                        if self.counts[it[1]] > 0:
                            E.wait_ge(handles[it[1]], self.counts[it[1]])
                    else:
                        _, fn, inc, amt, selfwait = it
                        if selfwait is not None:
                            E.wait_ge(handles[selfwait[0]], selfwait[1])
                        ins = fn(E)
                        if inc is not None:
                            ins.then_inc(handles[inc], amt)

            getattr(block, eng)(body)


class Ctx:

    def __init__(self, nc, em, ctx):
        self.nc = nc
        self.em = em
        self.free = {}
        E = ctx.enter_context
        self.A0 = E(nc.sbuf_tensor("A0", [128, 32768], BF16))
        self.A1 = E(nc.sbuf_tensor("A1", [128, 32768], BF16))
        self.ST = E(nc.sbuf_tensor("ST", [128, 6144], F32))
        self.WS = E(nc.sbuf_tensor("WS", [128, 8192], BF16))
        self.RW = E(nc.sbuf_tensor("RW", [128, 3072], F32))
        self.ident_f = E(nc.sbuf_tensor("ident_f", [128, 128], F32))
        self.ident_b = E(nc.sbuf_tensor("ident_b", [128, 128], BF16))
        self.ones_col_b = E(nc.sbuf_tensor("ones_col_b", [128, 1], BF16))
        self.ones_col_f = E(nc.sbuf_tensor("ones_col_f", [128, 1], F32))
        self.ones_row_b = E(nc.sbuf_tensor("ones_row_b", [1, 512], BF16))
        self.ones_row_f = E(nc.sbuf_tensor("ones_row_f", [1, 512], F32))
        self.SM = E(nc.sbuf_tensor("SM", [128, 1024], F32))
        self.SMB = E(nc.sbuf_tensor("SMB", [128, 2048], BF16))
        self.PS = E(nc.psum_tensor("PS", [128, 4096], F32))

    def bank(self, i, n=1):
        return self.PS[:, 512 * i:512 * (i + n)]

    def bank_bf(self, i, n=1):
        return self.PS[:, :].bitcast(BF16)[:, 1024 * i:1024 * (i + n)]

    def stb(self):
        return self.ST[:, :].bitcast(BF16)

    def wait_free(self, eng, key):
        fr = self.free.get(key)
        if fr:
            self.em.wait(eng, fr[0], fr[1])

    def set_free(self, key, sem, cnt):
        self.free[key] = (sem, cnt)


def emit_proj(C, tag, wt, njb, jw, x_rhs, out_cb, extra=None, nkc=KC,
              banks=(0, 1, 2, 3), tbs=(0, 1), jb_order=None):
    em = C.em
    nc = C.nc
    wsrc = wt.rearrange("(jb p) f -> p jb f", p=128)
    wready = {}
    order = list(jb_order) if jb_order is not None else list(range(njb))
    slot_of = {jb: idx % 2 for idx, jb in enumerate(order)}

    def issue_w(jb):
        if jb is None:
            return
        s = slot_of[jb]
        C.wait_free("gpsimd", ("WS", s))
        dst = C.WS[:, 4096 * s:4096 * s + nkc * jw]
        wready[jb] = em.dma("gpsimd", dst, wsrc[:, jb, :], "w%d" % s)

    issue_w(order[0])
    issue_w(order[1])
    for idx, jb in enumerate(order):
        s = slot_of[jb]
        em.wait("tensor", "w%d" % s, wready[jb])
        for ti, tb in enumerate(tbs):
            pb = banks[(idx % 2) * 2 + ti]
            C.wait_free("tensor", ("bank", pb))
            ps = C.PS[0:jw, 512 * pb:512 * pb + 512]
            ex = extra(jb, tb) if extra is not None else []
            for kc in range(nkc):
                last = (kc == nkc - 1) and not ex
                em.op("tensor",
                      (lambda E, o=ps, l=C.WS[:, 4096 * s + kc * jw:4096 * s + (kc + 1) * jw],
                       r=x_rhs(kc, tb), st=(kc == 0), sp=last:
                       E.matmul(o, l, r, start=st, stop=sp)),
                      inc=("mm" if last else None))
            for i, (lt, rr) in enumerate(ex):
                last = i == len(ex) - 1
                em.op("tensor",
                      (lambda E, o=ps, l=lt, r=rr, sp=last:
                       E.matmul(o, l, r, start=False, stop=sp)),
                      inc=("mm" if last else None))
            mmcnt = em.count("mm")
            fsem, fcnt = out_cb(jb, tb, ps, mmcnt)
            C.set_free(("bank", pb), fsem, fcnt)
        C.set_free(("WS", s), "mm", em.count("mm"))
        issue_w(order[idx + 2] if idx + 2 < njb else None)


class Ep:

    def __init__(self, C, tag, xprev, r_out, xf_out, bbuf, bias_cols=None):
        self.C = C
        self.tag = tag
        self.xprev = xprev
        self.r_out = r_out
        self.xf_out = xf_out
        self.bbuf = bbuf
        self.bias_cols = bias_cols
        self.pending = []
        self.first = {}

    def chunk_cb(self, jb, tb, ps, mmcnt):
        C, em = self.C, self.C.em
        tag = self.tag
        i = (jb * 2 + tb) % 4
        xsl = C.ST[:, 512 * i:512 * (i + 1)]
        rsl = C.ST[:, 2048 + 512 * i:2048 + 512 * (i + 1)]
        rbf = C.SMB[:, 1024 * (i % 2):1024 * (i % 2) + 512]
        sq = C.SMB[:, 1024 * (i % 2) + 512:1024 * (i % 2) + 1024]
        C.wait_free("sync", ("xsl", i))
        off = jb * 1024 + 512 * tb
        xc = em.dma("sync", xsl, self.xprev[:, off:off + 512], "x%d" % i)
        em.wait("vector", "mm", mmcnt)
        em.wait("vector", "x%d" % i, xc)
        C.wait_free("vector", ("rsl", i))
        C.wait_free("vector", ("rbf", i % 2))
        if self.bias_cols is not None:
            em.op("vector",
                  (lambda E, o=rsl, a=ps, b=xsl, s=self.bias_cols[:, jb:jb + 1]:
                   E.scalar_tensor_tensor(o, a, s, b, ALU.add, ALU.add)), inc="dv")
        else:
            em.op("vector", lambda E, o=rsl, a=ps, b=xsl: E.tensor_add(o, a, b), inc="dv")
        bankfree = ("dv", em.count("dv"))
        C.set_free(("xsl", i), "dv", em.count("dv"))
        em.op("vector", lambda E, o=rbf, a=rsl: E.tensor_copy(o, a), inc="dv")
        em.op("vector", lambda E, o=sq, a=rbf: E.tensor_mul(o, a, a), inc="dv")
        dvc = em.count("dv")
        em.wait("sync", "dv", dvc - 2)
        rs = em.dma("sync", self.r_out[:, off:off + 512], rsl, "r%d" % i)
        C.set_free(("rsl", i), "r%d" % i, rs)
        self.pending.append((jb, tb, rbf, sq, dvc, i % 2))
        if len(self.pending) > 1:
            self._flush_stat(self.pending.pop(0))
        return bankfree

    def _flush_stat(self, rec):
        C, em = self.C, self.C.em
        jb, tb, rbf, sq, dvc, ri = rec
        em.wait("tensor", "dv", dvc)
        st = (jb == 0)
        sp = (jb == KC - 1)
        em.op("tensor",
              (lambda E, o=C.PS[0:1, 512 * (4 + tb):512 * (4 + tb) + 512], l=C.ones_col_b[:, :],
               r=rbf, s0=st, s1=sp: E.matmul(o, l, r, start=s0, stop=s1)),
              inc=("mm" if sp else None))
        em.op("tensor",
              (lambda E, o=C.PS[0:1, 512 * (6 + tb):512 * (6 + tb) + 512], l=C.ones_col_b[:, :],
               r=sq, s0=st, s1=sp: E.matmul(o, l, r, start=s0, stop=s1)),
              inc=("mm" if sp else None))
        C.set_free(("rbf", ri), "mm", em.count("mm"))

    def finish(self):
        C, em = self.C, self.C.em
        tag = self.tag
        while self.pending:
            self._flush_stat(self.pending.pop(0))
        em.wait("vector", "mm")
        R = C.RW
        for tb in (0, 1):
            mu = R[0:1, 512 * 0:512 * 1]
            msq = R[0:1, 512 * 1:512 * 2]
            al = R[0:1, 512 * 2:512 * 3]
            be = R[0:1, 512 * 3:512 * 4]
            em.op("vector", lambda E, o=mu, a=C.PS[0:1, 512 * (4 + tb):512 * (4 + tb) + 512]:
                  E.tensor_scalar_mul(o, a, 1.0 / D), inc="dv")
            em.op("vector", lambda E, o=msq, a=C.PS[0:1, 512 * (6 + tb):512 * (6 + tb) + 512]:
                  E.tensor_scalar_mul(o, a, 1.0 / D), inc="dv")
            em.op("vector", lambda E, o=be, a=mu, b=mu: E.tensor_mul(o, a, b), inc="dv")
            em.op("vector", lambda E, o=msq, a=msq, b=be: E.tensor_sub(o, a, b), inc="dv")
            em.op("vector", lambda E, o=msq, a=msq: E.tensor_scalar_add(o, a, EPS), inc="dv")
            em.wait("scalar", "dv")
            em.op("scalar", lambda E, o=msq, a=msq: E.activation(o, a, AF.Ln), inc="ac")
            em.wait("vector", "ac")
            em.op("vector", lambda E, o=msq, a=msq: E.tensor_scalar_mul(o, a, -0.5), inc="dv")
            em.wait("scalar", "dv")
            em.op("scalar", lambda E, o=al, a=msq: E.activation(o, a, AF.Exp), inc="ac")
            em.wait("vector", "ac")
            em.op("vector", lambda E, o=be, a=mu, b=al: E.tensor_mul(o, a, b), inc="dv")
            em.op("vector", lambda E, o=be, a=be: E.tensor_scalar_mul(o, a, -1.0), inc="dv")
            em.wait("tensor", "dv")
            for k, row in ((0, al), (1, be)):
                pb = tb * 2 + k
                C.wait_free("tensor", ("bank", pb))
                em.op("tensor",
                      (lambda E, o=C.bank(pb), l=C.ones_row_f[0:1, 0:128], r=row:
                       E.matmul(o, l, r, start=True, stop=True)), inc="mm")
            em.wait("vector", "mm")
            for k in (0, 1):
                pb = tb * 2 + k
                em.op("vector",
                      (lambda E,
                       o=C.ST[:, 4096 + 1024 * tb + 512 * k:4096 + 1024 * tb + 512 * (k + 1)],
                       a=C.bank(pb): E.tensor_copy(o, a)), inc="dv")
                C.set_free(("bank", pb), "dv", em.count("dv"))
        spbase = em.count("dv")
        for i in range(4):
            em.wait("sync", "r%d" % i)
            em.wait("gpsimd", "r%d" % i)
        for jb in range(KC):
            for tb in (0, 1):
                i = (jb * 2 + tb) % 4
                off = jb * 1024 + 512 * tb
                xsl = C.ST[:, 512 * i:512 * (i + 1)]
                rsl = C.ST[:, 2048 + 512 * i:2048 + 512 * (i + 1)]
                arep = C.ST[:, 4096 + 1024 * tb:4096 + 1024 * tb + 512]
                brep = C.ST[:, 4096 + 1024 * tb + 512:4096 + 1024 * tb + 1024]
                C.wait_free("gpsimd", ("xsl", i))
                rc = em.dma("gpsimd", xsl, self.r_out[:, off:off + 512], "x%d" % i)
                em.wait("vector", "x%d" % i, rc)
                em.wait("vector", "dv", spbase)
                C.wait_free("vector", ("rsl", i))
                em.op("vector", lambda E, o=rsl, a=xsl, b=arep: E.tensor_mul(o, a, b), inc="dv")
                C.set_free(("xsl", i), "dv", em.count("dv"))
                em.op("vector", lambda E, o=rsl, a=rsl, b=brep: E.tensor_add(o, a, b), inc="dv")
                dvc = em.count("dv")
                em.wait("sync", "dv", dvc)
                rs = em.dma("sync", self.xf_out[:, off:off + 512], rsl, "r%d" % i)
                C.set_free(("rsl", i), "r%d" % i, rs)
                if self.bbuf is not None:
                    em.op("vector",
                          (lambda E,
                           o=self.bbuf[:, jb * 1024 + 512 * tb:jb * 1024 + 512 * (tb + 1)],
                           a=rsl: E.tensor_copy(o, a)), inc="dv")
        return ("dv", em.count("dv"))


def setup_consts(C, io):
    em = C.em
    em.dma("sync", C.ident_f[:, :], io["ident"][:, :], "c0")
    em.dma("gpsimd", C.ident_b[:, :], io["ident"][:, :], "c0g")
    em.op("vector", lambda E, t=C.ones_col_b: E.memset(t[:, :], 1.0), inc="dv")
    em.op("vector", lambda E, t=C.ones_col_f: E.memset(t[:, :], 1.0), inc="dv")
    em.op("vector", lambda E, t=C.ones_row_b: E.memset(t[:, :], 1.0), inc="dv")
    em.op("vector", lambda E, t=C.ones_row_f: E.memset(t[:, :], 1.0), inc="dv")
    em.wait_fin("tensor", "c0")
    em.wait_fin("tensor", "c0g")
    em.wait("tensor", "dv")
    em.wait("scalar", "dv")


def emit_l0ffn(C, io, bprev, ep):
    em = C.em
    tag = "f0"

    def xr(kc, tb):
        return bprev[:, kc * 1024 + tb * 512:kc * 1024 + (tb + 1) * 512]

    C.wait_free("gpsimd", ("WS", 0))
    C.wait_free("gpsimd", ("WS", 1))
    wr_rdy = em.dma("gpsimd", C.WS[:, 0:256], io["l0_wrT"][:, :], "wf0")
    at_rdy = em.dma("gpsimd", C.WS[:, 256:4352], io["l0_aT"][:, :], "wf0")
    C.wait_free("gpsimd", ("rbf", 1))
    br_rdy = em.dma("gpsimd", C.SMB[0:1, 2040:2048], io["l0_br"][:, :], "wf0")
    mx_rdy = em.dma("sync", C.RW[0:8, 2048:2176], io["mexp"][:, :], "c2")

    em.wait("tensor", "wf0", br_rdy)
    for tb in (0, 1):
        pb = 4 + tb
        ps = C.PS[0:8, 512 * pb:512 * pb + 512]
        for kc in range(KC):
            em.op("tensor",
                  (lambda E, o=ps, l=C.WS[:, kc * 8:(kc + 1) * 8], r=xr(kc, tb),
                   st=(kc == 0): E.matmul(o, l, r, start=st, stop=False)))
        em.op("tensor",
              (lambda E, o=ps, l=C.SMB[0:1, 2040:2048], r=C.ones_row_b[0:1, :]:
               E.matmul(o, l, r, start=False, stop=True)), inc="mm")
    lg_mm = em.count("mm")
    em.wait("vector", "mm", lg_mm)
    for i_ in range(2):
        C.wait_free("vector", ("xsl", i_))
    for tb in (0, 1):
        em.op("vector",
              (lambda E, o=C.ST[0:8, 512 * tb:512 * (tb + 1)],
               a=C.PS[0:8, 512 * (4 + tb):512 * (4 + tb) + 512]:
               E.tensor_copy(o, a)), inc="dv")
    lg_dv = em.count("dv")

    em.wait("tensor", "dv", lg_dv)
    em.wait("tensor", "c2", mx_rdy)
    for tb in (0, 1):
        for tc in range(4):
            gtc = tb * 4 + tc
            lgt = C.RW[:, 2176:2184]
            et = C.RW[:, 2184:2192]
            negmax = C.RW[:, 2192:2193]
            ssum = C.RW[:, 2193:2194]
            rc = C.RW[:, 2194:2195]
            sct = C.RW[:, 2196:2204]
            scT = C.RW[0:8, 2204:2332]
            em.wait("tensor", "dv")
            em.op("tensor",
                  (lambda E, o=C.PS[:, 512 * 6:512 * 6 + 8],
                   a=C.ST[0:8, gtc * 128:(gtc + 1) * 128], i=C.ident_f[0:8, 0:8]:
                   E.transpose(o, a, i)), inc="mm")
            em.wait("vector", "mm")
            em.op("vector", lambda E, o=lgt, a=C.PS[:, 512 * 6:512 * 6 + 8]:
                  E.tensor_copy(o, a), inc="dv")
            em.op("vector", lambda E, o=negmax, a=lgt:
                  E.reduce_max(o, a, axis=AX.X, negate=True), inc="dv")
            em.wait("scalar", "dv")
            em.op("scalar", lambda E, o=et, a=lgt, b=negmax, s=ssum:
                  E.activation(o, a, AF.Exp, bias=b, accum_out=s), inc="ac")
            em.wait("vector", "ac")
            em.op("vector", lambda E, o=rc, a=ssum: E.reciprocal(o, a), inc="dv")
            em.op("vector", lambda E, o=sct, a=et, s=rc:
                  E.tensor_scalar(o, a, s, 16.0, ALU.mult, ALU.mult), inc="dv")
            em.wait("tensor", "dv")
            em.op("tensor",
                  (lambda E, o=C.PS[0:8, 512 * 6 + 128:512 * 6 + 256], a=sct, i=C.ident_f[:, :]:
                   E.transpose(o, a, i)), inc="mm")
            em.wait("vector", "mm")
            em.op("vector", lambda E, o=scT, a=C.PS[0:8, 512 * 6 + 128:512 * 6 + 256]:
                  E.tensor_copy(o, a), inc="dv")
            em.wait("tensor", "dv")
            em.op("tensor",
                  (lambda E, o=C.PS[:, 512 * 7 + 128 * tc:512 * 7 + 128 * (tc + 1)],
                   l=C.RW[0:8, 2048:2176], r=scT:
                   E.matmul(o, l, r, start=True, stop=True)), inc="mm")
        em.wait("vector", "mm")
        em.op("vector", (lambda E, o=C.ST[:, 4096 + 512 * tb:4096 + 512 * (tb + 1)], a=C.bank(7):
                         E.tensor_copy(o, a)), inc="dv")
        if "dbg_srep" in io:
            em.wait("sync", "dv")
            em.dma("sync", io["dbg_srep"][:, 512 * tb:512 * (tb + 1)],
                   C.ST[:, 4096 + 512 * tb:4096 + 512 * (tb + 1)], "dg")
    srep_dv = em.count("dv")
    C.set_free(("xsl", 0), "mm", em.count("mm"))
    C.set_free(("xsl", 1), "mm", em.count("mm"))

    em.wait("tensor", "wf0", br_rdy)
    em.wait("tensor", "dv", srep_dv)
    for tb in (0, 1):
        pb = 4 + tb
        ps = C.bank(pb)
        for kc in range(KC):
            em.op("tensor",
                  (lambda E, o=ps, l=C.WS[:, 256 + kc * 128:256 + (kc + 1) * 128],
                   r=xr(kc, tb), st=(kc == 0), sp=(kc == KC - 1):
                   E.matmul(o, l, r, start=st, stop=sp)),
                  inc=("mm" if kc == KC - 1 else None))
        em.wait("vector", "mm")
        em.op("vector",
              (lambda E, o=C.RW[:, :].bitcast(BF16)[:, 4864 + 512 * tb:4864 + 512 * (tb + 1)],
               a=ps, b=C.ST[:, 4096 + 512 * tb:4096 + 512 * (tb + 1)]:
               E.tensor_mul(o, a, b)), inc="dv")
        if "dbg_lowS" in io:
            em.wait("sync", "dv")
            em.dma("sync", io["dbg_lowS"][:, 512 * tb:512 * (tb + 1)],
                   C.RW[:, :].bitcast(BF16)[:, 4864 + 512 * tb:4864 + 512 * (tb + 1)], "dg")
    low_dv = em.count("dv")

    em.wait("gpsimd", "mm")
    bt_rdy = em.dma("gpsimd", C.WS[:, 256:4352], io["l0_bT"][:, :], "wb")

    if "dbg_lowS" in io:
        em.wait("vector", "dg")
    if "dbg_bt" in io:
        em.wait("sync", "wb", bt_rdy)
        em.dma("sync", io["dbg_bt"][:, :], C.WS[:, 256:4352], "dg2")
        em.wait("tensor", "dg2")
    em.wait("tensor", "wb", bt_rdy)
    em.wait("tensor", "dv", low_dv)
    for jb in range(KC):
        for tb in (0, 1):
            pb = (jb % 2) * 2 + tb
            C.wait_free("tensor", ("bank", pb))
            ps = C.bank(pb)
            em.op("tensor",
                  (lambda E, o=ps, l=C.WS[:, 256 + jb * 128:256 + (jb + 1) * 128],
                   r=C.RW[:, :].bitcast(BF16)[:, 4864 + 512 * tb:4864 + 512 * (tb + 1)]:
                   E.matmul(o, l, r, start=True, stop=True)), inc="mm")
            fs = ep.chunk_cb(jb, tb, ps, em.count("mm"))
            C.set_free(("bank", pb), fs[0], fs[1])
    return ep.finish()


def _f(a):
    return np.ascontiguousarray(a, dtype=np.float32)


def _bm(wt, njb, jw):
    nkc = wt.shape[0] // 128
    return _f(wt.reshape(nkc, 128, njb, jw).transpose(2, 1, 0, 3)
              .reshape(njb * 128, nkc * jw))


def host_prep(inp):
    sh = {}
    x = inp["L0_attn_hidden_states"][0]
    xT = _f(x.T)
    sh["ident"] = np.eye(128, dtype=np.float32)
    sh["mexp"] = np.repeat(np.eye(8, dtype=np.float32), 16, axis=1)
    sh["jcv"] = np.broadcast_to(np.arange(4, dtype=np.float32)[None, :] * 128.0,
                                (128, 4)).copy()
    sh["posr"] = (np.arange(68, dtype=np.float32)[None, :] * 128.0).copy()
    sh["l0_qkvT"] = _bm(np.concatenate(
        [inp["L0_attn_Wq"], inp["L0_attn_Wk"], inp["L0_attn_Wv"]], axis=0).T, 48, 128)
    sh["l0_bv"] = _f(inp["L0_attn_bv"][None, :])
    sh["l0_woT"] = _bm(inp["L0_attn_Wo"].T, KC, 128)
    sh["l0_wrT"] = _bm(inp["L0_ffn_Wr"].T, 1, 8)
    sh["l0_br"] = _f(inp["L0_ffn_br"][None, :])
    sh["l0_aT"] = _bm(inp["L0_ffn_A"].transpose(2, 0, 1).reshape(D, 128), 1, 128)
    sh["l0_bT"] = _f(inp["L0_ffn_B"].transpose(0, 2, 1).reshape(128, D))
    sh["l1_woT"] = _bm(inp["L1_attn_w_out"].T, KC, 128)
    sh["l1_wT"] = _bm(inp["L1_ffn_W"].T, KC, 128)
    sh["l1_bias"] = _f(inp["L1_ffn_bias"].reshape(32, 128).T)
    sh["l1_laT"] = _bm(inp["L1_ffn_lora_A"].transpose(2, 0, 1).reshape(D, 128), 1, 128)
    sh["l1_lbT"] = _f(inp["L1_ffn_lora_B"].transpose(0, 2, 1).reshape(128, D))
    sh["l1_lw"] = _f(inp["L1_ffn_lora_weights"][:, None])

    wqkvT = _f(inp["L1_attn_w_qkv"].T)
    percore = []
    import math
    slopes = 1.0 / (2.0 ** (np.arange(1, 33, dtype=np.float64) * (8.0 / 32)))
    ii = np.arange(128)[:, None]
    jj = np.arange(512)[None, :]
    for c in range(NCORE):
        pc = {}
        pc["x_in"] = _f(xT[:, c * T:(c + 1) * T].reshape(KC, 128, T)
                        .transpose(1, 0, 2).reshape(128, KC * T))
        cols = np.concatenate([np.arange(512 * c, 512 * c + 512),
                               4096 + np.arange(512 * c, 512 * c + 512),
                               8192 + np.arange(512 * c, 512 * c + 512)])
        pc["l1_qkvT"] = _bm(wqkvT[:, cols], 12, 128)
        for nm, key in (("qw", "L1_attn_q_ln_w"), ("qb", "L1_attn_q_ln_b"),
                        ("kw", "L1_attn_k_ln_w"), ("kb", "L1_attn_k_ln_b")):
            pc["l1_" + nm] = _f(inp[key][512 * c:512 * (c + 1)].reshape(4, 128).T)
        sl = slopes[4 * c:4 * c + 4]
        pc["l1_slopes"] = _f(np.broadcast_to(sl[None, :], (128, 4)))
        strips = np.zeros((4, 5, 128, 512), dtype=np.float64)
        for h in range(4):
            base = sl[h] * (ii - jj)
            strips[h, 0] = base
            for jc in range(4):
                m = np.where(ii <= jj - 128 * jc, base, NEG)
                strips[h, 1 + jc] = m
        pc["l1_strips"] = strips.astype(np.float32).reshape(4 * 5 * 128, 512)
        percore.append(pc)
    return sh, percore


def _decl(nc, name, shape, dtype, kind):
    return nc.dram_tensor(name, list(shape), dtype, kind=kind).ap()


def build_l0ffn_stage(dbg=False):
    nc = bass.Bass("TRN2", target_bir_lowering=False, debug=False, num_devices=8)
    io = {}
    io["x1f"] = _decl(nc, "x1f", [128, KC * T], F32, "ExternalInput")
    for nm, shp in (("l0_wrT", [128, 256]), ("l0_br", [1, 8]), ("l0_aT", [128, KC * 128]),
                    ("l0_bT", [128, D]), ("mexp", [8, 128]), ("ident", [128, 128])):
        io[nm] = _decl(nc, nm, shp, F32, "ExternalInput")
    io["r2"] = _decl(nc, "r2", [128, KC * T], F32, "Internal")
    io["x2f"] = _decl(nc, "x2f", [128, KC * T], F32, "ExternalOutput")
    if dbg:
        io["dbg_lg"] = _decl(nc, "dbg_lg", [8, 1024], F32, "ExternalOutput")
        io["dbg_srep"] = _decl(nc, "dbg_srep", [128, 1024], F32, "ExternalOutput")
        io["dbg_lowS"] = _decl(nc, "dbg_lowS", [128, 1024], BF16, "ExternalOutput")
        io["dbg_bt"] = _decl(nc, "dbg_bt", [128, 4096], BF16, "ExternalOutput")
    with ExitStack() as ctx:
        em = Em(nc)
        C = Ctx(nc, em, ctx)
        setup_consts(C, io)
        bld = em.dma("gpsimd", C.A0[:, 0:KC * 1024], io["x1f"][:, :], "c0g")
        em.wait_fin("tensor", "c0g")
        ep = Ep(C, "e2", io["x1f"], io["r2"], io["x2f"], C.A1)
        emit_l0ffn(C, io, C.A0, ep)
        for i in range(4):
            em.wait("sync", "r%d" % i)
        em.finalize(ctx)
    return nc


_NC_CACHE = {}


def _get_nc(name, builder):
    if name not in _NC_CACHE:
        _NC_CACHE[name] = builder()
    return _NC_CACHE[name]


def kernel(**inputs):
    return kernel_fused(**inputs)


def kernel_multilaunch(**inputs):
    sh, pc = host_prep(inputs)
    cores = list(range(NCORE))

    nc1 = _get_nc("s1", build_l0attn_stage)
    m1 = []
    for c in cores:
        m1.append({"x_in": pc[c]["x_in"], "l0_qkvT": sh["l0_qkvT"],
                   "l0_bv": sh["l0_bv"], "l0_woT": sh["l0_woT"],
                   "ident": sh["ident"]})
    r1 = run_bass_kernel_spmd(nc1, m1, core_ids=cores)
    x1 = [r1.results[c]["x1f"] for c in cores]

    nc2 = _get_nc("s2", build_l0ffn_stage)
    m2 = []
    for c in cores:
        m2.append({"x1f": x1[c], "l0_wrT": sh["l0_wrT"], "l0_br": sh["l0_br"],
                   "l0_aT": sh["l0_aT"], "l0_bT": sh["l0_bT"],
                   "mexp": sh["mexp"], "ident": sh["ident"]})
    r2 = run_bass_kernel_spmd(nc2, m2, core_ids=cores)
    x2 = [r2.results[c]["x2f"] for c in cores]

    nc3 = _get_nc("s3", build_l1attn_stage)
    m3 = []
    for c in cores:
        m = {"x2f": x2[c], "ident": sh["ident"], "jcv": sh["jcv"],
             "posr": sh["posr"], "l1_woT": sh["l1_woT"]}
        for nm in ("l1_qkvT", "l1_qw", "l1_qb", "l1_kw", "l1_kb",
                   "l1_slopes", "l1_strips"):
            m[nm] = pc[c][nm]
        m3.append(m)
    r3 = run_bass_kernel_spmd(nc3, m3, core_ids=cores)
    x3 = [r3.results[c]["x3f"] for c in cores]

    nc4 = _get_nc("s4", build_l1ffn_stage)
    m4 = []
    for c in cores:
        m4.append({"x3f": x3[c], "l1_wT": sh["l1_wT"], "l1_bias": sh["l1_bias"],
                   "l1_laT": sh["l1_laT"], "l1_lbT": sh["l1_lbT"],
                   "l1_lw": sh["l1_lw"], "mexp": sh["mexp"], "ident": sh["ident"]})
    r4 = run_bass_kernel_spmd(nc4, m4, core_ids=cores)
    y = np.concatenate(
        [np.asarray(r4.results[c]["y"]).reshape(128, KC, T)
         .transpose(1, 0, 2).reshape(D, T).T for c in cores], axis=0)
    return y[None, :, :].astype(np.float32)


def emit_l0attn(C, io, ep):
    em = C.em
    stb = C.stb()

    bv_rdy = em.dma("gpsimd", C.stb()[0:1, 0:1024], io["l0_bv"][:, :], "c0g")
    em.wait_fin("tensor", "c0g")

    qsp = io["qspill"].rearrange("(h p) t -> p h t", p=128)
    kgi = io["kgat_in"].rearrange("(kc p) t -> p kc t", p=128)
    vgi = io["vgat_in"].rearrange("(tc p) d -> p tc d", p=128)

    def xr(kc, tb):
        return C.A0[:, kc * 1024 + tb * 512:kc * 1024 + (tb + 1) * 512]

    def extra(jb, tb):
        if jb >= 40:
            return [(C.stb()[0:1, (jb - 40) * 128:(jb - 39) * 128],
                     C.ones_row_b[0:1, :])]
        return []

    def qkv_cb(jb, tb, ps, mmcnt):
        i = (jb * 2 + tb) % 4
        slot = C.SMB[:, 512 * i:512 * (i + 1)]
        em.wait("vector", "mm", mmcnt)
        C.wait_free("vector", ("esl", i))
        if jb < 32:
            em.op("vector", lambda E, o=slot, a=ps:
                  E.tensor_scalar_mul(o, a, ATT_SCALE), inc="dv")
        else:
            em.op("vector", lambda E, o=slot, a=ps: E.tensor_copy(o, a), inc="dv")
        dvc = em.count("dv")
        bankfree = ("dv", dvc)
        if jb < 40:
            em.wait("sync", "dv", dvc)
            dst = (qsp[:, jb, 512 * tb:512 * (tb + 1)] if jb < 32
                   else kgi[:, jb - 32, 512 * tb:512 * (tb + 1)])
            sc = em.dma("sync", dst, slot, "e%d" % i)
            C.set_free(("esl", i), "e%d" % i, sc)
        else:
            dvch = jb - 40
            for tc in range(4):
                pb = 4 + (tc % 2)
                C.wait_free("tensor", ("vtb", pb))
                em.wait("tensor", "dv", dvc)
                em.op("tensor",
                      (lambda E, o=C.bank_bf(pb)[:, 0:128],
                       a=slot[:, 128 * tc:128 * (tc + 1)], ii=C.ident_b[:, :]:
                       E.transpose(o, a, ii)), inc="mm")
                em.wait("vector", "mm")
                C.wait_free("vector", ("vsl", pb))
                vslot = C.stb()[:, 1024 + 128 * (pb - 4):1024 + 128 * (pb - 3)]
                em.op("vector", (lambda E, o=vslot, a=C.bank_bf(pb)[:, 0:128]:
                                 E.tensor_copy(o, a)), inc="dv")
                C.set_free(("vtb", pb), "dv", em.count("dv"))
                em.wait("sync", "dv")
                vc = em.dma("sync", vgi[:, tb * 4 + tc, 128 * dvch:128 * (dvch + 1)],
                            vslot, "v%d" % (pb - 4))
                C.set_free(("vsl", pb), "v%d" % (pb - 4), vc)
            C.set_free(("esl", i), "mm", em.count("mm"))
        return bankfree

    emit_proj(C, "qkv", io["l0_qkvT"], 48, 128, xr, qkv_cb, extra=extra,
              jb_order=list(range(32, 48)) + list(range(32)))
    for t in range(4):
        if ("bank", t) in C.free:
            C.set_free(("ab", t), *C.free[("bank", t)])

    for s in ("g0", "g1", "g2", "g3", "v0", "v1"):
        em.wait("gpsimd", s)
    em.op("gpsimd",
          (lambda E: E.collective_compute(
              "AllGather", ALU.bypass, replica_groups=[list(range(NCORE))],
              ins=[io["kgat_in"].opt()], outs=[io["kgat"].opt()])), inc="cc")
    em.op("gpsimd",
          (lambda E: E.collective_compute(
              "AllGather", ALU.bypass, replica_groups=[list(range(NCORE))],
              ins=[io["vgat_in"].opt()], outs=[io["vgat"].opt()])), inc="cc")
    cc_done = em.count("cc")

    mm_qkv_done = em.count("mm")
    kgs = io["kgat"].rearrange("(r f) t -> f r t", f=1024)
    vgs = io["vgat"].rearrange("(ch p) d -> p ch d", p=128)
    em.wait("sync", "cc", cc_done)

    def load_kv(g):
        if g >= 8:
            return None
        s = g % 2
        C.wait_free("sync", ("kvs", s))
        em.dma("sync",
               C.A0[:, 16384 * s:16384 * s + 8192].rearrange("p (r t) -> p r t", t=1024),
               kgs[128 * g:128 * (g + 1), :, :], "k%d" % s)
        kc_ = em.dma("sync",
                     C.A0[:, 16384 * s + 8192:16384 * (s + 1)].rearrange(
                         "p (ch d) -> p ch d", d=128),
                     vgs[:, :, 128 * g:128 * (g + 1)], "k%d" % s)
        return kc_

    def load_q(qh, g):
        sq = (4 * g + qh) % 2
        C.wait_free("sync", ("qts", sq))
        return em.dma("sync", stb[:, 1024 * sq:1024 * (sq + 1)],
                      qsp[:, 4 * g + qh, :], "q%d" % sq)

    em.wait("sync", "v0")
    em.wait("sync", "v1")
    for s_ in ("e0", "e1", "e2", "e3"):
        em.wait("sync", s_)
    em.wait("sync", "mm", mm_qkv_done)
    kvrdy = {0: load_kv(0), 1: load_kv(1)}
    qrdy = {0: load_q(0, 0)}
    for g in range(8):
        s = g % 2
        kT = C.A0[:, 16384 * s:16384 * s + 8192]
        vT = C.A0[:, 16384 * s + 8192:16384 * (s + 1)]
        em.wait("tensor", "k%d" % s, kvrdy[g])
        for qh in range(4):
            hq = 4 * g + qh
            sq = hq % 2
            em.wait("tensor", "q%d" % sq, qrdy[hq])
            if hq + 1 < 32:
                qrdy[hq + 1] = load_q((hq + 1) % 4, (hq + 1) // 4)
            rhs_q0 = stb[:, 1024 * sq:1024 * sq + 512]
            for t in range(4):
                C.wait_free("tensor", ("bank", t))
                em.op("tensor",
                      (lambda E, o=C.bank(t), l=kT[:, 128 * t:128 * (t + 1)], r=rhs_q0:
                       E.matmul(o, l, r, start=True, stop=True)), inc="mm")
            cmm = em.count("mm")
            em.wait("vector", "mm", cmm)
            em.op("vector", (lambda E, o=C.ST[:, 3072:3073], a=C.PS[:, 0:2048]:
                             E.reduce_max(o, a, axis=AX.X)), inc="dv")
            for t in range(4):
                C.set_free(("bank", t), "dv", em.count("dv"))
            em.wait("tensor", "dv")
            em.op("tensor",
                  (lambda E, o=C.PS[0:1, 2048:2176], a=C.ST[:, 3072:3073],
                   ii=C.ident_f[:, :]: E.transpose(o, a, ii)), inc="mm")
            em.wait("vector", "mm")
            em.op("vector", (lambda E, o=C.RW[0:1, 0:1], a=C.PS[0:1, 2048:2176]:
                             E.reduce_max(o, a, axis=AX.X)), inc="dv")
            em.op("vector", (lambda E, o=C.RW[0:1, 1:2], a=C.RW[0:1, 0:1]:
                             E.tensor_scalar_mul(o, a, -1.0)), inc="dv")
            em.wait("tensor", "dv")
            em.op("tensor",
                  (lambda E, o=C.PS[:, 2048:2049], l=C.ones_row_f[0:1, 0:128],
                   r=C.RW[0:1, 1:2]: E.matmul(o, l, r, start=True, stop=True)),
                  inc="mm")
            em.wait("vector", "mm")
            em.wait("vector", "ac")
            em.op("vector", (lambda E, o=C.ST[:, 3074:3075], a=C.PS[:, 2048:2049]:
                             E.tensor_copy(o, a)), inc="dv")
            negc_dv = em.count("dv")
            em.wait("scalar", "dv", negc_dv)
            acc0 = C.ST[:, 4608:5632]
            for b in range(2):
                rhs_q = stb[:, 1024 * sq + 512 * b:1024 * sq + 512 * (b + 1)]
                pend = []

                def flush_l0(rec, vT=vT):
                    rs_, t0_, fin_ = rec
                    em.wait("tensor", "ac", rs_[1])
                    if fin_:
                        C.wait_free("tensor", ("cbank",))
                    for k2 in range(2):
                        t = t0_ + k2
                        pr = stb[:, 2048 + 1024 * rs_[0] + 512 * k2:
                                 2048 + 1024 * rs_[0] + 512 * (k2 + 1)]
                        st_ = t == 0
                        sp_ = t == 63
                        em.op("tensor",
                              (lambda E, o=C.PS[:, 3584:4096],
                               l=vT[:, 128 * t:128 * (t + 1)], r=pr, s0=st_, s1=sp_:
                               E.matmul(o, l, r, start=s0, stop=s1)),
                              inc=("mm" if k2 == 1 else None))
                    C.set_free(("ring", rs_[0]), "mm", em.count("mm"))

                for p in range(32):
                    t0 = 2 * p
                    pa = (p % 2) * 2
                    rs = p % 4
                    for k2 in range(2):
                        C.wait_free("tensor", ("bank", pa + k2))
                        em.op("tensor",
                              (lambda E, o=C.bank(pa + k2),
                               l=kT[:, 128 * (t0 + k2):128 * (t0 + k2 + 1)],
                               r=rhs_q: E.matmul(o, l, r, start=True, stop=True)),
                              inc="mm")
                    qkmm = em.count("mm")
                    em.wait("scalar", "mm", qkmm)
                    C.wait_free("scalar", ("ring", rs))
                    C.wait_free("scalar", ("ringv", rs))
                    em.op("scalar",
                          (lambda E, o=stb[:, 2048 + 1024 * rs:2048 + 1024 * (rs + 1)],
                           a=C.PS[:, 1024 * (p % 2):1024 * (p % 2) + 1024],
                           bb=C.ST[:, 3074:3075]: E.activation(o, a, AF.Exp, bias=bb)),
                          inc="ac", noser=True)
                    eac = em.count("ac")
                    C.set_free(("bank", pa), "ac", eac)
                    C.set_free(("bank", pa + 1), "ac", eac)
                    em.wait("vector", "ac", eac)
                    tile = stb[:, 2048 + 1024 * rs:2048 + 1024 * (rs + 1)]
                    if p == 0:
                        C.wait_free("vector", ("accl0",))
                        em.op("vector", lambda E, o=acc0, a=tile: E.tensor_copy(o, a),
                              inc="dv")
                    else:
                        em.op("vector", lambda E, o=acc0, a=tile: E.tensor_add(o, o, a),
                              inc="dv")
                    C.set_free(("ringv", rs), "dv", em.count("dv"))
                    pend.append(((rs, eac), t0, p == 0))
                    if len(pend) > 2:
                        flush_l0(pend.pop(0))
                while pend:
                    flush_l0(pend.pop(0))
                em.op("vector",
                      (lambda E, o=acc0[:, 0:512], a=acc0[:, 0:512], b=acc0[:, 512:1024]:
                       E.tensor_add(o, a, b)), inc="dv")
                C.wait_free("tensor", ("sbank",))
                em.wait("tensor", "dv", em.count("dv"))
                em.op("tensor",
                      (lambda E, o=C.PS[0:1, 3072:3584], l=C.ones_col_f[:, :],
                       r=acc0[:, 0:512]:
                       E.matmul(o, l, r, start=True, stop=True)), inc="mm")
                C.set_free(("accl0",), "mm", em.count("mm"))
                pvmm = em.count("mm")
                em.wait("scalar", "mm", pvmm)
                em.op("scalar", (lambda E, o=C.RW[0:1, 512:1024], a=C.PS[0:1, 3072:3584]:
                                 E.activation(o, a, AF.Ln)), inc="ac")
                em.wait("vector", "ac")
                em.op("vector", (lambda E, o=C.RW[0:1, 512:1024], a=C.RW[0:1, 512:1024]:
                                 E.tensor_scalar_mul(o, a, -1.0)), inc="dv")
                em.wait("scalar", "dv")
                em.op("scalar", (lambda E, o=C.RW[0:1, 1024:1536], a=C.RW[0:1, 512:1024]:
                                 E.activation(o, a, AF.Exp)), inc="ac")
                em.wait("tensor", "ac")
                em.op("tensor",
                      (lambda E, o=C.PS[:, 2560:3072], l=C.ones_row_f[0:1, 0:128],
                       r=C.RW[0:1, 1024:1536]: E.matmul(o, l, r, start=True, stop=True)),
                      inc="mm")
                em.wait("vector", "mm")
                em.op("vector", (lambda E, o=C.ST[:, 3584:4096], a=C.PS[:, 2560:3072]:
                                 E.tensor_copy(o, a)), inc="dv")
                em.op("vector",
                      (lambda E, o=C.A1[:, hq * 1024 + 512 * b:hq * 1024 + 512 * (b + 1)],
                       a=C.PS[:, 3584:4096], bx=C.ST[:, 3584:4096]:
                       E.tensor_mul(o, a, bx)), inc="dv")
                C.set_free(("sbank",), "ac", em.count("ac"))
                C.set_free(("cbank",), "dv", em.count("dv"))
            C.set_free(("qts", sq), "mm", em.count("mm"))
        kvdone = em.count("mm")
        C.set_free(("kvs", s), "mm", kvdone)
        if g + 2 < 8:
            kvrdy[g + 2] = load_kv(g + 2)
    return em.count("dv")


def build_l0attn_stage():
    nc = bass.Bass("TRN2", target_bir_lowering=False, debug=False, num_devices=8)
    io = {}
    io["x_in"] = _decl(nc, "x_in", [128, KC * T], F32, "ExternalInput")
    io["l0_qkvT"] = _decl(nc, "l0_qkvT", [48 * 128, KC * 128], F32, "ExternalInput")
    io["l0_bv"] = _decl(nc, "l0_bv", [1, 1024], F32, "ExternalInput")
    io["l0_woT"] = _decl(nc, "l0_woT", [KC * 128, KC * 128], F32, "ExternalInput")
    io["ident"] = _decl(nc, "ident", [128, 128], F32, "ExternalInput")
    io["qspill"] = _decl(nc, "qspill", [D, T], BF16, "Internal")
    io["kgat_in"] = _decl(nc, "kgat_in", [1024, T], BF16, "Internal")
    io["kgat"] = _decl(nc, "kgat", [S, T], BF16, "Internal")
    io["vgat_in"] = _decl(nc, "vgat_in", [T, 1024], BF16, "Internal")
    io["vgat"] = _decl(nc, "vgat", [S, 1024], BF16, "Internal")
    io["r1"] = _decl(nc, "r1", [128, KC * T], F32, "Internal")
    io["x1f"] = _decl(nc, "x1f", [128, KC * T], F32, "ExternalOutput")
    with ExitStack() as ctx:
        em = Em(nc)
        C = Ctx(nc, em, ctx)
        setup_consts(C, io)
        em.dma("gpsimd", C.A0[:, 0:KC * 1024], io["x_in"][:, :], "c0g")
        em.wait_fin("tensor", "c0g")
        for t in range(4):
            C.free[("ab", t)] = None
        ctx_all = emit_l0attn_with_wo(C, io)
        for i in range(4):
            em.wait("sync", "r%d" % i)
        em.finalize(ctx)
    return nc


def emit_l0attn_with_wo(C, io):
    em = C.em
    ctx_all = emit_l0attn(C, io, None)
    ep = Ep(C, "e1", io["x_in"], io["r1"], io["x1f"], C.A0)
    for i in range(4):
        C.set_free(("xsl", i), "dv", ctx_all)
        C.set_free(("rsl", i), "dv", ctx_all)
    em.wait("tensor", "dv", ctx_all)
    for pb in range(4):
        C.set_free(("bank", pb), C.free.get(("ab", pb), ("dv", 0))[0]
                   if C.free.get(("ab", pb)) else "dv",
                   C.free.get(("ab", pb))[1] if C.free.get(("ab", pb)) else 0)

    def ctx_rhs(kc, tb):
        return C.A1[:, kc * 1024 + tb * 512:kc * 1024 + (tb + 1) * 512]

    emit_proj(C, "wo", io["l0_woT"], KC, 128, ctx_rhs, ep.chunk_cb)
    return ep.finish()


def emit_l1attn(C, io, ep):
    em = C.em
    stb = C.stb()
    RG = [list(range(NCORE))]

    gxc = em.dma("sync", io["xgat_in"].rearrange("(kc p) t -> p kc t", p=128),
                 C.A1[:, :].rearrange("p (kc t) -> p kc t", t=1024), "gx")
    em.wait("gpsimd", "gx", gxc)
    em.op("gpsimd",
          (lambda E: E.collective_compute(
              "AllGather", ALU.bypass, replica_groups=RG,
              ins=[io["xgat_in"].opt()], outs=[io["xgat"].opt()])), inc="cc")
    cc_x = em.count("cc")

    em.dma("sync", C.SM[:, 0:4], io["l1_slopes"][:, :], "c3")
    em.dma("sync", C.SM[:, 4:8], io["l1_qw"][:, :], "c3")
    em.dma("sync", C.SM[:, 8:12], io["l1_qb"][:, :], "c3")
    em.dma("sync", C.SM[:, 12:16], io["l1_kw"][:, :], "c3")
    em.dma("sync", C.SM[:, 16:20], io["l1_kb"][:, :], "c3")
    em.dma("sync", C.SM[:, 20:24], io["jcv"][:, :], "c3")
    em.dma("sync", C.RW[0:1, 2944:3012], io["posr"][:, :], "c3")
    em.wait("vector", "c3")
    em.op("vector", lambda E, o=C.SM[:, 28:32], a=C.SM[:, 4:8]:
          E.tensor_scalar_mul(o, a, ATT_SCALE), inc="dv")
    em.op("vector", lambda E, o=C.SM[:, 32:36], a=C.SM[:, 8:12]:
          E.tensor_scalar_mul(o, a, ATT_SCALE), inc="dv")

    em.wait("sync", "cc", cc_x)
    xg = io["xgat"].rearrange("(r kc p) t -> p r kc t", p=128, kc=KC)
    wq = io["l1_qkvT"].rearrange("(jb p) f -> p jb f", p=128)
    qrs = io["qraw"].rearrange("(h p) t -> p h t", p=128)
    krs = io["kraw"].rearrange("(h p) t -> p h t", p=128)
    vts = io["vtok1"].rearrange("(tc p) d -> p tc d", p=128)
    sqi = io["sqk_in"]

    em.wait("gpsimd", "cc", cc_x)
    groups = (list(range(8)), list(range(8, 12)))
    wg_rdy = {}
    pbctr = [0]
    for gi, grp in enumerate(groups):
        if gi == 1:
            em.wait("gpsimd", "mm")
        for ji, jb in enumerate(grp):
            wg_rdy[jb] = em.dma(
                "gpsimd",
                C.A1[:, ji * 4096:(ji + 1) * 4096],
                wq[:, jb, :], "wg")
        for tblk in range(16):
            s = tblk % 2
            C.wait_free("sync", ("xb", s))
            xrd = em.dma("sync",
                         C.A0[:, 16384 * s:16384 * (s + 1)].rearrange(
                             "p (kc t) -> p kc t", t=512),
                         xg[:, tblk // 2, :, 512 * (tblk % 2):512 * (tblk % 2 + 1)],
                         "xb%d" % s)
            em.wait("tensor", "xb%d" % s, xrd)
            em.wait("tensor", "wg", wg_rdy[grp[-1]])
            for ji, jb in enumerate(grp):
                pb = pbctr[0] % 4
                pbctr[0] += 1
                C.wait_free("tensor", ("bank", pb))
                ps = C.bank(pb)
                for kc in range(KC):
                    em.op("tensor",
                          (lambda E, o=ps, l=C.A1[:, ji * 4096 + kc * 128:ji * 4096 + (kc + 1) * 128],
                           r=C.A0[:, 16384 * s + kc * 512:16384 * s + (kc + 1) * 512],
                           st=(kc == 0), sp=(kc == KC - 1):
                           E.matmul(o, l, r, start=st, stop=sp)),
                          inc=("mm" if kc == KC - 1 else None))
                mmc = em.count("mm")
                ei = pb % 2
                em.wait("vector", "mm", mmc)
                C.wait_free("vector", ("cls", ei))
                cslot = C.SMB[:, 512 * ei:512 * (ei + 1)]
                em.op("vector", lambda E, o=cslot, a=ps:
                      E.tensor_scalar(o, a, 8.0, -8.0, ALU.min, ALU.max), inc="dv")
                C.set_free(("bank", pb), "dv", em.count("dv"))
                if jb < 8:
                    sq_ = C.SMB[:, 1024 + 512 * ei:1024 + 512 * (ei + 1)]
                    C.wait_free("vector", ("sqs", ei))
                    em.op("vector", lambda E, o=sq_, a=cslot: E.tensor_mul(o, a, a),
                          inc="dv")
                    dvc = em.count("dv")
                    em.wait("sync", "dv", dvc)
                    dst = (qrs[:, jb, 512 * tblk:512 * (tblk + 1)] if jb < 4
                           else krs[:, jb - 4, 512 * tblk:512 * (tblk + 1)])
                    sc_ = em.dma("sync", dst, cslot, "p%d" % ei)
                    C.set_free(("cls", ei), "p%d" % ei, sc_)
                    sb0 = 4 if jb < 4 else 6
                    st_ = jb % 4 == 0
                    sp_ = jb % 4 == 3
                    em.wait("tensor", "dv", dvc)
                    C.wait_free("tensor", ("bank", sb0))
                    C.wait_free("tensor", ("bank", sb0 + 1))
                    em.op("tensor",
                          (lambda E, o=C.PS[0:1, 512 * sb0:512 * sb0 + 512],
                           l=C.ones_col_b[:, :], r=cslot, s0=st_, s1=sp_:
                           E.matmul(o, l, r, start=s0, stop=s1)))
                    em.op("tensor",
                          (lambda E, o=C.PS[0:1, 512 * (sb0 + 1):512 * (sb0 + 1) + 512],
                           l=C.ones_col_b[:, :], r=sq_, s0=st_, s1=sp_:
                           E.matmul(o, l, r, start=s0, stop=s1)),
                          inc="mm")
                    C.set_free(("sqs", ei), "mm", em.count("mm"))
                    if sp_:
                        em.wait("vector", "mm")
                        ri = tblk % 2
                        C.wait_free("vector", ("srw", ri))
                        for k2 in range(2):
                            em.op("vector",
                                  (lambda E, o=C.RW[0:1, 1024 * ri + 512 * k2:1024 * ri + 512 * (k2 + 1)],
                                   a=C.PS[0:1, 512 * (sb0 + k2):512 * (sb0 + k2) + 512]:
                                   E.tensor_copy(o, a)), inc="dv")
                        C.set_free(("bank", sb0), "dv", em.count("dv"))
                        C.set_free(("bank", sb0 + 1), "dv", em.count("dv"))
                        em.wait("sync", "dv")
                        row0 = 0 if jb < 4 else 2
                        for k2 in range(2):
                            sc2 = em.dma("sync",
                                         sqi[row0 + k2:row0 + k2 + 1,
                                             512 * tblk:512 * (tblk + 1)],
                                         C.RW[0:1, 1024 * ri + 512 * k2:1024 * ri + 512 * (k2 + 1)],
                                         "sa%d" % ri)
                        C.set_free(("srw", ri), "sa%d" % ri, sc2)
                else:
                    dvch = jb - 8
                    dvc = em.count("dv")
                    for tc in range(4):
                        pt = 4 + (tc % 2)
                        C.wait_free("tensor", ("bank", pt))
                        em.wait("tensor", "dv", dvc)
                        em.op("tensor",
                              (lambda E, o=C.bank_bf(pt)[:, 0:128],
                               a=cslot[:, 128 * tc:128 * (tc + 1)], ii=C.ident_b[:, :]:
                               E.transpose(o, a, ii)), inc="mm")
                        em.wait("vector", "mm")
                        C.wait_free("vector", ("vsl", pt))
                        vslot = C.SMB[:, 1024 + 128 * (pt - 4):1024 + 128 * (pt - 3)]
                        em.op("vector", (lambda E, o=vslot, a=C.bank_bf(pt)[:, 0:128]:
                                         E.tensor_copy(o, a)), inc="dv")
                        C.set_free(("bank", pt), "dv", em.count("dv"))
                        em.wait("sync", "dv")
                        vc = em.dma("sync",
                                    vts[:, 4 * tblk + tc, 128 * dvch:128 * (dvch + 1)],
                                    vslot, "v%d" % (pt - 4))
                        C.set_free(("vsl", pt), "v%d" % (pt - 4), vc)
                    C.set_free(("cls", ei), "mm", em.count("mm"))
    qkv_mm = em.count("mm")
    qkv_dv = em.count("dv")

    for s_ in ("sa0", "sa1"):
        em.wait("gpsimd", s_)
    em.op("gpsimd",
          (lambda E: E.collective_compute(
              "AllReduce", ALU.add, replica_groups=RG,
              ins=[io["sqk_in"].opt()], outs=[io["sqk"].opt()])), inc="cc")
    cc_s = em.count("cc")

    em.wait("sync", "cc", cc_s)
    for s_ in ("p0", "p1", "v0", "v1", "sa0", "sa1"):
        em.wait("vector", s_)
        em.wait("sync", s_)
    em.wait("gpsimd", "dv", qkv_dv)
    em.wait("gpsimd", "mm", qkv_mm)
    for s_ in ("p0", "p1", "v0", "v1"):
        em.wait("gpsimd", s_)
    qns = io["qn"].rearrange("(h p) t -> p h t", p=128)
    kns = io["kn"].rearrange("(h p) t -> p h t", p=128)
    for tblk in range(16):
        C.wait_free("sync", ("strow",))
        src = em.dma("sync",
                     C.RW[0:1, 0:2048].rearrange("p (k t) -> p k t", t=512),
                     io["sqk"][0:4, 512 * tblk:512 * (tblk + 1)], "sr")
        em.wait("vector", "sr", src)
        for qk in range(2):
            srow = C.RW[0:1, 1024 * qk:1024 * qk + 512]
            sqrow = C.RW[0:1, 1024 * qk + 512:1024 * qk + 1024]
            mu = C.RW[0:1, 2048:2560]
            al = C.RW[0:1, 2560:2944]
            em.op("vector", lambda E, o=mu, a=srow: E.tensor_scalar_mul(o, a, 1.0 / D),
                  inc="dv")
            em.op("vector", lambda E, o=sqrow, a=sqrow: E.tensor_scalar_mul(o, a, 1.0 / D),
                  inc="dv")
            em.op("vector", lambda E, o=srow, a=mu, b=mu: E.tensor_mul(o, a, b), inc="dv")
            em.op("vector", lambda E, o=sqrow, a=sqrow, b=srow: E.tensor_sub(o, a, b),
                  inc="dv")
            em.op("vector", lambda E, o=sqrow, a=sqrow: E.tensor_scalar_add(o, a, EPS),
                  inc="dv")
            em.wait("scalar", "dv")
            em.op("scalar", lambda E, o=sqrow, a=sqrow: E.activation(o, a, AF.Ln),
                  inc="ac")
            em.wait("vector", "ac")
            em.op("vector", lambda E, o=sqrow, a=sqrow: E.tensor_scalar_mul(o, a, -0.5),
                  inc="dv")
            em.wait("scalar", "dv")
            em.op("scalar", lambda E, o=sqrow, a=sqrow: E.activation(o, a, AF.Exp),
                  inc="ac")
            em.wait("vector", "ac")
            em.op("vector", lambda E, o=mu, a=mu, b=sqrow: E.tensor_mul(o, a, b), inc="dv")
            em.op("vector", lambda E, o=mu, a=mu: E.tensor_scalar_mul(o, a, -1.0), inc="dv")
            rowsdv = em.count("dv")
            em.wait("tensor", "dv", rowsdv)
            for k2, row in ((0, sqrow), (1, mu)):
                pb = 4 + 2 * qk + k2
                C.wait_free("tensor", ("bank", pb))
                em.op("tensor",
                      (lambda E, o=C.bank(pb), l=C.ones_row_f[0:1, 0:128], r=row:
                       E.matmul(o, l, r, start=True, stop=True)), inc="mm")
            em.wait("vector", "mm")
            for k2 in range(2):
                pb = 4 + 2 * qk + k2
                em.op("vector",
                      (lambda E, o=C.ST[:, 1024 + 1024 * qk + 512 * k2:1024 + 1024 * qk + 512 * (k2 + 1)],
                       a=C.bank(pb): E.tensor_copy(o, a)), inc="dv")
                C.set_free(("bank", pb), "dv", em.count("dv"))
        C.set_free(("strow",), "dv", em.count("dv"))
        repdv = em.count("dv")
        for qk in range(2):
            raws = qrs if qk == 0 else krs
            ns = qns if qk == 0 else kns
            wcol = C.SM[:, 28:32] if qk == 0 else C.SM[:, 12:16]
            bcol = C.SM[:, 32:36] if qk == 0 else C.SM[:, 16:20]
            arep = C.ST[:, 1024 + 1024 * qk:1024 + 1024 * qk + 512]
            brep = C.ST[:, 1024 + 1024 * qk + 512:1024 + 1024 * qk + 1024]
            for ch in range(4):
                mi = ch % 2
                C.wait_free("gpsimd", ("mrs", mi))
                mrd = em.dma("gpsimd", C.SMB[:, 512 * mi:512 * (mi + 1)],
                             raws[:, ch, 512 * tblk:512 * (tblk + 1)], "m%d" % mi)
                em.wait("vector", "m%d" % mi, mrd)
                em.wait("vector", "dv", repdv)
                tmp = C.ST[:, 512 * mi:512 * (mi + 1)]
                C.wait_free("vector", ("tmp", mi))
                em.op("vector", (lambda E, o=tmp, a=C.SMB[:, 512 * mi:512 * (mi + 1)],
                                 b=arep: E.tensor_mul(o, a, b)), inc="dv")
                C.set_free(("mrs", mi), "dv", em.count("dv"))
                em.op("vector", (lambda E, o=tmp, a=tmp, b=brep: E.tensor_add(o, a, b)),
                      inc="dv")
                ni = 2 + mi
                C.wait_free("vector", ("nsl", ni))
                em.op("vector",
                      (lambda E, o=C.SMB[:, 512 * ni:512 * (ni + 1)], a=tmp,
                       w=wcol[:, ch:ch + 1], bb=bcol[:, ch:ch + 1]:
                       E.tensor_scalar(o, a, w, bb, ALU.mult, ALU.add)), inc="dv")
                C.set_free(("tmp", mi), "dv", em.count("dv"))
                em.wait("sync", "dv")
                nc_ = em.dma("sync", ns[:, ch, 512 * tblk:512 * (tblk + 1)],
                             C.SMB[:, 512 * ni:512 * (ni + 1)], "n%d" % mi)
                C.set_free(("nsl", ni), "n%d" % mi, nc_)
    norm_dv = em.count("dv")

    kts = io["kn"].rearrange("(h p) t -> p h t", p=128)
    for s_ in ("n0", "n1"):
        em.wait("sync", s_)
    for h in range(HPC):
        C.wait_free("sync", ("hbuf",))
        em.dma("sync", C.A0[:, 0:8192], kts[:, h, :], "k0")
        em.dma("sync",
               C.A0[:, 8192:16384].rearrange("p (ch d) -> p ch d", d=128),
               vts[:, :, 128 * h:128 * (h + 1)], "k0")
        krd = em.dma("sync", C.A0[:, 16384:24576], qns[:, h, :], "k0")
        C.wait_free("sync", ("strips",))
        srd = em.dma("sync",
                     C.ST[:, 0:2560].rearrange("p (v j) -> p v j", j=512),
                     io["l1_strips"].rearrange("(h v p) j -> p h v j", p=128, v=5)[:, h, :, :],
                     "k0")
        em.wait("vector", "k0", srd)
        em.op("vector", (lambda E, o=C.SM[:, 24:28], a=C.SM[:, 20:24],
                         sl=C.SM[:, h:h + 1]:
                         E.tensor_scalar(o, a, sl, None, ALU.mult)), inc="dv")
        em.wait("tensor", "k0", srd)
        kT = C.A0[:, 0:8192]
        vT = C.A0[:, 8192:16384]
        if h == 0:
            for r_ in range(4):
                C.set_free(("ring", r_), "mm", qkv_mm)
        rhs_q0 = C.A0[:, 16384:16896]
        for jc in range(4):
            C.wait_free("tensor", ("bank", jc))
            em.op("tensor",
                  (lambda E, o=C.bank(jc), l=kT[:, 128 * jc:128 * (jc + 1)], r=rhs_q0:
                   E.matmul(o, l, r, start=True, stop=True)), inc="mm")
        dmm = em.count("mm")
        em.wait("vector", "mm", dmm)
        for jc in range(4):
            em.op("vector",
                  (lambda E, o=C.ST[:, 2560 + 512 * jc:2560 + 512 * (jc + 1)],
                   a=C.bank(jc), s=C.SM[:, 24 + jc:25 + jc],
                   strip=C.ST[:, 512 * (1 + jc):512 * (2 + jc)]:
                   E.scalar_tensor_tensor(o, a, s, strip, ALU.add, ALU.add)),
                  inc="dv")
            C.set_free(("bank", jc), "dv", em.count("dv"))
        em.op("vector", (lambda E, o=C.ST[:, 5188:5189], a=C.ST[:, 2560:4608]:
                         E.reduce_max(o, a, axis=AX.X)), inc="dv")
        em.wait("tensor", "dv")
        em.op("tensor",
              (lambda E, o=C.PS[0:1, 2048:2176], a=C.ST[:, 5188:5189],
               ii=C.ident_f[:, :]: E.transpose(o, a, ii)), inc="mm")
        em.wait("vector", "mm")
        em.op("vector", (lambda E, o=C.RW[0:1, 0:1], a=C.PS[0:1, 2048:2176]:
                         E.reduce_max(o, a, axis=AX.X)), inc="dv")
        for b in range(16):
            nt = 4 * b + 4
            rhs_q = C.A0[:, 16384 + 512 * b:16384 + 512 * (b + 1)]
            em.op("vector", (lambda E, o=C.RW[0:1, 2:3], a=C.SM[0:1, h:h + 1], b_=b:
                             E.tensor_scalar_mul(o, a, 512.0 * b_)), inc="dv")
            em.op("vector", (lambda E, o=C.RW[0:1, 2:3], a=C.RW[0:1, 2:3],
                             b=C.RW[0:1, 0:1]: E.tensor_add(o, a, b)), inc="dv")
            em.op("vector",
                  (lambda E, o=C.RW[0:1, 4:4 + nt], a=C.RW[0:1, 2944:2944 + nt],
                   s1=C.SM[0:1, h:h + 1], s2=C.RW[0:1, 2:3], n_=nt:
                   E.tensor_scalar(o, a, s1, s2, ALU.mult, ALU.subtract)), inc="dv")
            bdv = em.count("dv")
            em.wait("tensor", "dv", bdv)
            em.op("tensor",
                  (lambda E, o=C.PS[:, 2048:2048 + nt], l=C.ones_row_f[0:1, 0:128],
                   r=C.RW[0:1, 4:4 + nt], n_=nt: E.matmul(o, l, r, start=True, stop=True)),
                  inc="mm")
            em.wait("vector", "mm")
            em.op("vector", (lambda E, o=C.ST[:, 4608:4608 + nt], a=C.PS[:, 2048:2048 + nt]:
                             E.tensor_copy(o, a)), inc="dv")
            pend = []

            def flush_l1(rec, vT=vT, b=b):
                rs_, eac_, t0_ = rec
                em.wait("tensor", "ac", eac_)
                if t0_ == 0:
                    C.wait_free("tensor", ("sbank",))
                    C.wait_free("tensor", ("cbank",))
                for k2 in range(2):
                    t = t0_ + k2
                    pr = C.A1[:, 1024 * rs_ + 512 * k2:1024 * rs_ + 512 * (k2 + 1)]
                    st_ = t == 0
                    sp_ = t == 4 * b + 3
                    em.op("tensor",
                          (lambda E, o=C.PS[0:1, 3072:3584], l=C.ones_col_b[:, :],
                           r=pr, s0=st_, s1=sp_: E.matmul(o, l, r, start=s0, stop=s1)))
                    em.op("tensor",
                          (lambda E, o=C.PS[:, 3584:4096],
                           l=vT[:, 128 * t:128 * (t + 1)], r=pr, s0=st_, s1=sp_:
                           E.matmul(o, l, r, start=s0, stop=s1)),
                          inc=("mm" if k2 == 1 else None))
                C.set_free(("ring", rs_), "mm", em.count("mm"))

            for p_ in range(nt // 2):
                t0 = 2 * p_
                rs = p_ % 4
                for k2, t in enumerate((t0, t0 + 1)):
                    C.wait_free("tensor", ("bank", k2))
                    em.op("tensor",
                          (lambda E, o=C.bank(k2), l=kT[:, 128 * t:128 * (t + 1)],
                           r=rhs_q: E.matmul(o, l, r, start=True, stop=True)), inc="mm")
                qkmm = em.count("mm")
                em.wait("vector", "mm", qkmm)
                for k2, t in enumerate((t0, t0 + 1)):
                    v_ = 0 if t < 4 * b else 1 + (t - 4 * b)
                    C.wait_free("vector", ("bank", 2 + k2))
                    em.op("vector",
                          (lambda E, o=C.bank(2 + k2), a=C.bank(k2),
                           s=C.ST[:, 4608 + t:4609 + t],
                           strip=C.ST[:, 512 * v_:512 * (v_ + 1)]:
                           E.scalar_tensor_tensor(o, a, s, strip, ALU.add, ALU.add)),
                          inc="dv")
                    C.set_free(("bank", k2), "dv", em.count("dv"))
                sttdv = em.count("dv")
                em.wait("scalar", "dv", sttdv)
                C.wait_free("scalar", ("ring", rs))
                em.op("scalar",
                      (lambda E, o=C.A1[:, 1024 * rs:1024 * (rs + 1)],
                       a=C.PS[:, 1024:2048]:
                       E.activation(o, a, AF.Exp)), inc="ac", noser=True)
                eac = em.count("ac")
                C.set_free(("bank", 2), "ac", eac)
                C.set_free(("bank", 3), "ac", eac)
                pend.append((rs, eac, t0))
                if len(pend) > 2:
                    flush_l1(pend.pop(0))
            while pend:
                flush_l1(pend.pop(0))
            pvmm = em.count("mm")
            em.wait("scalar", "mm", pvmm)
            em.op("scalar", (lambda E, o=C.RW[0:1, 512:1024], a=C.PS[0:1, 3072:3584]:
                             E.activation(o, a, AF.Ln)), inc="ac")
            em.wait("vector", "ac")
            em.op("vector", (lambda E, o=C.RW[0:1, 512:1024], a=C.RW[0:1, 512:1024]:
                             E.tensor_scalar_mul(o, a, -1.0)), inc="dv")
            em.wait("scalar", "dv")
            em.op("scalar", (lambda E, o=C.RW[0:1, 1024:1536], a=C.RW[0:1, 512:1024]:
                             E.activation(o, a, AF.Exp)), inc="ac")
            em.wait("tensor", "ac")
            em.op("tensor",
                  (lambda E, o=C.PS[:, 2560:3072], l=C.ones_row_f[0:1, 0:128],
                   r=C.RW[0:1, 1024:1536]: E.matmul(o, l, r, start=True, stop=True)),
                  inc="mm")
            em.wait("vector", "mm")
            em.op("vector", (lambda E, o=C.ST[:, 4676:5188], a=C.PS[:, 2560:3072]:
                             E.tensor_copy(o, a)), inc="dv")
            C.wait_free("vector", ("ctxh",))
            em.op("vector",
                  (lambda E, o=C.A0[:, 24576 + 512 * b:24576 + 512 * (b + 1)],
                   a=C.PS[:, 3584:4096], bx=C.ST[:, 4676:5188]:
                   E.tensor_mul(o, a, bx)), inc="dv")
            C.set_free(("sbank",), "ac", em.count("ac"))
            C.set_free(("cbank",), "dv", em.count("dv"))
        em.wait("sync", "dv")
        cgc = em.dma("sync",
                     io["ctxg_in"].rearrange("(h p) t -> p h t", p=128)[:, h, :],
                     C.A0[:, 24576:32768], "cg")
        C.set_free(("ctxh",), "cg", cgc)
        C.set_free(("hbuf",), "mm", em.count("mm"))
        C.set_free(("strips",), "dv", em.count("dv"))
    attn_mm = em.count("mm")
    attn_ac = em.count("ac")
    attn_dv = em.count("dv")

    em.wait("gpsimd", "cg")
    em.op("gpsimd",
          (lambda E: E.collective_compute(
              "AllGather", ALU.bypass, replica_groups=RG,
              ins=[io["ctxg_in"].opt()], outs=[io["ctxg"].opt()])), inc="cc")
    cc_c = em.count("cc")

    em.wait("gpsimd", "cc", cc_c)
    em.wait("gpsimd", "mm", attn_mm)
    cgr = io["ctxg"].rearrange("(kc p) t -> p kc t", p=128)
    clc = em.op(
        "gpsimd",
        (lambda E: E.dma_start(
            out=C.A1[:, :].rearrange("p (kc t) -> p kc t", t=1024),
            in_=cgr[:, :, bass.ts(E.partition_id(), 1024)])),
        inc="cl", amt=16)

    for i in range(4):
        C.set_free(("xsl", i), "dv", attn_dv)
        C.set_free(("rsl", i), "dv", attn_dv)
    em.wait("tensor", "cl", clc)

    def ctx_rhs(kc, tb):
        return C.A1[:, kc * 1024 + tb * 512:kc * 1024 + (tb + 1) * 512]

    emit_proj(C, "wo1", io["l1_woT"], KC, 128, ctx_rhs, ep.chunk_cb)
    return ep.finish()


def build_l1attn_stage():
    nc = bass.Bass("TRN2", target_bir_lowering=False, debug=False, num_devices=8)
    io = {}
    io["x2f"] = _decl(nc, "x2f", [128, KC * T], F32, "ExternalInput")
    io["l1_qkvT"] = _decl(nc, "l1_qkvT", [12 * 128, KC * 128], F32, "ExternalInput")
    for nm in ("l1_qw", "l1_qb", "l1_kw", "l1_kb"):
        io[nm] = _decl(nc, nm, [128, 4], F32, "ExternalInput")
    io["l1_slopes"] = _decl(nc, "l1_slopes", [128, 4], F32, "ExternalInput")
    io["l1_strips"] = _decl(nc, "l1_strips", [4 * 5 * 128, 512], F32, "ExternalInput")
    io["l1_woT"] = _decl(nc, "l1_woT", [KC * 128, KC * 128], F32, "ExternalInput")
    io["ident"] = _decl(nc, "ident", [128, 128], F32, "ExternalInput")
    io["jcv"] = _decl(nc, "jcv", [128, 4], F32, "ExternalInput")
    io["posr"] = _decl(nc, "posr", [1, 68], F32, "ExternalInput")
    io["xgat_in"] = _decl(nc, "xgat_in", [D, T], BF16, "Internal")
    io["xgat"] = _decl(nc, "xgat", [NCORE * D, T], BF16, "Internal")
    io["qraw"] = _decl(nc, "qraw", [512, S], BF16, "Internal")
    io["kraw"] = _decl(nc, "kraw", [512, S], BF16, "Internal")
    io["vtok1"] = _decl(nc, "vtok1", [S, 512], BF16, "Internal")
    io["qn"] = _decl(nc, "qn", [512, S], BF16, "Internal")
    io["kn"] = _decl(nc, "kn", [512, S], BF16, "Internal")
    io["sqk_in"] = _decl(nc, "sqk_in", [4, S], F32, "Internal")
    io["sqk"] = _decl(nc, "sqk", [4, S], F32, "Internal")
    io["ctxg_in"] = _decl(nc, "ctxg_in", [512, S], BF16, "Internal")
    io["ctxg"] = _decl(nc, "ctxg", [D, S], BF16, "Internal")
    io["r3"] = _decl(nc, "r3", [128, KC * T], F32, "Internal")
    io["x3f"] = _decl(nc, "x3f", [128, KC * T], F32, "ExternalOutput")
    with ExitStack() as ctx:
        em = Em(nc)
        C = Ctx(nc, em, ctx)
        setup_consts(C, io)
        em.dma("gpsimd", C.A1[:, 0:KC * 1024], io["x2f"][:, :], "c0g")
        em.wait_fin("sync", "c0g")
        ep = Ep(C, "e3", io["x2f"], io["r3"], io["x3f"], C.A0)
        emit_l1attn(C, io, ep)
        for i in range(4):
            em.wait("sync", "r%d" % i)
        em.finalize(ctx)
    return nc


def emit_l1ffn(C, io, bprev, ep):
    em = C.em

    def xr(kc, tb):
        return bprev[:, kc * 1024 + tb * 512:kc * 1024 + (tb + 1) * 512]

    la_rdy = em.dma("gpsimd", C.A1[:, 0:4096], io["l1_laT"][:, :], "lw1")
    lb_rdy = em.dma("gpsimd", C.A1[:, 4096:8192], io["l1_lbT"][:, :], "lw1")
    em.dma("sync", C.SM[0:8, 36:37], io["l1_lw"][:, :], "c4")
    em.dma("sync", C.SM[0:8, 80:208], io["mexp"][:, :], "c4")
    em.dma("sync", C.SM[:, 48:80], io["l1_bias"][:, :], "c4")
    em.wait("tensor", "c4")
    em.wait("tensor", "lw1", lb_rdy)

    C.wait_free("tensor", ("bank", 6))
    em.op("tensor",
          (lambda E, o=C.PS[:, 512 * 6:512 * 6 + 1], l=C.SM[0:8, 80:208],
           r=C.SM[0:8, 36:37]: E.matmul(o, l, r, start=True, stop=True)), inc="mm")
    em.wait("vector", "mm")
    em.op("vector", (lambda E, o=C.SM[:, 40:41], a=C.PS[:, 512 * 6:512 * 6 + 1]:
                     E.tensor_scalar_mul(o, a, 2.0)), inc="dv")
    C.set_free(("bank", 6), "dv", em.count("dv"))

    for tb in (0, 1):
        pb = 4 + tb
        C.wait_free("tensor", ("bank", pb))
        ps = C.bank(pb)
        for kc in range(KC):
            em.op("tensor",
                  (lambda E, o=ps, l=C.A1[:, kc * 128:(kc + 1) * 128],
                   r=xr(kc, tb), st=(kc == 0), sp=(kc == KC - 1):
                   E.matmul(o, l, r, start=st, stop=sp)),
                  inc=("mm" if kc == KC - 1 else None))
        em.wait("vector", "mm")
        em.op("vector",
              (lambda E, o=C.RW[:, :].bitcast(BF16)[:, 4864 + 512 * tb:4864 + 512 * (tb + 1)],
               a=ps, s=C.SM[:, 40:41]: E.tensor_scalar(o, a, s, None, ALU.mult)),
              inc="dv")
        C.set_free(("bank", pb), "dv", em.count("dv"))
    low_dv = em.count("dv")

    em.wait("tensor", "dv", low_dv)

    def extra(jb, tb):
        return [(C.A1[:, 4096 + jb * 128:4096 + (jb + 1) * 128],
                 C.RW[:, :].bitcast(BF16)[:, 4864 + 512 * tb:4864 + 512 * (tb + 1)])]

    emit_proj(C, "f1", io["l1_wT"], KC, 128, xr, ep.chunk_cb, extra=extra)
    return ep.finish()


def build_l1ffn_stage():
    nc = bass.Bass("TRN2", target_bir_lowering=False, debug=False, num_devices=8)
    io = {}
    io["x3f"] = _decl(nc, "x3f", [128, KC * T], F32, "ExternalInput")
    io["l1_wT"] = _decl(nc, "l1_wT", [KC * 128, KC * 128], F32, "ExternalInput")
    io["l1_bias"] = _decl(nc, "l1_bias", [128, 32], F32, "ExternalInput")
    io["l1_laT"] = _decl(nc, "l1_laT", [128, KC * 128], F32, "ExternalInput")
    io["l1_lbT"] = _decl(nc, "l1_lbT", [128, D], F32, "ExternalInput")
    io["l1_lw"] = _decl(nc, "l1_lw", [8, 1], F32, "ExternalInput")
    io["mexp"] = _decl(nc, "mexp", [8, 128], F32, "ExternalInput")
    io["ident"] = _decl(nc, "ident", [128, 128], F32, "ExternalInput")
    io["r4"] = _decl(nc, "r4", [128, KC * T], F32, "Internal")
    io["y"] = _decl(nc, "y", [128, KC * T], F32, "ExternalOutput")
    with ExitStack() as ctx:
        em = Em(nc)
        C = Ctx(nc, em, ctx)
        setup_consts(C, io)
        em.dma("gpsimd", C.A0[:, 0:KC * 1024], io["x3f"][:, :], "c0g")
        em.wait_fin("tensor", "c0g")
        ep = Ep(C, "e4", io["x3f"], io["r4"], io["y"], None,
                bias_cols=C.SM[:, 48:80])
        emit_l1ffn(C, io, C.A0, ep)
        for i in range(4):
            em.wait("sync", "r%d" % i)
        em.finalize(ctx)
    return nc


def build_fused():
    nc = bass.Bass("TRN2", target_bir_lowering=False, debug=False, num_devices=8)
    io = {}
    EI = lambda n, s, d=F32: _decl(nc, n, s, d, "ExternalInput")
    IN = lambda n, s, d=F32: _decl(nc, n, s, d, "Internal")
    io["x_in"] = EI("x_in", [128, KC * T])
    io["ident"] = EI("ident", [128, 128])
    io["mexp"] = EI("mexp", [8, 128])
    io["jcv"] = EI("jcv", [128, 4])
    io["posr"] = EI("posr", [1, 68])
    io["l0_qkvT"] = EI("l0_qkvT", [48 * 128, KC * 128])
    io["l0_bv"] = EI("l0_bv", [1, 1024])
    io["l0_woT"] = EI("l0_woT", [KC * 128, KC * 128])
    io["l0_wrT"] = EI("l0_wrT", [128, 256])
    io["l0_br"] = EI("l0_br", [1, 8])
    io["l0_aT"] = EI("l0_aT", [128, KC * 128])
    io["l0_bT"] = EI("l0_bT", [128, D])
    io["l1_qkvT"] = EI("l1_qkvT", [12 * 128, KC * 128])
    for nm in ("l1_qw", "l1_qb", "l1_kw", "l1_kb", "l1_slopes"):
        io[nm] = EI(nm, [128, 4])
    io["l1_strips"] = EI("l1_strips", [4 * 5 * 128, 512])
    io["l1_woT"] = EI("l1_woT", [KC * 128, KC * 128])
    io["l1_wT"] = EI("l1_wT", [KC * 128, KC * 128])
    io["l1_bias"] = EI("l1_bias", [128, 32])
    io["l1_laT"] = EI("l1_laT", [128, KC * 128])
    io["l1_lbT"] = EI("l1_lbT", [128, D])
    io["l1_lw"] = EI("l1_lw", [8, 1])
    io["qspill"] = IN("qspill", [D, T], BF16)
    io["kgat_in"] = IN("kgat_in", [1024, T], BF16)
    io["kgat"] = IN("kgat", [S, T], BF16)
    io["vgat_in"] = IN("vgat_in", [T, 1024], BF16)
    io["vgat"] = IN("vgat", [S, 1024], BF16)
    io["r1"] = IN("r1", [128, KC * T]); io["x1f"] = IN("x1f", [128, KC * T])
    io["r2"] = IN("r2", [128, KC * T]); io["x2f"] = IN("x2f", [128, KC * T])
    io["r3"] = IN("r3", [128, KC * T]); io["x3f"] = IN("x3f", [128, KC * T])
    io["r4"] = IN("r4", [128, KC * T])
    io["xgat_in"] = IN("xgat_in", [D, T], BF16)
    io["xgat"] = IN("xgat", [NCORE * D, T], BF16)
    io["qraw"] = IN("qraw", [512, S], BF16)
    io["kraw"] = IN("kraw", [512, S], BF16)
    io["vtok1"] = IN("vtok1", [S, 512], BF16)
    io["qn"] = IN("qn", [512, S], BF16)
    io["kn"] = IN("kn", [512, S], BF16)
    io["sqk_in"] = IN("sqk_in", [4, S])
    io["sqk"] = IN("sqk", [4, S])
    io["ctxg_in"] = IN("ctxg_in", [512, S], BF16)
    io["ctxg"] = IN("ctxg", [D, S], BF16)
    io["y"] = _decl(nc, "y", [128, KC * T], F32, "ExternalOutput")

    with ExitStack() as ctx:
        em = Em(nc)
        C = Ctx(nc, em, ctx)
        setup_consts(C, io)
        em.dma("gpsimd", C.A0[:, 0:KC * 1024], io["x_in"][:, :], "c0g")
        em.wait_fin("tensor", "c0g")

        def barrier(done):
            for eng in ("tensor", "vector", "scalar", "gpsimd", "sync"):
                em.wait(eng, "dv", done)
            for i in range(4):
                em.wait("sync", "r%d" % i)

        d1 = emit_l0attn_with_wo(C, io)
        barrier(d1[1])
        ep2 = Ep(C, "e2", io["x1f"], io["r2"], io["x2f"], C.A1)
        d2 = emit_l0ffn(C, io, C.A0, ep2)
        barrier(d2[1])
        ep3 = Ep(C, "e3", io["x2f"], io["r3"], io["x3f"], C.A0)
        d3 = emit_l1attn(C, io, ep3)
        barrier(d3[1])
        ep4 = Ep(C, "e4", io["x3f"], io["r4"], io["y"], None,
                 bias_cols=C.SM[:, 48:80])
        emit_l1ffn(C, io, C.A0, ep4)
        for i in range(4):
            em.wait("sync", "r%d" % i)
        em.finalize(ctx)
    return nc


def kernel_fused(**inputs):
    sh, pc = host_prep(inputs)
    cores = list(range(NCORE))
    nc = _get_nc("fused", build_fused)
    maps = []
    for c in cores:
        m = {}
        for nm in ("ident", "mexp", "jcv", "posr", "l0_qkvT", "l0_bv", "l0_woT",
                   "l0_wrT", "l0_br", "l0_aT", "l0_bT", "l1_woT", "l1_wT",
                   "l1_bias", "l1_laT", "l1_lbT", "l1_lw"):
            m[nm] = sh[nm]
        for nm in ("x_in", "l1_qkvT", "l1_qw", "l1_qb", "l1_kw", "l1_kb",
                   "l1_slopes", "l1_strips"):
            m[nm] = pc[c][nm]
        maps.append(m)
    res = run_bass_kernel_spmd(nc, maps, core_ids=cores)
    y = np.concatenate(
        [np.asarray(res.results[c]["y"]).reshape(128, KC, T)
         .transpose(1, 0, 2).reshape(D, T).T for c in cores], axis=0)
    return y[None, :, :].astype(np.float32)


# version 12
# speedup vs baseline: 1.2029x; 1.0031x over previous
import numpy as np
import ml_dtypes
from contextlib import ExitStack

import concourse.bass as bass
import concourse.mybir as mybir
from concourse.bass_utils import run_bass_kernel_spmd

F32 = mybir.dt.float32
BF16 = mybir.dt.bfloat16
AF = mybir.ActivationFunctionType
ALU = mybir.AluOpType
AX = mybir.AxisListType

NCORE = 8
S = 8192
D = 4096
T = S // NCORE
KC = D // 128
TB = T // 512
EPS = 1e-5
NEG = -1.0e9

L0_NQ, L0_NKV, HD = 32, 8, 128
ATT_SCALE = HD ** -0.5
L1_NH = 32
HPC = L1_NH // NCORE


AUTO_SERIAL = True


class Em:
    ENGS = ("tensor", "vector", "scalar", "gpsimd", "sync")

    def __init__(self, nc):
        self.nc = nc
        self.streams = {e: [] for e in self.ENGS}
        self.counts = {}

    def count(self, name):
        return self.counts.get(name, 0)

    def wait(self, eng, name, val=None):
        self.counts.setdefault(name, 0)
        v = self.counts[name] if val is None else val
        if v > 0:
            self.streams[eng].append(("w", name, v))
        return v

    ENG_SEM = {"vector": "dv", "scalar": "ac"}

    def op(self, eng, fn, inc=None, amt=1, noser=False):
        selfwait = None
        if eng in self.ENG_SEM:
            es = self.ENG_SEM[eng]
            assert inc is None or inc == es, (eng, inc)
            inc = es
            amt = 1
            self.counts.setdefault(es, 0)
            if AUTO_SERIAL and not noser and self.counts[es] > 0:
                selfwait = (es, self.counts[es])
        if inc is not None:
            self.counts.setdefault(inc, 0)
            self.counts[inc] += amt
        self.streams[eng].append(("o", fn, inc, amt, selfwait))
        return self.counts[inc] if inc is not None else None

    def wait_fin(self, eng, name):
        self.counts.setdefault(name, 0)
        self.streams[eng].append(("wf", name))

    def gop(self, fn):
        prev = self.counts.get("gp", 0)
        selfwait = ("gp", prev) if prev > 0 else None
        self.counts["gp"] = prev + 1
        self.streams["gpsimd"].append(("o", fn, "gp", 1, selfwait))
        return self.counts["gp"]

    def dma(self, eng, out, in_, inc, **kw):
        selfwait = None
        self.counts.setdefault(inc, 0)
        self.counts[inc] += 16
        self.streams[eng].append(
            ("o", (lambda E, o=out, i=in_, k=dict(kw): E.dma_start(out=o, in_=i, **k)),
             inc, 16, selfwait))
        return self.counts[inc]

    def finalize(self, ctx):
        handles = {n: ctx.enter_context(self.nc.semaphore(n)) for n in self.counts}
        block = ctx.enter_context(self.nc.Block())
        for eng in self.ENGS:
            items = self.streams[eng]
            if not items:
                continue

            def body(E, items=items):
                for it in items:
                    if it[0] == "w":
                        E.wait_ge(handles[it[1]], it[2])
                    elif it[0] == "wf":
                        if self.counts[it[1]] > 0:
                            E.wait_ge(handles[it[1]], self.counts[it[1]])
                    else:
                        _, fn, inc, amt, selfwait = it
                        if selfwait is not None:
                            E.wait_ge(handles[selfwait[0]], selfwait[1])
                        ins = fn(E)
                        if inc is not None:
                            ins.then_inc(handles[inc], amt)

            getattr(block, eng)(body)


class Ctx:

    def __init__(self, nc, em, ctx):
        self.nc = nc
        self.em = em
        self.free = {}
        E = ctx.enter_context
        self.A0 = E(nc.sbuf_tensor("A0", [128, 32768], BF16))
        self.A1 = E(nc.sbuf_tensor("A1", [128, 32768], BF16))
        self.ST = E(nc.sbuf_tensor("ST", [128, 6144], F32))
        self.WS = E(nc.sbuf_tensor("WS", [128, 8192], BF16))
        self.RW = E(nc.sbuf_tensor("RW", [128, 3072], F32))
        self.ident_f = E(nc.sbuf_tensor("ident_f", [128, 128], F32))
        self.ident_b = E(nc.sbuf_tensor("ident_b", [128, 128], BF16))
        self.ones_col_b = E(nc.sbuf_tensor("ones_col_b", [128, 1], BF16))
        self.ones_col_f = E(nc.sbuf_tensor("ones_col_f", [128, 1], F32))
        self.ones_row_b = E(nc.sbuf_tensor("ones_row_b", [1, 512], BF16))
        self.ones_row_f = E(nc.sbuf_tensor("ones_row_f", [1, 512], F32))
        self.SM = E(nc.sbuf_tensor("SM", [128, 1024], F32))
        self.SMB = E(nc.sbuf_tensor("SMB", [128, 2048], BF16))
        self.PS = E(nc.psum_tensor("PS", [128, 4096], F32))

    def bank(self, i, n=1):
        return self.PS[:, 512 * i:512 * (i + n)]

    def bank_bf(self, i, n=1):
        return self.PS[:, :].bitcast(BF16)[:, 1024 * i:1024 * (i + n)]

    def stb(self):
        return self.ST[:, :].bitcast(BF16)

    def wait_free(self, eng, key):
        fr = self.free.get(key)
        if fr:
            self.em.wait(eng, fr[0], fr[1])

    def set_free(self, key, sem, cnt):
        self.free[key] = (sem, cnt)


def emit_proj(C, tag, wt, njb, jw, x_rhs, out_cb, extra=None, nkc=KC,
              banks=(0, 1, 2, 3), tbs=(0, 1), jb_order=None):
    em = C.em
    nc = C.nc
    wsrc = wt.rearrange("(jb p) f -> p jb f", p=128)
    wready = {}
    order = list(jb_order) if jb_order is not None else list(range(njb))
    slot_of = {jb: idx % 2 for idx, jb in enumerate(order)}

    def issue_w(jb):
        if jb is None:
            return
        s = slot_of[jb]
        C.wait_free("gpsimd", ("WS", s))
        dst = C.WS[:, 4096 * s:4096 * s + nkc * jw]
        wready[jb] = em.dma("gpsimd", dst, wsrc[:, jb, :], "w%d" % s)

    issue_w(order[0])
    issue_w(order[1])
    for idx, jb in enumerate(order):
        s = slot_of[jb]
        em.wait("tensor", "w%d" % s, wready[jb])
        for ti, tb in enumerate(tbs):
            pb = banks[(idx % 2) * 2 + ti]
            C.wait_free("tensor", ("bank", pb))
            ps = C.PS[0:jw, 512 * pb:512 * pb + 512]
            ex = extra(jb, tb) if extra is not None else []
            for kc in range(nkc):
                last = (kc == nkc - 1) and not ex
                em.op("tensor",
                      (lambda E, o=ps, l=C.WS[:, 4096 * s + kc * jw:4096 * s + (kc + 1) * jw],
                       r=x_rhs(kc, tb), st=(kc == 0), sp=last:
                       E.matmul(o, l, r, start=st, stop=sp)),
                      inc=("mm" if last else None))
            for i, (lt, rr) in enumerate(ex):
                last = i == len(ex) - 1
                em.op("tensor",
                      (lambda E, o=ps, l=lt, r=rr, sp=last:
                       E.matmul(o, l, r, start=False, stop=sp)),
                      inc=("mm" if last else None))
            mmcnt = em.count("mm")
            fsem, fcnt = out_cb(jb, tb, ps, mmcnt)
            C.set_free(("bank", pb), fsem, fcnt)
        C.set_free(("WS", s), "mm", em.count("mm"))
        issue_w(order[idx + 2] if idx + 2 < njb else None)


class Ep:

    def __init__(self, C, tag, xprev, r_out, xf_out, bbuf, bias_cols=None):
        self.C = C
        self.tag = tag
        self.xprev = xprev
        self.r_out = r_out
        self.xf_out = xf_out
        self.bbuf = bbuf
        self.bias_cols = bias_cols
        self.pending = []
        self.first = {}

    def chunk_cb(self, jb, tb, ps, mmcnt):
        C, em = self.C, self.C.em
        tag = self.tag
        i = (jb * 2 + tb) % 4
        xsl = C.ST[:, 512 * i:512 * (i + 1)]
        rsl = C.ST[:, 2048 + 512 * i:2048 + 512 * (i + 1)]
        rbf = C.SMB[:, 1024 * (i % 2):1024 * (i % 2) + 512]
        sq = C.SMB[:, 1024 * (i % 2) + 512:1024 * (i % 2) + 1024]
        C.wait_free("sync", ("xsl", i))
        off = jb * 1024 + 512 * tb
        xc = em.dma("sync", xsl, self.xprev[:, off:off + 512], "x%d" % i)
        em.wait("vector", "mm", mmcnt)
        em.wait("vector", "x%d" % i, xc)
        C.wait_free("vector", ("rsl", i))
        C.wait_free("vector", ("rbf", i % 2))
        if self.bias_cols is not None:
            em.op("vector",
                  (lambda E, o=rsl, a=ps, b=xsl, s=self.bias_cols[:, jb:jb + 1]:
                   E.scalar_tensor_tensor(o, a, s, b, ALU.add, ALU.add)), inc="dv")
        else:
            em.op("vector", lambda E, o=rsl, a=ps, b=xsl: E.tensor_add(o, a, b), inc="dv")
        bankfree = ("dv", em.count("dv"))
        C.set_free(("xsl", i), "dv", em.count("dv"))
        em.op("vector", lambda E, o=rbf, a=rsl: E.tensor_copy(o, a), inc="dv")
        em.op("vector", lambda E, o=sq, a=rbf: E.tensor_mul(o, a, a), inc="dv")
        dvc = em.count("dv")
        em.wait("sync", "dv", dvc - 2)
        rs = em.dma("sync", self.r_out[:, off:off + 512], rsl, "r%d" % i)
        C.set_free(("rsl", i), "r%d" % i, rs)
        self.pending.append((jb, tb, rbf, sq, dvc, i % 2))
        if len(self.pending) > 1:
            self._flush_stat(self.pending.pop(0))
        return bankfree

    def _flush_stat(self, rec):
        C, em = self.C, self.C.em
        jb, tb, rbf, sq, dvc, ri = rec
        em.wait("tensor", "dv", dvc)
        st = (jb == 0)
        sp = (jb == KC - 1)
        em.op("tensor",
              (lambda E, o=C.PS[0:1, 512 * (4 + tb):512 * (4 + tb) + 512], l=C.ones_col_b[:, :],
               r=rbf, s0=st, s1=sp: E.matmul(o, l, r, start=s0, stop=s1)),
              inc=("mm" if sp else None))
        em.op("tensor",
              (lambda E, o=C.PS[0:1, 512 * (6 + tb):512 * (6 + tb) + 512], l=C.ones_col_b[:, :],
               r=sq, s0=st, s1=sp: E.matmul(o, l, r, start=s0, stop=s1)),
              inc=("mm" if sp else None))
        C.set_free(("rbf", ri), "mm", em.count("mm"))

    def finish(self):
        C, em = self.C, self.C.em
        tag = self.tag
        while self.pending:
            self._flush_stat(self.pending.pop(0))
        em.wait("vector", "mm")
        R = C.RW
        for tb in (0, 1):
            mu = R[0:1, 512 * 0:512 * 1]
            msq = R[0:1, 512 * 1:512 * 2]
            al = R[0:1, 512 * 2:512 * 3]
            be = R[0:1, 512 * 3:512 * 4]
            em.op("vector", lambda E, o=mu, a=C.PS[0:1, 512 * (4 + tb):512 * (4 + tb) + 512]:
                  E.tensor_scalar_mul(o, a, 1.0 / D), inc="dv")
            em.op("vector", lambda E, o=msq, a=C.PS[0:1, 512 * (6 + tb):512 * (6 + tb) + 512]:
                  E.tensor_scalar_mul(o, a, 1.0 / D), inc="dv")
            em.op("vector", lambda E, o=be, a=mu, b=mu: E.tensor_mul(o, a, b), inc="dv")
            em.op("vector", lambda E, o=msq, a=msq, b=be: E.tensor_sub(o, a, b), inc="dv")
            em.op("vector", lambda E, o=msq, a=msq: E.tensor_scalar_add(o, a, EPS), inc="dv")
            em.wait("scalar", "dv")
            em.op("scalar", lambda E, o=msq, a=msq: E.activation(o, a, AF.Ln), inc="ac")
            em.wait("vector", "ac")
            em.op("vector", lambda E, o=msq, a=msq: E.tensor_scalar_mul(o, a, -0.5), inc="dv")
            em.wait("scalar", "dv")
            em.op("scalar", lambda E, o=al, a=msq: E.activation(o, a, AF.Exp), inc="ac")
            em.wait("vector", "ac")
            em.op("vector", lambda E, o=be, a=mu, b=al: E.tensor_mul(o, a, b), inc="dv")
            em.op("vector", lambda E, o=be, a=be: E.tensor_scalar_mul(o, a, -1.0), inc="dv")
            em.wait("tensor", "dv")
            for k, row in ((0, al), (1, be)):
                pb = tb * 2 + k
                C.wait_free("tensor", ("bank", pb))
                em.op("tensor",
                      (lambda E, o=C.bank(pb), l=C.ones_row_f[0:1, 0:128], r=row:
                       E.matmul(o, l, r, start=True, stop=True)), inc="mm")
            em.wait("vector", "mm")
            for k in (0, 1):
                pb = tb * 2 + k
                dst = 4096 + 1024 * k + 512 * tb
                em.op("vector",
                      (lambda E, o=C.ST[:, dst:dst + 512],
                       a=C.bank(pb): E.tensor_copy(o, a)), inc="dv")
                C.set_free(("bank", pb), "dv", em.count("dv"))
        spbase = em.count("dv")
        for i in range(4):
            em.wait("sync", "r%d" % i)
            em.wait("gpsimd", "r%d" % i)
            C.wait_free("gpsimd", ("xsl", i))
            C.wait_free("vector", ("rsl", i))
        arep = C.ST[:, 4096:5120]
        brep = C.ST[:, 5120:6144]
        for jb in range(KC):
            i = jb % 2
            off = jb * 1024
            xsl = C.ST[:, 1024 * i:1024 * (i + 1)]
            rsl = C.ST[:, 2048 + 1024 * i:2048 + 1024 * (i + 1)]
            C.wait_free("gpsimd", ("xsw", i))
            rc = em.dma("gpsimd", xsl, self.r_out[:, off:off + 1024], "x%d" % i)
            em.wait("vector", "x%d" % i, rc)
            em.wait("vector", "dv", spbase)
            C.wait_free("vector", ("rsw", i))
            em.op("vector", lambda E, o=rsl, a=xsl, b=arep: E.tensor_mul(o, a, b), inc="dv")
            C.set_free(("xsw", i), "dv", em.count("dv"))
            em.op("vector", lambda E, o=rsl, a=rsl, b=brep: E.tensor_add(o, a, b), inc="dv")
            dvc = em.count("dv")
            em.wait("sync", "dv", dvc)
            rs = em.dma("sync", self.xf_out[:, off:off + 1024], rsl, "r%d" % i)
            C.set_free(("rsw", i), "r%d" % i, rs)
            if self.bbuf is not None:
                em.op("vector",
                      (lambda E, o=self.bbuf[:, off:off + 1024], a=rsl:
                       E.tensor_copy(o, a)), inc="dv")
        return ("dv", em.count("dv"))


def setup_consts(C, io):
    em = C.em
    em.dma("sync", C.ident_f[:, :], io["ident"][:, :], "c0")
    em.dma("gpsimd", C.ident_b[:, :], io["ident"][:, :], "c0g")
    em.op("vector", lambda E, t=C.ones_col_b: E.memset(t[:, :], 1.0), inc="dv")
    em.op("vector", lambda E, t=C.ones_col_f: E.memset(t[:, :], 1.0), inc="dv")
    em.op("vector", lambda E, t=C.ones_row_b: E.memset(t[:, :], 1.0), inc="dv")
    em.op("vector", lambda E, t=C.ones_row_f: E.memset(t[:, :], 1.0), inc="dv")
    em.wait_fin("tensor", "c0")
    em.wait_fin("tensor", "c0g")
    em.wait("tensor", "dv")
    em.wait("scalar", "dv")


def emit_l0ffn(C, io, bprev, ep):
    em = C.em
    tag = "f0"

    def xr(kc, tb):
        return bprev[:, kc * 1024 + tb * 512:kc * 1024 + (tb + 1) * 512]

    C.wait_free("gpsimd", ("WS", 0))
    C.wait_free("gpsimd", ("WS", 1))
    wr_rdy = em.dma("gpsimd", C.WS[:, 0:256], io["l0_wrT"][:, :], "wf0")
    at_rdy = em.dma("gpsimd", C.WS[:, 256:4352], io["l0_aT"][:, :], "wf0")
    C.wait_free("gpsimd", ("rbf", 1))
    br_rdy = em.dma("gpsimd", C.SMB[0:1, 2040:2048], io["l0_br"][:, :], "wf0")
    mx_rdy = em.dma("sync", C.RW[0:8, 2048:2176], io["mexp"][:, :], "c2")

    em.wait("tensor", "wf0", br_rdy)
    for tb in (0, 1):
        pb = 4 + tb
        ps = C.PS[0:8, 512 * pb:512 * pb + 512]
        for kc in range(KC):
            em.op("tensor",
                  (lambda E, o=ps, l=C.WS[:, kc * 8:(kc + 1) * 8], r=xr(kc, tb),
                   st=(kc == 0): E.matmul(o, l, r, start=st, stop=False)))
        em.op("tensor",
              (lambda E, o=ps, l=C.SMB[0:1, 2040:2048], r=C.ones_row_b[0:1, :]:
               E.matmul(o, l, r, start=False, stop=True)), inc="mm")
    lg_mm = em.count("mm")
    em.wait("vector", "mm", lg_mm)
    for i_ in range(2):
        C.wait_free("vector", ("xsl", i_))
    for tb in (0, 1):
        em.op("vector",
              (lambda E, o=C.ST[0:8, 512 * tb:512 * (tb + 1)],
               a=C.PS[0:8, 512 * (4 + tb):512 * (4 + tb) + 512]:
               E.tensor_copy(o, a)), inc="dv")
    lg_dv = em.count("dv")

    em.wait("tensor", "dv", lg_dv)
    em.wait("tensor", "c2", mx_rdy)
    for tb in (0, 1):
        for tc in range(4):
            gtc = tb * 4 + tc
            lgt = C.RW[:, 2176:2184]
            et = C.RW[:, 2184:2192]
            negmax = C.RW[:, 2192:2193]
            ssum = C.RW[:, 2193:2194]
            rc = C.RW[:, 2194:2195]
            sct = C.RW[:, 2196:2204]
            scT = C.RW[0:8, 2204:2332]
            em.wait("tensor", "dv")
            em.op("tensor",
                  (lambda E, o=C.PS[:, 512 * 6:512 * 6 + 8],
                   a=C.ST[0:8, gtc * 128:(gtc + 1) * 128], i=C.ident_f[0:8, 0:8]:
                   E.transpose(o, a, i)), inc="mm")
            em.wait("vector", "mm")
            em.op("vector", lambda E, o=lgt, a=C.PS[:, 512 * 6:512 * 6 + 8]:
                  E.tensor_copy(o, a), inc="dv")
            em.op("vector", lambda E, o=negmax, a=lgt:
                  E.reduce_max(o, a, axis=AX.X, negate=True), inc="dv")
            em.wait("scalar", "dv")
            em.op("scalar", lambda E, o=et, a=lgt, b=negmax, s=ssum:
                  E.activation(o, a, AF.Exp, bias=b, accum_out=s), inc="ac")
            em.wait("vector", "ac")
            em.op("vector", lambda E, o=rc, a=ssum: E.reciprocal(o, a), inc="dv")
            em.op("vector", lambda E, o=sct, a=et, s=rc:
                  E.tensor_scalar(o, a, s, 16.0, ALU.mult, ALU.mult), inc="dv")
            em.wait("tensor", "dv")
            em.op("tensor",
                  (lambda E, o=C.PS[0:8, 512 * 6 + 128:512 * 6 + 256], a=sct, i=C.ident_f[:, :]:
                   E.transpose(o, a, i)), inc="mm")
            em.wait("vector", "mm")
            em.op("vector", lambda E, o=scT, a=C.PS[0:8, 512 * 6 + 128:512 * 6 + 256]:
                  E.tensor_copy(o, a), inc="dv")
            em.wait("tensor", "dv")
            em.op("tensor",
                  (lambda E, o=C.PS[:, 512 * 7 + 128 * tc:512 * 7 + 128 * (tc + 1)],
                   l=C.RW[0:8, 2048:2176], r=scT:
                   E.matmul(o, l, r, start=True, stop=True)), inc="mm")
        em.wait("vector", "mm")
        em.op("vector", (lambda E, o=C.ST[:, 4096 + 512 * tb:4096 + 512 * (tb + 1)], a=C.bank(7):
                         E.tensor_copy(o, a)), inc="dv")
        if "dbg_srep" in io:
            em.wait("sync", "dv")
            em.dma("sync", io["dbg_srep"][:, 512 * tb:512 * (tb + 1)],
                   C.ST[:, 4096 + 512 * tb:4096 + 512 * (tb + 1)], "dg")
    srep_dv = em.count("dv")
    C.set_free(("xsl", 0), "mm", em.count("mm"))
    C.set_free(("xsl", 1), "mm", em.count("mm"))

    em.wait("tensor", "wf0", br_rdy)
    em.wait("tensor", "dv", srep_dv)
    for tb in (0, 1):
        pb = 4 + tb
        ps = C.bank(pb)
        for kc in range(KC):
            em.op("tensor",
                  (lambda E, o=ps, l=C.WS[:, 256 + kc * 128:256 + (kc + 1) * 128],
                   r=xr(kc, tb), st=(kc == 0), sp=(kc == KC - 1):
                   E.matmul(o, l, r, start=st, stop=sp)),
                  inc=("mm" if kc == KC - 1 else None))
        em.wait("vector", "mm")
        em.op("vector",
              (lambda E, o=C.RW[:, :].bitcast(BF16)[:, 4864 + 512 * tb:4864 + 512 * (tb + 1)],
               a=ps, b=C.ST[:, 4096 + 512 * tb:4096 + 512 * (tb + 1)]:
               E.tensor_mul(o, a, b)), inc="dv")
        if "dbg_lowS" in io:
            em.wait("sync", "dv")
            em.dma("sync", io["dbg_lowS"][:, 512 * tb:512 * (tb + 1)],
                   C.RW[:, :].bitcast(BF16)[:, 4864 + 512 * tb:4864 + 512 * (tb + 1)], "dg")
    low_dv = em.count("dv")

    em.wait("gpsimd", "mm")
    bt_rdy = em.dma("gpsimd", C.WS[:, 256:4352], io["l0_bT"][:, :], "wb")

    if "dbg_lowS" in io:
        em.wait("vector", "dg")
    if "dbg_bt" in io:
        em.wait("sync", "wb", bt_rdy)
        em.dma("sync", io["dbg_bt"][:, :], C.WS[:, 256:4352], "dg2")
        em.wait("tensor", "dg2")
    em.wait("tensor", "wb", bt_rdy)
    em.wait("tensor", "dv", low_dv)
    for jb in range(KC):
        for tb in (0, 1):
            pb = (jb % 2) * 2 + tb
            C.wait_free("tensor", ("bank", pb))
            ps = C.bank(pb)
            em.op("tensor",
                  (lambda E, o=ps, l=C.WS[:, 256 + jb * 128:256 + (jb + 1) * 128],
                   r=C.RW[:, :].bitcast(BF16)[:, 4864 + 512 * tb:4864 + 512 * (tb + 1)]:
                   E.matmul(o, l, r, start=True, stop=True)), inc="mm")
            fs = ep.chunk_cb(jb, tb, ps, em.count("mm"))
            C.set_free(("bank", pb), fs[0], fs[1])
    return ep.finish()


def _f(a):
    return np.ascontiguousarray(a, dtype=np.float32)


def _bm(wt, njb, jw):
    nkc = wt.shape[0] // 128
    return _f(wt.reshape(nkc, 128, njb, jw).transpose(2, 1, 0, 3)
              .reshape(njb * 128, nkc * jw))


def host_prep(inp):
    sh = {}
    x = inp["L0_attn_hidden_states"][0]
    xT = _f(x.T)
    sh["ident"] = np.eye(128, dtype=np.float32)
    sh["mexp"] = np.repeat(np.eye(8, dtype=np.float32), 16, axis=1)
    sh["jcv"] = np.broadcast_to(np.arange(4, dtype=np.float32)[None, :] * 128.0,
                                (128, 4)).copy()
    sh["posr"] = (np.arange(68, dtype=np.float32)[None, :] * 128.0).copy()
    sh["l0_qkvT"] = _bm(np.concatenate(
        [inp["L0_attn_Wq"], inp["L0_attn_Wk"], inp["L0_attn_Wv"]], axis=0).T, 48, 128)
    sh["l0_bv"] = _f(inp["L0_attn_bv"][None, :])
    sh["l0_woT"] = _bm(inp["L0_attn_Wo"].T, KC, 128)
    sh["l0_wrT"] = _bm(inp["L0_ffn_Wr"].T, 1, 8)
    sh["l0_br"] = _f(inp["L0_ffn_br"][None, :])
    sh["l0_aT"] = _bm(inp["L0_ffn_A"].transpose(2, 0, 1).reshape(D, 128), 1, 128)
    sh["l0_bT"] = _f(inp["L0_ffn_B"].transpose(0, 2, 1).reshape(128, D))
    sh["l1_woT"] = _bm(inp["L1_attn_w_out"].T, KC, 128)
    sh["l1_wT"] = _bm(inp["L1_ffn_W"].T, KC, 128)
    sh["l1_bias"] = _f(inp["L1_ffn_bias"].reshape(32, 128).T)
    sh["l1_laT"] = _bm(inp["L1_ffn_lora_A"].transpose(2, 0, 1).reshape(D, 128), 1, 128)
    sh["l1_lbT"] = _f(inp["L1_ffn_lora_B"].transpose(0, 2, 1).reshape(128, D))
    sh["l1_lw"] = _f(inp["L1_ffn_lora_weights"][:, None])

    wqkvT = _f(inp["L1_attn_w_qkv"].T)
    percore = []
    import math
    slopes = 1.0 / (2.0 ** (np.arange(1, 33, dtype=np.float64) * (8.0 / 32)))
    ii = np.arange(128)[:, None]
    jj = np.arange(512)[None, :]
    for c in range(NCORE):
        pc = {}
        pc["x_in"] = _f(xT[:, c * T:(c + 1) * T].reshape(KC, 128, T)
                        .transpose(1, 0, 2).reshape(128, KC * T))
        cols = np.concatenate([np.arange(512 * c, 512 * c + 512),
                               4096 + np.arange(512 * c, 512 * c + 512),
                               8192 + np.arange(512 * c, 512 * c + 512)])
        pc["l1_qkvT"] = _bm(wqkvT[:, cols], 12, 128)
        for nm, key in (("qw", "L1_attn_q_ln_w"), ("qb", "L1_attn_q_ln_b"),
                        ("kw", "L1_attn_k_ln_w"), ("kb", "L1_attn_k_ln_b")):
            pc["l1_" + nm] = _f(inp[key][512 * c:512 * (c + 1)].reshape(4, 128).T)
        sl = slopes[4 * c:4 * c + 4]
        pc["l1_slopes"] = _f(np.broadcast_to(sl[None, :], (128, 4)))
        strips = np.zeros((4, 5, 128, 512), dtype=np.float64)
        for h in range(4):
            base = sl[h] * (ii - jj)
            strips[h, 0] = base
            for jc in range(4):
                m = np.where(ii <= jj - 128 * jc, base, NEG)
                strips[h, 1 + jc] = m
        pc["l1_strips"] = strips.astype(np.float32).reshape(4 * 5 * 128, 512)
        percore.append(pc)
    return sh, percore


def _decl(nc, name, shape, dtype, kind):
    return nc.dram_tensor(name, list(shape), dtype, kind=kind).ap()


def build_l0ffn_stage(dbg=False):
    nc = bass.Bass("TRN2", target_bir_lowering=False, debug=False, num_devices=8)
    io = {}
    io["x1f"] = _decl(nc, "x1f", [128, KC * T], F32, "ExternalInput")
    for nm, shp in (("l0_wrT", [128, 256]), ("l0_br", [1, 8]), ("l0_aT", [128, KC * 128]),
                    ("l0_bT", [128, D]), ("mexp", [8, 128]), ("ident", [128, 128])):
        io[nm] = _decl(nc, nm, shp, F32, "ExternalInput")
    io["r2"] = _decl(nc, "r2", [128, KC * T], F32, "Internal")
    io["x2f"] = _decl(nc, "x2f", [128, KC * T], F32, "ExternalOutput")
    if dbg:
        io["dbg_lg"] = _decl(nc, "dbg_lg", [8, 1024], F32, "ExternalOutput")
        io["dbg_srep"] = _decl(nc, "dbg_srep", [128, 1024], F32, "ExternalOutput")
        io["dbg_lowS"] = _decl(nc, "dbg_lowS", [128, 1024], BF16, "ExternalOutput")
        io["dbg_bt"] = _decl(nc, "dbg_bt", [128, 4096], BF16, "ExternalOutput")
    with ExitStack() as ctx:
        em = Em(nc)
        C = Ctx(nc, em, ctx)
        setup_consts(C, io)
        bld = em.dma("gpsimd", C.A0[:, 0:KC * 1024], io["x1f"][:, :], "c0g")
        em.wait_fin("tensor", "c0g")
        ep = Ep(C, "e2", io["x1f"], io["r2"], io["x2f"], C.A1)
        emit_l0ffn(C, io, C.A0, ep)
        for i in range(4):
            em.wait("sync", "r%d" % i)
        em.finalize(ctx)
    return nc


_NC_CACHE = {}


def _get_nc(name, builder):
    if name not in _NC_CACHE:
        _NC_CACHE[name] = builder()
    return _NC_CACHE[name]


def kernel(**inputs):
    return kernel_fused(**inputs)


def kernel_multilaunch(**inputs):
    sh, pc = host_prep(inputs)
    cores = list(range(NCORE))

    nc1 = _get_nc("s1", build_l0attn_stage)
    m1 = []
    for c in cores:
        m1.append({"x_in": pc[c]["x_in"], "l0_qkvT": sh["l0_qkvT"],
                   "l0_bv": sh["l0_bv"], "l0_woT": sh["l0_woT"],
                   "ident": sh["ident"]})
    r1 = run_bass_kernel_spmd(nc1, m1, core_ids=cores)
    x1 = [r1.results[c]["x1f"] for c in cores]

    nc2 = _get_nc("s2", build_l0ffn_stage)
    m2 = []
    for c in cores:
        m2.append({"x1f": x1[c], "l0_wrT": sh["l0_wrT"], "l0_br": sh["l0_br"],
                   "l0_aT": sh["l0_aT"], "l0_bT": sh["l0_bT"],
                   "mexp": sh["mexp"], "ident": sh["ident"]})
    r2 = run_bass_kernel_spmd(nc2, m2, core_ids=cores)
    x2 = [r2.results[c]["x2f"] for c in cores]

    nc3 = _get_nc("s3", build_l1attn_stage)
    m3 = []
    for c in cores:
        m = {"x2f": x2[c], "ident": sh["ident"], "jcv": sh["jcv"],
             "posr": sh["posr"], "l1_woT": sh["l1_woT"]}
        for nm in ("l1_qkvT", "l1_qw", "l1_qb", "l1_kw", "l1_kb",
                   "l1_slopes", "l1_strips"):
            m[nm] = pc[c][nm]
        m3.append(m)
    r3 = run_bass_kernel_spmd(nc3, m3, core_ids=cores)
    x3 = [r3.results[c]["x3f"] for c in cores]

    nc4 = _get_nc("s4", build_l1ffn_stage)
    m4 = []
    for c in cores:
        m4.append({"x3f": x3[c], "l1_wT": sh["l1_wT"], "l1_bias": sh["l1_bias"],
                   "l1_laT": sh["l1_laT"], "l1_lbT": sh["l1_lbT"],
                   "l1_lw": sh["l1_lw"], "mexp": sh["mexp"], "ident": sh["ident"]})
    r4 = run_bass_kernel_spmd(nc4, m4, core_ids=cores)
    y = np.concatenate(
        [np.asarray(r4.results[c]["y"]).reshape(128, KC, T)
         .transpose(1, 0, 2).reshape(D, T).T for c in cores], axis=0)
    return y[None, :, :].astype(np.float32)


def emit_l0attn(C, io, ep):
    em = C.em
    stb = C.stb()

    bv_rdy = em.dma("gpsimd", C.stb()[0:1, 0:1024], io["l0_bv"][:, :], "c0g")
    em.wait_fin("tensor", "c0g")

    qsp = io["qspill"].rearrange("(h p) t -> p h t", p=128)
    kgi = io["kgat_in"].rearrange("(kc p) t -> p kc t", p=128)
    vgi = io["vgat_in"].rearrange("(tc p) d -> p tc d", p=128)

    def xr(kc, tb):
        return C.A0[:, kc * 1024 + tb * 512:kc * 1024 + (tb + 1) * 512]

    def extra(jb, tb):
        if jb >= 40:
            return [(C.stb()[0:1, (jb - 40) * 128:(jb - 39) * 128],
                     C.ones_row_b[0:1, :])]
        return []

    def qkv_cb(jb, tb, ps, mmcnt):
        i = (jb * 2 + tb) % 4
        slot = C.SMB[:, 512 * i:512 * (i + 1)]
        em.wait("vector", "mm", mmcnt)
        C.wait_free("vector", ("esl", i))
        if jb < 32:
            em.op("vector", lambda E, o=slot, a=ps:
                  E.tensor_scalar_mul(o, a, ATT_SCALE), inc="dv")
        else:
            em.op("vector", lambda E, o=slot, a=ps: E.tensor_copy(o, a), inc="dv")
        dvc = em.count("dv")
        bankfree = ("dv", dvc)
        if jb < 40:
            em.wait("sync", "dv", dvc)
            dst = (qsp[:, jb, 512 * tb:512 * (tb + 1)] if jb < 32
                   else kgi[:, jb - 32, 512 * tb:512 * (tb + 1)])
            sc = em.dma("sync", dst, slot, "e%d" % i)
            C.set_free(("esl", i), "e%d" % i, sc)
        else:
            dvch = jb - 40
            for tc in range(4):
                pb = 4 + (tc % 2)
                C.wait_free("tensor", ("vtb", pb))
                em.wait("tensor", "dv", dvc)
                em.op("tensor",
                      (lambda E, o=C.bank_bf(pb)[:, 0:128],
                       a=slot[:, 128 * tc:128 * (tc + 1)], ii=C.ident_b[:, :]:
                       E.transpose(o, a, ii)), inc="mm")
                em.wait("vector", "mm")
                C.wait_free("vector", ("vsl", pb))
                vslot = C.stb()[:, 1024 + 128 * (pb - 4):1024 + 128 * (pb - 3)]
                em.op("vector", (lambda E, o=vslot, a=C.bank_bf(pb)[:, 0:128]:
                                 E.tensor_copy(o, a)), inc="dv")
                C.set_free(("vtb", pb), "dv", em.count("dv"))
                em.wait("sync", "dv")
                vc = em.dma("sync", vgi[:, tb * 4 + tc, 128 * dvch:128 * (dvch + 1)],
                            vslot, "v%d" % (pb - 4))
                C.set_free(("vsl", pb), "v%d" % (pb - 4), vc)
            C.set_free(("esl", i), "mm", em.count("mm"))
        return bankfree

    emit_proj(C, "qkv", io["l0_qkvT"], 48, 128, xr, qkv_cb, extra=extra,
              jb_order=list(range(32, 48)) + list(range(32)))
    for t in range(4):
        if ("bank", t) in C.free:
            C.set_free(("ab", t), *C.free[("bank", t)])

    for s in ("g0", "g1", "g2", "g3", "v0", "v1"):
        em.wait("gpsimd", s)
    em.op("gpsimd",
          (lambda E: E.collective_compute(
              "AllGather", ALU.bypass, replica_groups=[list(range(NCORE))],
              ins=[io["kgat_in"].opt()], outs=[io["kgat"].opt()])), inc="cc")
    em.op("gpsimd",
          (lambda E: E.collective_compute(
              "AllGather", ALU.bypass, replica_groups=[list(range(NCORE))],
              ins=[io["vgat_in"].opt()], outs=[io["vgat"].opt()])), inc="cc")
    cc_done = em.count("cc")

    mm_qkv_done = em.count("mm")
    kgs = io["kgat"].rearrange("(r f) t -> f r t", f=1024)
    vgs = io["vgat"].rearrange("(ch p) d -> p ch d", p=128)
    em.wait("sync", "cc", cc_done)

    def load_kv(g):
        if g >= 8:
            return None
        s = g % 2
        C.wait_free("sync", ("kvs", s))
        em.dma("sync",
               C.A0[:, 16384 * s:16384 * s + 8192].rearrange("p (r t) -> p r t", t=1024),
               kgs[128 * g:128 * (g + 1), :, :], "k%d" % s)
        kc_ = em.dma("sync",
                     C.A0[:, 16384 * s + 8192:16384 * (s + 1)].rearrange(
                         "p (ch d) -> p ch d", d=128),
                     vgs[:, :, 128 * g:128 * (g + 1)], "k%d" % s)
        return kc_

    def load_q(qh, g):
        sq = (4 * g + qh) % 2
        C.wait_free("sync", ("qts", sq))
        return em.dma("sync", stb[:, 1024 * sq:1024 * (sq + 1)],
                      qsp[:, 4 * g + qh, :], "q%d" % sq)

    em.wait("sync", "v0")
    em.wait("sync", "v1")
    for s_ in ("e0", "e1", "e2", "e3"):
        em.wait("sync", s_)
    em.wait("sync", "mm", mm_qkv_done)
    kvrdy = {0: load_kv(0), 1: load_kv(1)}
    qrdy = {0: load_q(0, 0)}
    for g in range(8):
        s = g % 2
        kT = C.A0[:, 16384 * s:16384 * s + 8192]
        vT = C.A0[:, 16384 * s + 8192:16384 * (s + 1)]
        em.wait("tensor", "k%d" % s, kvrdy[g])
        for qh in range(4):
            hq = 4 * g + qh
            sq = hq % 2
            em.wait("tensor", "q%d" % sq, qrdy[hq])
            if hq + 1 < 32:
                qrdy[hq + 1] = load_q((hq + 1) % 4, (hq + 1) // 4)
            rhs_q0 = stb[:, 1024 * sq:1024 * sq + 512]
            for t in range(4):
                C.wait_free("tensor", ("bank", t))
                em.op("tensor",
                      (lambda E, o=C.bank(t), l=kT[:, 128 * t:128 * (t + 1)], r=rhs_q0:
                       E.matmul(o, l, r, start=True, stop=True)), inc="mm")
            cmm = em.count("mm")
            em.wait("vector", "mm", cmm)
            em.op("vector", (lambda E, o=C.ST[:, 3072:3073], a=C.PS[:, 0:2048]:
                             E.reduce_max(o, a, axis=AX.X)), inc="dv")
            for t in range(4):
                C.set_free(("bank", t), "dv", em.count("dv"))
            em.wait("tensor", "dv")
            em.op("tensor",
                  (lambda E, o=C.PS[0:1, 2048:2176], a=C.ST[:, 3072:3073],
                   ii=C.ident_f[:, :]: E.transpose(o, a, ii)), inc="mm")
            em.wait("vector", "mm")
            em.op("vector", (lambda E, o=C.RW[0:1, 0:1], a=C.PS[0:1, 2048:2176]:
                             E.reduce_max(o, a, axis=AX.X)), inc="dv")
            em.op("vector", (lambda E, o=C.RW[0:1, 1:2], a=C.RW[0:1, 0:1]:
                             E.tensor_scalar_mul(o, a, -1.0)), inc="dv")
            em.wait("tensor", "dv")
            em.op("tensor",
                  (lambda E, o=C.PS[:, 2048:2049], l=C.ones_row_f[0:1, 0:128],
                   r=C.RW[0:1, 1:2]: E.matmul(o, l, r, start=True, stop=True)),
                  inc="mm")
            em.wait("vector", "mm")
            em.wait("vector", "ac")
            em.op("vector", (lambda E, o=C.ST[:, 3074:3075], a=C.PS[:, 2048:2049]:
                             E.tensor_copy(o, a)), inc="dv")
            negc_dv = em.count("dv")
            em.wait("scalar", "dv", negc_dv)
            acc0 = C.ST[:, 4608:5632]
            for b in range(2):
                rhs_q = stb[:, 1024 * sq + 512 * b:1024 * sq + 512 * (b + 1)]
                pend = []

                def flush_l0(rec, vT=vT):
                    rs_, t0_, fin_ = rec
                    em.wait("tensor", "ac", rs_[1])
                    if fin_:
                        C.wait_free("tensor", ("cbank",))
                    for k2 in range(2):
                        t = t0_ + k2
                        pr = stb[:, 2048 + 1024 * rs_[0] + 512 * k2:
                                 2048 + 1024 * rs_[0] + 512 * (k2 + 1)]
                        st_ = t == 0
                        sp_ = t == 63
                        em.op("tensor",
                              (lambda E, o=C.PS[:, 3584:4096],
                               l=vT[:, 128 * t:128 * (t + 1)], r=pr, s0=st_, s1=sp_:
                               E.matmul(o, l, r, start=s0, stop=s1)),
                              inc=("mm" if k2 == 1 else None))
                    C.set_free(("ring", rs_[0]), "mm", em.count("mm"))

                for p in range(32):
                    t0 = 2 * p
                    pa = (p % 2) * 2
                    rs = p % 4
                    for k2 in range(2):
                        C.wait_free("tensor", ("bank", pa + k2))
                        em.op("tensor",
                              (lambda E, o=C.bank(pa + k2),
                               l=kT[:, 128 * (t0 + k2):128 * (t0 + k2 + 1)],
                               r=rhs_q: E.matmul(o, l, r, start=True, stop=True)),
                              inc="mm")
                    qkmm = em.count("mm")
                    em.wait("scalar", "mm", qkmm)
                    C.wait_free("scalar", ("ring", rs))
                    C.wait_free("scalar", ("ringv", rs))
                    em.op("scalar",
                          (lambda E, o=stb[:, 2048 + 1024 * rs:2048 + 1024 * (rs + 1)],
                           a=C.PS[:, 1024 * (p % 2):1024 * (p % 2) + 1024],
                           bb=C.ST[:, 3074:3075]: E.activation(o, a, AF.Exp, bias=bb)),
                          inc="ac", noser=True)
                    eac = em.count("ac")
                    C.set_free(("bank", pa), "ac", eac)
                    C.set_free(("bank", pa + 1), "ac", eac)
                    em.wait("vector", "ac", eac)
                    tile = stb[:, 2048 + 1024 * rs:2048 + 1024 * (rs + 1)]
                    if p == 0:
                        C.wait_free("vector", ("accl0",))
                        em.op("vector", lambda E, o=acc0, a=tile: E.tensor_copy(o, a),
                              inc="dv")
                    else:
                        em.op("vector", lambda E, o=acc0, a=tile: E.tensor_add(o, o, a),
                              inc="dv")
                    C.set_free(("ringv", rs), "dv", em.count("dv"))
                    pend.append(((rs, eac), t0, p == 0))
                    if len(pend) > 2:
                        flush_l0(pend.pop(0))
                while pend:
                    flush_l0(pend.pop(0))
                em.op("vector",
                      (lambda E, o=acc0[:, 0:512], a=acc0[:, 0:512], b=acc0[:, 512:1024]:
                       E.tensor_add(o, a, b)), inc="dv")
                C.wait_free("tensor", ("sbank",))
                em.wait("tensor", "dv", em.count("dv"))
                em.op("tensor",
                      (lambda E, o=C.PS[0:1, 3072:3584], l=C.ones_col_f[:, :],
                       r=acc0[:, 0:512]:
                       E.matmul(o, l, r, start=True, stop=True)), inc="mm")
                C.set_free(("accl0",), "mm", em.count("mm"))
                pvmm = em.count("mm")
                em.wait("scalar", "mm", pvmm)
                em.op("scalar", (lambda E, o=C.RW[0:1, 512:1024], a=C.PS[0:1, 3072:3584]:
                                 E.activation(o, a, AF.Ln)), inc="ac")
                em.wait("vector", "ac")
                em.op("vector", (lambda E, o=C.RW[0:1, 512:1024], a=C.RW[0:1, 512:1024]:
                                 E.tensor_scalar_mul(o, a, -1.0)), inc="dv")
                em.wait("scalar", "dv")
                em.op("scalar", (lambda E, o=C.RW[0:1, 1024:1536], a=C.RW[0:1, 512:1024]:
                                 E.activation(o, a, AF.Exp)), inc="ac")
                em.wait("tensor", "ac")
                em.op("tensor",
                      (lambda E, o=C.PS[:, 2560:3072], l=C.ones_row_f[0:1, 0:128],
                       r=C.RW[0:1, 1024:1536]: E.matmul(o, l, r, start=True, stop=True)),
                      inc="mm")
                em.wait("vector", "mm")
                em.op("vector", (lambda E, o=C.ST[:, 3584:4096], a=C.PS[:, 2560:3072]:
                                 E.tensor_copy(o, a)), inc="dv")
                em.op("vector",
                      (lambda E, o=C.A1[:, hq * 1024 + 512 * b:hq * 1024 + 512 * (b + 1)],
                       a=C.PS[:, 3584:4096], bx=C.ST[:, 3584:4096]:
                       E.tensor_mul(o, a, bx)), inc="dv")
                C.set_free(("sbank",), "ac", em.count("ac"))
                C.set_free(("cbank",), "dv", em.count("dv"))
            C.set_free(("qts", sq), "mm", em.count("mm"))
        kvdone = em.count("mm")
        C.set_free(("kvs", s), "mm", kvdone)
        if g + 2 < 8:
            kvrdy[g + 2] = load_kv(g + 2)
    return em.count("dv")


def build_l0attn_stage():
    nc = bass.Bass("TRN2", target_bir_lowering=False, debug=False, num_devices=8)
    io = {}
    io["x_in"] = _decl(nc, "x_in", [128, KC * T], F32, "ExternalInput")
    io["l0_qkvT"] = _decl(nc, "l0_qkvT", [48 * 128, KC * 128], F32, "ExternalInput")
    io["l0_bv"] = _decl(nc, "l0_bv", [1, 1024], F32, "ExternalInput")
    io["l0_woT"] = _decl(nc, "l0_woT", [KC * 128, KC * 128], F32, "ExternalInput")
    io["ident"] = _decl(nc, "ident", [128, 128], F32, "ExternalInput")
    io["qspill"] = _decl(nc, "qspill", [D, T], BF16, "Internal")
    io["kgat_in"] = _decl(nc, "kgat_in", [1024, T], BF16, "Internal")
    io["kgat"] = _decl(nc, "kgat", [S, T], BF16, "Internal")
    io["vgat_in"] = _decl(nc, "vgat_in", [T, 1024], BF16, "Internal")
    io["vgat"] = _decl(nc, "vgat", [S, 1024], BF16, "Internal")
    io["r1"] = _decl(nc, "r1", [128, KC * T], F32, "Internal")
    io["x1f"] = _decl(nc, "x1f", [128, KC * T], F32, "ExternalOutput")
    with ExitStack() as ctx:
        em = Em(nc)
        C = Ctx(nc, em, ctx)
        setup_consts(C, io)
        em.dma("gpsimd", C.A0[:, 0:KC * 1024], io["x_in"][:, :], "c0g")
        em.wait_fin("tensor", "c0g")
        for t in range(4):
            C.free[("ab", t)] = None
        ctx_all = emit_l0attn_with_wo(C, io)
        for i in range(4):
            em.wait("sync", "r%d" % i)
        em.finalize(ctx)
    return nc


def emit_l0attn_with_wo(C, io):
    em = C.em
    ctx_all = emit_l0attn(C, io, None)
    ep = Ep(C, "e1", io["x_in"], io["r1"], io["x1f"], C.A0)
    for i in range(4):
        C.set_free(("xsl", i), "dv", ctx_all)
        C.set_free(("rsl", i), "dv", ctx_all)
    em.wait("tensor", "dv", ctx_all)
    for pb in range(4):
        C.set_free(("bank", pb), C.free.get(("ab", pb), ("dv", 0))[0]
                   if C.free.get(("ab", pb)) else "dv",
                   C.free.get(("ab", pb))[1] if C.free.get(("ab", pb)) else 0)

    def ctx_rhs(kc, tb):
        return C.A1[:, kc * 1024 + tb * 512:kc * 1024 + (tb + 1) * 512]

    emit_proj(C, "wo", io["l0_woT"], KC, 128, ctx_rhs, ep.chunk_cb)
    return ep.finish()


def emit_l1attn(C, io, ep):
    em = C.em
    stb = C.stb()
    RG = [list(range(NCORE))]

    gxc = em.dma("sync", io["xgat_in"].rearrange("(kc p) t -> p kc t", p=128),
                 C.A1[:, :].rearrange("p (kc t) -> p kc t", t=1024), "gx")
    em.wait("gpsimd", "gx", gxc)
    em.op("gpsimd",
          (lambda E: E.collective_compute(
              "AllGather", ALU.bypass, replica_groups=RG,
              ins=[io["xgat_in"].opt()], outs=[io["xgat"].opt()])), inc="cc")
    cc_x = em.count("cc")

    em.dma("sync", C.SM[:, 0:4], io["l1_slopes"][:, :], "c3")
    em.dma("sync", C.SM[:, 4:8], io["l1_qw"][:, :], "c3")
    em.dma("sync", C.SM[:, 8:12], io["l1_qb"][:, :], "c3")
    em.dma("sync", C.SM[:, 12:16], io["l1_kw"][:, :], "c3")
    em.dma("sync", C.SM[:, 16:20], io["l1_kb"][:, :], "c3")
    em.dma("sync", C.SM[:, 20:24], io["jcv"][:, :], "c3")
    em.dma("sync", C.RW[0:1, 2944:3012], io["posr"][:, :], "c3")
    em.wait("vector", "c3")
    em.op("vector", lambda E, o=C.SM[:, 28:32], a=C.SM[:, 4:8]:
          E.tensor_scalar_mul(o, a, ATT_SCALE), inc="dv")
    em.op("vector", lambda E, o=C.SM[:, 32:36], a=C.SM[:, 8:12]:
          E.tensor_scalar_mul(o, a, ATT_SCALE), inc="dv")

    em.wait("sync", "cc", cc_x)
    xg = io["xgat"].rearrange("(r kc p) t -> p r kc t", p=128, kc=KC)
    wq = io["l1_qkvT"].rearrange("(jb p) f -> p jb f", p=128)
    qrs = io["qraw"].rearrange("(h p) t -> p h t", p=128)
    krs = io["kraw"].rearrange("(h p) t -> p h t", p=128)
    vts = io["vtok1"].rearrange("(tc p) d -> p tc d", p=128)
    sqi = io["sqk_in"]

    em.wait("gpsimd", "cc", cc_x)
    groups = (list(range(8)), list(range(8, 12)))
    wg_rdy = {}
    pbctr = [0]
    for gi, grp in enumerate(groups):
        if gi == 1:
            em.wait("gpsimd", "mm")
        for ji, jb in enumerate(grp):
            wg_rdy[jb] = em.dma(
                "gpsimd",
                C.A1[:, ji * 4096:(ji + 1) * 4096],
                wq[:, jb, :], "wg")
        for tblk in range(16):
            s = tblk % 2
            C.wait_free("sync", ("xb", s))
            xrd = em.dma("sync",
                         C.A0[:, 16384 * s:16384 * (s + 1)].rearrange(
                             "p (kc t) -> p kc t", t=512),
                         xg[:, tblk // 2, :, 512 * (tblk % 2):512 * (tblk % 2 + 1)],
                         "xb%d" % s)
            em.wait("tensor", "xb%d" % s, xrd)
            em.wait("tensor", "wg", wg_rdy[grp[-1]])
            for ji, jb in enumerate(grp):
                pb = pbctr[0] % 4
                pbctr[0] += 1
                C.wait_free("tensor", ("bank", pb))
                ps = C.bank(pb)
                for kc in range(KC):
                    em.op("tensor",
                          (lambda E, o=ps, l=C.A1[:, ji * 4096 + kc * 128:ji * 4096 + (kc + 1) * 128],
                           r=C.A0[:, 16384 * s + kc * 512:16384 * s + (kc + 1) * 512],
                           st=(kc == 0), sp=(kc == KC - 1):
                           E.matmul(o, l, r, start=st, stop=sp)),
                          inc=("mm" if kc == KC - 1 else None))
                mmc = em.count("mm")
                ei = pb % 2
                em.wait("vector", "mm", mmc)
                C.wait_free("vector", ("cls", ei))
                cslot = C.SMB[:, 512 * ei:512 * (ei + 1)]
                em.op("vector", lambda E, o=cslot, a=ps:
                      E.tensor_scalar(o, a, 8.0, -8.0, ALU.min, ALU.max), inc="dv")
                C.set_free(("bank", pb), "dv", em.count("dv"))
                if jb < 8:
                    sq_ = C.SMB[:, 1024 + 512 * ei:1024 + 512 * (ei + 1)]
                    C.wait_free("vector", ("sqs", ei))
                    em.op("vector", lambda E, o=sq_, a=cslot: E.tensor_mul(o, a, a),
                          inc="dv")
                    dvc = em.count("dv")
                    em.wait("sync", "dv", dvc)
                    dst = (qrs[:, jb, 512 * tblk:512 * (tblk + 1)] if jb < 4
                           else krs[:, jb - 4, 512 * tblk:512 * (tblk + 1)])
                    sc_ = em.dma("sync", dst, cslot, "p%d" % ei)
                    C.set_free(("cls", ei), "p%d" % ei, sc_)
                    sb0 = 4 if jb < 4 else 6
                    st_ = jb % 4 == 0
                    sp_ = jb % 4 == 3
                    em.wait("tensor", "dv", dvc)
                    C.wait_free("tensor", ("bank", sb0))
                    C.wait_free("tensor", ("bank", sb0 + 1))
                    em.op("tensor",
                          (lambda E, o=C.PS[0:1, 512 * sb0:512 * sb0 + 512],
                           l=C.ones_col_b[:, :], r=cslot, s0=st_, s1=sp_:
                           E.matmul(o, l, r, start=s0, stop=s1)))
                    em.op("tensor",
                          (lambda E, o=C.PS[0:1, 512 * (sb0 + 1):512 * (sb0 + 1) + 512],
                           l=C.ones_col_b[:, :], r=sq_, s0=st_, s1=sp_:
                           E.matmul(o, l, r, start=s0, stop=s1)),
                          inc="mm")
                    C.set_free(("sqs", ei), "mm", em.count("mm"))
                    if sp_:
                        em.wait("vector", "mm")
                        ri = tblk % 2
                        C.wait_free("vector", ("srw", ri))
                        for k2 in range(2):
                            em.op("vector",
                                  (lambda E, o=C.RW[0:1, 1024 * ri + 512 * k2:1024 * ri + 512 * (k2 + 1)],
                                   a=C.PS[0:1, 512 * (sb0 + k2):512 * (sb0 + k2) + 512]:
                                   E.tensor_copy(o, a)), inc="dv")
                        C.set_free(("bank", sb0), "dv", em.count("dv"))
                        C.set_free(("bank", sb0 + 1), "dv", em.count("dv"))
                        em.wait("sync", "dv")
                        row0 = 0 if jb < 4 else 2
                        for k2 in range(2):
                            sc2 = em.dma("sync",
                                         sqi[row0 + k2:row0 + k2 + 1,
                                             512 * tblk:512 * (tblk + 1)],
                                         C.RW[0:1, 1024 * ri + 512 * k2:1024 * ri + 512 * (k2 + 1)],
                                         "sa%d" % ri)
                        C.set_free(("srw", ri), "sa%d" % ri, sc2)
                else:
                    dvch = jb - 8
                    dvc = em.count("dv")
                    for tc in range(4):
                        pt = 4 + (tc % 2)
                        C.wait_free("tensor", ("bank", pt))
                        em.wait("tensor", "dv", dvc)
                        em.op("tensor",
                              (lambda E, o=C.bank_bf(pt)[:, 0:128],
                               a=cslot[:, 128 * tc:128 * (tc + 1)], ii=C.ident_b[:, :]:
                               E.transpose(o, a, ii)), inc="mm")
                        em.wait("vector", "mm")
                        C.wait_free("vector", ("vsl", pt))
                        vslot = C.SMB[:, 1024 + 128 * (pt - 4):1024 + 128 * (pt - 3)]
                        em.op("vector", (lambda E, o=vslot, a=C.bank_bf(pt)[:, 0:128]:
                                         E.tensor_copy(o, a)), inc="dv")
                        C.set_free(("bank", pt), "dv", em.count("dv"))
                        em.wait("sync", "dv")
                        vc = em.dma("sync",
                                    vts[:, 4 * tblk + tc, 128 * dvch:128 * (dvch + 1)],
                                    vslot, "v%d" % (pt - 4))
                        C.set_free(("vsl", pt), "v%d" % (pt - 4), vc)
                    C.set_free(("cls", ei), "mm", em.count("mm"))
    qkv_mm = em.count("mm")
    qkv_dv = em.count("dv")

    for s_ in ("sa0", "sa1"):
        em.wait("gpsimd", s_)
    em.op("gpsimd",
          (lambda E: E.collective_compute(
              "AllReduce", ALU.add, replica_groups=RG,
              ins=[io["sqk_in"].opt()], outs=[io["sqk"].opt()])), inc="cc")
    cc_s = em.count("cc")

    em.wait("sync", "cc", cc_s)
    for s_ in ("p0", "p1", "v0", "v1", "sa0", "sa1"):
        em.wait("vector", s_)
        em.wait("sync", s_)
    em.wait("gpsimd", "dv", qkv_dv)
    em.wait("gpsimd", "mm", qkv_mm)
    for s_ in ("p0", "p1", "v0", "v1"):
        em.wait("gpsimd", s_)
    qns = io["qn"].rearrange("(h p) t -> p h t", p=128)
    kns = io["kn"].rearrange("(h p) t -> p h t", p=128)
    for tblk in range(16):
        C.wait_free("sync", ("strow",))
        src = em.dma("sync",
                     C.RW[0:1, 0:2048].rearrange("p (k t) -> p k t", t=512),
                     io["sqk"][0:4, 512 * tblk:512 * (tblk + 1)], "sr")
        em.wait("vector", "sr", src)
        for qk in range(2):
            srow = C.RW[0:1, 1024 * qk:1024 * qk + 512]
            sqrow = C.RW[0:1, 1024 * qk + 512:1024 * qk + 1024]
            mu = C.RW[0:1, 2048:2560]
            al = C.RW[0:1, 2560:2944]
            em.op("vector", lambda E, o=mu, a=srow: E.tensor_scalar_mul(o, a, 1.0 / D),
                  inc="dv")
            em.op("vector", lambda E, o=sqrow, a=sqrow: E.tensor_scalar_mul(o, a, 1.0 / D),
                  inc="dv")
            em.op("vector", lambda E, o=srow, a=mu, b=mu: E.tensor_mul(o, a, b), inc="dv")
            em.op("vector", lambda E, o=sqrow, a=sqrow, b=srow: E.tensor_sub(o, a, b),
                  inc="dv")
            em.op("vector", lambda E, o=sqrow, a=sqrow: E.tensor_scalar_add(o, a, EPS),
                  inc="dv")
            em.wait("scalar", "dv")
            em.op("scalar", lambda E, o=sqrow, a=sqrow: E.activation(o, a, AF.Ln),
                  inc="ac")
            em.wait("vector", "ac")
            em.op("vector", lambda E, o=sqrow, a=sqrow: E.tensor_scalar_mul(o, a, -0.5),
                  inc="dv")
            em.wait("scalar", "dv")
            em.op("scalar", lambda E, o=sqrow, a=sqrow: E.activation(o, a, AF.Exp),
                  inc="ac")
            em.wait("vector", "ac")
            em.op("vector", lambda E, o=mu, a=mu, b=sqrow: E.tensor_mul(o, a, b), inc="dv")
            em.op("vector", lambda E, o=mu, a=mu: E.tensor_scalar_mul(o, a, -1.0), inc="dv")
            rowsdv = em.count("dv")
            em.wait("tensor", "dv", rowsdv)
            for k2, row in ((0, sqrow), (1, mu)):
                pb = 4 + 2 * qk + k2
                C.wait_free("tensor", ("bank", pb))
                em.op("tensor",
                      (lambda E, o=C.bank(pb), l=C.ones_row_f[0:1, 0:128], r=row:
                       E.matmul(o, l, r, start=True, stop=True)), inc="mm")
            em.wait("vector", "mm")
            for k2 in range(2):
                pb = 4 + 2 * qk + k2
                em.op("vector",
                      (lambda E, o=C.ST[:, 1024 + 1024 * qk + 512 * k2:1024 + 1024 * qk + 512 * (k2 + 1)],
                       a=C.bank(pb): E.tensor_copy(o, a)), inc="dv")
                C.set_free(("bank", pb), "dv", em.count("dv"))
        C.set_free(("strow",), "dv", em.count("dv"))
        repdv = em.count("dv")
        for qk in range(2):
            raws = qrs if qk == 0 else krs
            ns = qns if qk == 0 else kns
            wcol = C.SM[:, 28:32] if qk == 0 else C.SM[:, 12:16]
            bcol = C.SM[:, 32:36] if qk == 0 else C.SM[:, 16:20]
            arep = C.ST[:, 1024 + 1024 * qk:1024 + 1024 * qk + 512]
            brep = C.ST[:, 1024 + 1024 * qk + 512:1024 + 1024 * qk + 1024]
            for ch in range(4):
                mi = ch % 2
                C.wait_free("gpsimd", ("mrs", mi))
                mrd = em.dma("gpsimd", C.SMB[:, 512 * mi:512 * (mi + 1)],
                             raws[:, ch, 512 * tblk:512 * (tblk + 1)], "m%d" % mi)
                em.wait("vector", "m%d" % mi, mrd)
                em.wait("vector", "dv", repdv)
                tmp = C.ST[:, 512 * mi:512 * (mi + 1)]
                C.wait_free("vector", ("tmp", mi))
                em.op("vector", (lambda E, o=tmp, a=C.SMB[:, 512 * mi:512 * (mi + 1)],
                                 b=arep: E.tensor_mul(o, a, b)), inc="dv")
                C.set_free(("mrs", mi), "dv", em.count("dv"))
                em.op("vector", (lambda E, o=tmp, a=tmp, b=brep: E.tensor_add(o, a, b)),
                      inc="dv")
                ni = 2 + mi
                C.wait_free("vector", ("nsl", ni))
                em.op("vector",
                      (lambda E, o=C.SMB[:, 512 * ni:512 * (ni + 1)], a=tmp,
                       w=wcol[:, ch:ch + 1], bb=bcol[:, ch:ch + 1]:
                       E.tensor_scalar(o, a, w, bb, ALU.mult, ALU.add)), inc="dv")
                C.set_free(("tmp", mi), "dv", em.count("dv"))
                em.wait("sync", "dv")
                nc_ = em.dma("sync", ns[:, ch, 512 * tblk:512 * (tblk + 1)],
                             C.SMB[:, 512 * ni:512 * (ni + 1)], "n%d" % mi)
                C.set_free(("nsl", ni), "n%d" % mi, nc_)
    norm_dv = em.count("dv")

    kts = io["kn"].rearrange("(h p) t -> p h t", p=128)
    for s_ in ("n0", "n1"):
        em.wait("sync", s_)
    for h in range(HPC):
        C.wait_free("sync", ("hbuf",))
        em.dma("sync", C.A0[:, 0:8192], kts[:, h, :], "k0")
        em.dma("sync",
               C.A0[:, 8192:16384].rearrange("p (ch d) -> p ch d", d=128),
               vts[:, :, 128 * h:128 * (h + 1)], "k0")
        krd = em.dma("sync", C.A0[:, 16384:24576], qns[:, h, :], "k0")
        C.wait_free("sync", ("strips",))
        srd = em.dma("sync",
                     C.ST[:, 0:2560].rearrange("p (v j) -> p v j", j=512),
                     io["l1_strips"].rearrange("(h v p) j -> p h v j", p=128, v=5)[:, h, :, :],
                     "k0")
        em.wait("vector", "k0", srd)
        em.op("vector", (lambda E, o=C.SM[:, 24:28], a=C.SM[:, 20:24],
                         sl=C.SM[:, h:h + 1]:
                         E.tensor_scalar(o, a, sl, None, ALU.mult)), inc="dv")
        em.wait("tensor", "k0", srd)
        kT = C.A0[:, 0:8192]
        vT = C.A0[:, 8192:16384]
        if h == 0:
            for r_ in range(4):
                C.set_free(("ring", r_), "mm", qkv_mm)
        rhs_q0 = C.A0[:, 16384:16896]
        for jc in range(4):
            C.wait_free("tensor", ("bank", jc))
            em.op("tensor",
                  (lambda E, o=C.bank(jc), l=kT[:, 128 * jc:128 * (jc + 1)], r=rhs_q0:
                   E.matmul(o, l, r, start=True, stop=True)), inc="mm")
        dmm = em.count("mm")
        em.wait("vector", "mm", dmm)
        for jc in range(4):
            em.op("vector",
                  (lambda E, o=C.ST[:, 2560 + 512 * jc:2560 + 512 * (jc + 1)],
                   a=C.bank(jc), s=C.SM[:, 24 + jc:25 + jc],
                   strip=C.ST[:, 512 * (1 + jc):512 * (2 + jc)]:
                   E.scalar_tensor_tensor(o, a, s, strip, ALU.add, ALU.add)),
                  inc="dv")
            C.set_free(("bank", jc), "dv", em.count("dv"))
        em.op("vector", (lambda E, o=C.ST[:, 5188:5189], a=C.ST[:, 2560:4608]:
                         E.reduce_max(o, a, axis=AX.X)), inc="dv")
        em.wait("tensor", "dv")
        em.op("tensor",
              (lambda E, o=C.PS[0:1, 2048:2176], a=C.ST[:, 5188:5189],
               ii=C.ident_f[:, :]: E.transpose(o, a, ii)), inc="mm")
        em.wait("vector", "mm")
        em.op("vector", (lambda E, o=C.RW[0:1, 0:1], a=C.PS[0:1, 2048:2176]:
                         E.reduce_max(o, a, axis=AX.X)), inc="dv")
        for b in range(16):
            nt = 4 * b + 4
            rhs_q = C.A0[:, 16384 + 512 * b:16384 + 512 * (b + 1)]
            em.op("vector", (lambda E, o=C.RW[0:1, 2:3], a=C.SM[0:1, h:h + 1], b_=b:
                             E.tensor_scalar_mul(o, a, 512.0 * b_)), inc="dv")
            em.op("vector", (lambda E, o=C.RW[0:1, 2:3], a=C.RW[0:1, 2:3],
                             b=C.RW[0:1, 0:1]: E.tensor_add(o, a, b)), inc="dv")
            em.op("vector",
                  (lambda E, o=C.RW[0:1, 4:4 + nt], a=C.RW[0:1, 2944:2944 + nt],
                   s1=C.SM[0:1, h:h + 1], s2=C.RW[0:1, 2:3], n_=nt:
                   E.tensor_scalar(o, a, s1, s2, ALU.mult, ALU.subtract)), inc="dv")
            bdv = em.count("dv")
            em.wait("tensor", "dv", bdv)
            em.op("tensor",
                  (lambda E, o=C.PS[:, 2048:2048 + nt], l=C.ones_row_f[0:1, 0:128],
                   r=C.RW[0:1, 4:4 + nt], n_=nt: E.matmul(o, l, r, start=True, stop=True)),
                  inc="mm")
            em.wait("vector", "mm")
            em.op("vector", (lambda E, o=C.ST[:, 4608:4608 + nt], a=C.PS[:, 2048:2048 + nt]:
                             E.tensor_copy(o, a)), inc="dv")
            pend = []

            def flush_l1(rec, vT=vT, b=b):
                rs_, eac_, t0_ = rec
                em.wait("tensor", "ac", eac_)
                if t0_ == 0:
                    C.wait_free("tensor", ("sbank",))
                    C.wait_free("tensor", ("cbank",))
                for k2 in range(2):
                    t = t0_ + k2
                    pr = C.A1[:, 1024 * rs_ + 512 * k2:1024 * rs_ + 512 * (k2 + 1)]
                    st_ = t == 0
                    sp_ = t == 4 * b + 3
                    em.op("tensor",
                          (lambda E, o=C.PS[0:1, 3072:3584], l=C.ones_col_b[:, :],
                           r=pr, s0=st_, s1=sp_: E.matmul(o, l, r, start=s0, stop=s1)))
                    em.op("tensor",
                          (lambda E, o=C.PS[:, 3584:4096],
                           l=vT[:, 128 * t:128 * (t + 1)], r=pr, s0=st_, s1=sp_:
                           E.matmul(o, l, r, start=s0, stop=s1)),
                          inc=("mm" if k2 == 1 else None))
                C.set_free(("ring", rs_), "mm", em.count("mm"))

            for p_ in range(nt // 2):
                t0 = 2 * p_
                rs = p_ % 4
                for k2, t in enumerate((t0, t0 + 1)):
                    C.wait_free("tensor", ("bank", k2))
                    em.op("tensor",
                          (lambda E, o=C.bank(k2), l=kT[:, 128 * t:128 * (t + 1)],
                           r=rhs_q: E.matmul(o, l, r, start=True, stop=True)), inc="mm")
                qkmm = em.count("mm")
                em.wait("vector", "mm", qkmm)
                for k2, t in enumerate((t0, t0 + 1)):
                    v_ = 0 if t < 4 * b else 1 + (t - 4 * b)
                    C.wait_free("vector", ("bank", 2 + k2))
                    em.op("vector",
                          (lambda E, o=C.bank(2 + k2), a=C.bank(k2),
                           s=C.ST[:, 4608 + t:4609 + t],
                           strip=C.ST[:, 512 * v_:512 * (v_ + 1)]:
                           E.scalar_tensor_tensor(o, a, s, strip, ALU.add, ALU.add)),
                          inc="dv")
                    C.set_free(("bank", k2), "dv", em.count("dv"))
                sttdv = em.count("dv")
                em.wait("scalar", "dv", sttdv)
                C.wait_free("scalar", ("ring", rs))
                em.op("scalar",
                      (lambda E, o=C.A1[:, 1024 * rs:1024 * (rs + 1)],
                       a=C.PS[:, 1024:2048]:
                       E.activation(o, a, AF.Exp)), inc="ac", noser=True)
                eac = em.count("ac")
                C.set_free(("bank", 2), "ac", eac)
                C.set_free(("bank", 3), "ac", eac)
                pend.append((rs, eac, t0))
                if len(pend) > 2:
                    flush_l1(pend.pop(0))
            while pend:
                flush_l1(pend.pop(0))
            pvmm = em.count("mm")
            em.wait("scalar", "mm", pvmm)
            em.op("scalar", (lambda E, o=C.RW[0:1, 512:1024], a=C.PS[0:1, 3072:3584]:
                             E.activation(o, a, AF.Ln)), inc="ac")
            em.wait("vector", "ac")
            em.op("vector", (lambda E, o=C.RW[0:1, 512:1024], a=C.RW[0:1, 512:1024]:
                             E.tensor_scalar_mul(o, a, -1.0)), inc="dv")
            em.wait("scalar", "dv")
            em.op("scalar", (lambda E, o=C.RW[0:1, 1024:1536], a=C.RW[0:1, 512:1024]:
                             E.activation(o, a, AF.Exp)), inc="ac")
            em.wait("tensor", "ac")
            em.op("tensor",
                  (lambda E, o=C.PS[:, 2560:3072], l=C.ones_row_f[0:1, 0:128],
                   r=C.RW[0:1, 1024:1536]: E.matmul(o, l, r, start=True, stop=True)),
                  inc="mm")
            em.wait("vector", "mm")
            em.op("vector", (lambda E, o=C.ST[:, 4676:5188], a=C.PS[:, 2560:3072]:
                             E.tensor_copy(o, a)), inc="dv")
            C.wait_free("vector", ("ctxh",))
            em.op("vector",
                  (lambda E, o=C.A0[:, 24576 + 512 * b:24576 + 512 * (b + 1)],
                   a=C.PS[:, 3584:4096], bx=C.ST[:, 4676:5188]:
                   E.tensor_mul(o, a, bx)), inc="dv")
            C.set_free(("sbank",), "ac", em.count("ac"))
            C.set_free(("cbank",), "dv", em.count("dv"))
        em.wait("sync", "dv")
        cgc = em.dma("sync",
                     io["ctxg_in"].rearrange("(h p) t -> p h t", p=128)[:, h, :],
                     C.A0[:, 24576:32768], "cg")
        C.set_free(("ctxh",), "cg", cgc)
        C.set_free(("hbuf",), "mm", em.count("mm"))
        C.set_free(("strips",), "dv", em.count("dv"))
    attn_mm = em.count("mm")
    attn_ac = em.count("ac")
    attn_dv = em.count("dv")

    em.wait("gpsimd", "cg")
    em.op("gpsimd",
          (lambda E: E.collective_compute(
              "AllGather", ALU.bypass, replica_groups=RG,
              ins=[io["ctxg_in"].opt()], outs=[io["ctxg"].opt()])), inc="cc")
    cc_c = em.count("cc")

    em.wait("gpsimd", "cc", cc_c)
    em.wait("gpsimd", "mm", attn_mm)
    cgr = io["ctxg"].rearrange("(kc p) t -> p kc t", p=128)
    clc = em.op(
        "gpsimd",
        (lambda E: E.dma_start(
            out=C.A1[:, :].rearrange("p (kc t) -> p kc t", t=1024),
            in_=cgr[:, :, bass.ts(E.partition_id(), 1024)])),
        inc="cl", amt=16)

    for i in range(4):
        C.set_free(("xsl", i), "dv", attn_dv)
        C.set_free(("rsl", i), "dv", attn_dv)
    em.wait("tensor", "cl", clc)

    def ctx_rhs(kc, tb):
        return C.A1[:, kc * 1024 + tb * 512:kc * 1024 + (tb + 1) * 512]

    emit_proj(C, "wo1", io["l1_woT"], KC, 128, ctx_rhs, ep.chunk_cb)
    return ep.finish()


def build_l1attn_stage():
    nc = bass.Bass("TRN2", target_bir_lowering=False, debug=False, num_devices=8)
    io = {}
    io["x2f"] = _decl(nc, "x2f", [128, KC * T], F32, "ExternalInput")
    io["l1_qkvT"] = _decl(nc, "l1_qkvT", [12 * 128, KC * 128], F32, "ExternalInput")
    for nm in ("l1_qw", "l1_qb", "l1_kw", "l1_kb"):
        io[nm] = _decl(nc, nm, [128, 4], F32, "ExternalInput")
    io["l1_slopes"] = _decl(nc, "l1_slopes", [128, 4], F32, "ExternalInput")
    io["l1_strips"] = _decl(nc, "l1_strips", [4 * 5 * 128, 512], F32, "ExternalInput")
    io["l1_woT"] = _decl(nc, "l1_woT", [KC * 128, KC * 128], F32, "ExternalInput")
    io["ident"] = _decl(nc, "ident", [128, 128], F32, "ExternalInput")
    io["jcv"] = _decl(nc, "jcv", [128, 4], F32, "ExternalInput")
    io["posr"] = _decl(nc, "posr", [1, 68], F32, "ExternalInput")
    io["xgat_in"] = _decl(nc, "xgat_in", [D, T], BF16, "Internal")
    io["xgat"] = _decl(nc, "xgat", [NCORE * D, T], BF16, "Internal")
    io["qraw"] = _decl(nc, "qraw", [512, S], BF16, "Internal")
    io["kraw"] = _decl(nc, "kraw", [512, S], BF16, "Internal")
    io["vtok1"] = _decl(nc, "vtok1", [S, 512], BF16, "Internal")
    io["qn"] = _decl(nc, "qn", [512, S], BF16, "Internal")
    io["kn"] = _decl(nc, "kn", [512, S], BF16, "Internal")
    io["sqk_in"] = _decl(nc, "sqk_in", [4, S], F32, "Internal")
    io["sqk"] = _decl(nc, "sqk", [4, S], F32, "Internal")
    io["ctxg_in"] = _decl(nc, "ctxg_in", [512, S], BF16, "Internal")
    io["ctxg"] = _decl(nc, "ctxg", [D, S], BF16, "Internal")
    io["r3"] = _decl(nc, "r3", [128, KC * T], F32, "Internal")
    io["x3f"] = _decl(nc, "x3f", [128, KC * T], F32, "ExternalOutput")
    with ExitStack() as ctx:
        em = Em(nc)
        C = Ctx(nc, em, ctx)
        setup_consts(C, io)
        em.dma("gpsimd", C.A1[:, 0:KC * 1024], io["x2f"][:, :], "c0g")
        em.wait_fin("sync", "c0g")
        ep = Ep(C, "e3", io["x2f"], io["r3"], io["x3f"], C.A0)
        emit_l1attn(C, io, ep)
        for i in range(4):
            em.wait("sync", "r%d" % i)
        em.finalize(ctx)
    return nc


def emit_l1ffn(C, io, bprev, ep):
    em = C.em

    def xr(kc, tb):
        return bprev[:, kc * 1024 + tb * 512:kc * 1024 + (tb + 1) * 512]

    la_rdy = em.dma("gpsimd", C.A1[:, 0:4096], io["l1_laT"][:, :], "lw1")
    lb_rdy = em.dma("gpsimd", C.A1[:, 4096:8192], io["l1_lbT"][:, :], "lw1")
    em.dma("sync", C.SM[0:8, 36:37], io["l1_lw"][:, :], "c4")
    em.dma("sync", C.SM[0:8, 80:208], io["mexp"][:, :], "c4")
    em.dma("sync", C.SM[:, 48:80], io["l1_bias"][:, :], "c4")
    em.wait("tensor", "c4")
    em.wait("tensor", "lw1", lb_rdy)

    C.wait_free("tensor", ("bank", 6))
    em.op("tensor",
          (lambda E, o=C.PS[:, 512 * 6:512 * 6 + 1], l=C.SM[0:8, 80:208],
           r=C.SM[0:8, 36:37]: E.matmul(o, l, r, start=True, stop=True)), inc="mm")
    em.wait("vector", "mm")
    em.op("vector", (lambda E, o=C.SM[:, 40:41], a=C.PS[:, 512 * 6:512 * 6 + 1]:
                     E.tensor_scalar_mul(o, a, 2.0)), inc="dv")
    C.set_free(("bank", 6), "dv", em.count("dv"))

    for tb in (0, 1):
        pb = 4 + tb
        C.wait_free("tensor", ("bank", pb))
        ps = C.bank(pb)
        for kc in range(KC):
            em.op("tensor",
                  (lambda E, o=ps, l=C.A1[:, kc * 128:(kc + 1) * 128],
                   r=xr(kc, tb), st=(kc == 0), sp=(kc == KC - 1):
                   E.matmul(o, l, r, start=st, stop=sp)),
                  inc=("mm" if kc == KC - 1 else None))
        em.wait("vector", "mm")
        em.op("vector",
              (lambda E, o=C.RW[:, :].bitcast(BF16)[:, 4864 + 512 * tb:4864 + 512 * (tb + 1)],
               a=ps, s=C.SM[:, 40:41]: E.tensor_scalar(o, a, s, None, ALU.mult)),
              inc="dv")
        C.set_free(("bank", pb), "dv", em.count("dv"))
    low_dv = em.count("dv")

    em.wait("tensor", "dv", low_dv)

    def extra(jb, tb):
        return [(C.A1[:, 4096 + jb * 128:4096 + (jb + 1) * 128],
                 C.RW[:, :].bitcast(BF16)[:, 4864 + 512 * tb:4864 + 512 * (tb + 1)])]

    emit_proj(C, "f1", io["l1_wT"], KC, 128, xr, ep.chunk_cb, extra=extra)
    return ep.finish()


def build_l1ffn_stage():
    nc = bass.Bass("TRN2", target_bir_lowering=False, debug=False, num_devices=8)
    io = {}
    io["x3f"] = _decl(nc, "x3f", [128, KC * T], F32, "ExternalInput")
    io["l1_wT"] = _decl(nc, "l1_wT", [KC * 128, KC * 128], F32, "ExternalInput")
    io["l1_bias"] = _decl(nc, "l1_bias", [128, 32], F32, "ExternalInput")
    io["l1_laT"] = _decl(nc, "l1_laT", [128, KC * 128], F32, "ExternalInput")
    io["l1_lbT"] = _decl(nc, "l1_lbT", [128, D], F32, "ExternalInput")
    io["l1_lw"] = _decl(nc, "l1_lw", [8, 1], F32, "ExternalInput")
    io["mexp"] = _decl(nc, "mexp", [8, 128], F32, "ExternalInput")
    io["ident"] = _decl(nc, "ident", [128, 128], F32, "ExternalInput")
    io["r4"] = _decl(nc, "r4", [128, KC * T], F32, "Internal")
    io["y"] = _decl(nc, "y", [128, KC * T], F32, "ExternalOutput")
    with ExitStack() as ctx:
        em = Em(nc)
        C = Ctx(nc, em, ctx)
        setup_consts(C, io)
        em.dma("gpsimd", C.A0[:, 0:KC * 1024], io["x3f"][:, :], "c0g")
        em.wait_fin("tensor", "c0g")
        ep = Ep(C, "e4", io["x3f"], io["r4"], io["y"], None,
                bias_cols=C.SM[:, 48:80])
        emit_l1ffn(C, io, C.A0, ep)
        for i in range(4):
            em.wait("sync", "r%d" % i)
        em.finalize(ctx)
    return nc


def build_fused():
    nc = bass.Bass("TRN2", target_bir_lowering=False, debug=False, num_devices=8)
    io = {}
    EI = lambda n, s, d=F32: _decl(nc, n, s, d, "ExternalInput")
    IN = lambda n, s, d=F32: _decl(nc, n, s, d, "Internal")
    io["x_in"] = EI("x_in", [128, KC * T])
    io["ident"] = EI("ident", [128, 128])
    io["mexp"] = EI("mexp", [8, 128])
    io["jcv"] = EI("jcv", [128, 4])
    io["posr"] = EI("posr", [1, 68])
    io["l0_qkvT"] = EI("l0_qkvT", [48 * 128, KC * 128])
    io["l0_bv"] = EI("l0_bv", [1, 1024])
    io["l0_woT"] = EI("l0_woT", [KC * 128, KC * 128])
    io["l0_wrT"] = EI("l0_wrT", [128, 256])
    io["l0_br"] = EI("l0_br", [1, 8])
    io["l0_aT"] = EI("l0_aT", [128, KC * 128])
    io["l0_bT"] = EI("l0_bT", [128, D])
    io["l1_qkvT"] = EI("l1_qkvT", [12 * 128, KC * 128])
    for nm in ("l1_qw", "l1_qb", "l1_kw", "l1_kb", "l1_slopes"):
        io[nm] = EI(nm, [128, 4])
    io["l1_strips"] = EI("l1_strips", [4 * 5 * 128, 512])
    io["l1_woT"] = EI("l1_woT", [KC * 128, KC * 128])
    io["l1_wT"] = EI("l1_wT", [KC * 128, KC * 128])
    io["l1_bias"] = EI("l1_bias", [128, 32])
    io["l1_laT"] = EI("l1_laT", [128, KC * 128])
    io["l1_lbT"] = EI("l1_lbT", [128, D])
    io["l1_lw"] = EI("l1_lw", [8, 1])
    io["qspill"] = IN("qspill", [D, T], BF16)
    io["kgat_in"] = IN("kgat_in", [1024, T], BF16)
    io["kgat"] = IN("kgat", [S, T], BF16)
    io["vgat_in"] = IN("vgat_in", [T, 1024], BF16)
    io["vgat"] = IN("vgat", [S, 1024], BF16)
    io["r1"] = IN("r1", [128, KC * T]); io["x1f"] = IN("x1f", [128, KC * T])
    io["r2"] = IN("r2", [128, KC * T]); io["x2f"] = IN("x2f", [128, KC * T])
    io["r3"] = IN("r3", [128, KC * T]); io["x3f"] = IN("x3f", [128, KC * T])
    io["r4"] = IN("r4", [128, KC * T])
    io["xgat_in"] = IN("xgat_in", [D, T], BF16)
    io["xgat"] = IN("xgat", [NCORE * D, T], BF16)
    io["qraw"] = IN("qraw", [512, S], BF16)
    io["kraw"] = IN("kraw", [512, S], BF16)
    io["vtok1"] = IN("vtok1", [S, 512], BF16)
    io["qn"] = IN("qn", [512, S], BF16)
    io["kn"] = IN("kn", [512, S], BF16)
    io["sqk_in"] = IN("sqk_in", [4, S])
    io["sqk"] = IN("sqk", [4, S])
    io["ctxg_in"] = IN("ctxg_in", [512, S], BF16)
    io["ctxg"] = IN("ctxg", [D, S], BF16)
    io["y"] = _decl(nc, "y", [128, KC * T], F32, "ExternalOutput")

    with ExitStack() as ctx:
        em = Em(nc)
        C = Ctx(nc, em, ctx)
        setup_consts(C, io)
        em.dma("gpsimd", C.A0[:, 0:KC * 1024], io["x_in"][:, :], "c0g")
        em.wait_fin("tensor", "c0g")

        def barrier(done):
            for eng in ("tensor", "vector", "scalar", "gpsimd", "sync"):
                em.wait(eng, "dv", done)
            for i in range(4):
                em.wait("sync", "r%d" % i)

        d1 = emit_l0attn_with_wo(C, io)
        barrier(d1[1])
        ep2 = Ep(C, "e2", io["x1f"], io["r2"], io["x2f"], C.A1)
        d2 = emit_l0ffn(C, io, C.A0, ep2)
        barrier(d2[1])
        ep3 = Ep(C, "e3", io["x2f"], io["r3"], io["x3f"], C.A0)
        d3 = emit_l1attn(C, io, ep3)
        barrier(d3[1])
        ep4 = Ep(C, "e4", io["x3f"], io["r4"], io["y"], None,
                 bias_cols=C.SM[:, 48:80])
        emit_l1ffn(C, io, C.A0, ep4)
        for i in range(4):
            em.wait("sync", "r%d" % i)
        em.finalize(ctx)
    return nc


def kernel_fused(**inputs):
    sh, pc = host_prep(inputs)
    cores = list(range(NCORE))
    nc = _get_nc("fused", build_fused)
    maps = []
    for c in cores:
        m = {}
        for nm in ("ident", "mexp", "jcv", "posr", "l0_qkvT", "l0_bv", "l0_woT",
                   "l0_wrT", "l0_br", "l0_aT", "l0_bT", "l1_woT", "l1_wT",
                   "l1_bias", "l1_laT", "l1_lbT", "l1_lw"):
            m[nm] = sh[nm]
        for nm in ("x_in", "l1_qkvT", "l1_qw", "l1_qb", "l1_kw", "l1_kb",
                   "l1_slopes", "l1_strips"):
            m[nm] = pc[c][nm]
        maps.append(m)
    res = run_bass_kernel_spmd(nc, maps, core_ids=cores)
    y = np.concatenate(
        [np.asarray(res.results[c]["y"]).reshape(128, KC, T)
         .transpose(1, 0, 2).reshape(D, T).T for c in cores], axis=0)
    return y[None, :, :].astype(np.float32)
